# version 31
# baseline (speedup 1.0000x reference)
import numpy as np
import concourse.bass as bass
import concourse.mybir as mybir
from concourse.bass_utils import run_bass_kernel_spmd
from contextlib import ExitStack

F32 = mybir.dt.float32
BF16 = mybir.dt.bfloat16
AF = mybir.ActivationFunctionType
OP = mybir.AluOpType

NCORES = 8
BSEQ = 2
T = 2048
C = 1024
H = 16
DH = 64
CH = C // 128
NJ = T // 512
KTT = T // 128
THETA = 10000.0


class Step:
    __slots__ = ("eng", "fn", "sem", "val", "deps", "snap", "inc", "waits",
                 "inc_on", "force_prev")

    def __init__(self, eng, fn, sem, val, deps, inc):
        self.eng, self.fn, self.sem, self.val = eng, fn, sem, val
        self.deps, self.inc = deps, inc
        self.snap = None
        self.waits = ()
        self.force_prev = None


class Prog:
    ENGS = ("Y", "G", "T", "V", "S")
    NOCHAIN = ("dka", "dkb", "dqa", "dqb")

    def __init__(self):
        self.steps = []
        self.sem_cnt = {}
        self.last_w = {}
        self.last_r = {}
        self.last_dma = {}

    def add(self, eng, fn, reads=(), writes=(), deps=(), dma_sem=None):
        is_dma = dma_sem is not None
        dl = [d for d in deps if d is not None]
        for k in reads:
            w = self.last_w.get(k)
            if w is not None:
                dl.append(w)
        for k in writes:
            w = self.last_w.get(k)
            if w is not None:
                dl.append(w)
            dl.extend(self.last_r.get(k, ()))
        if dma_sem is not None:
            sem, inc = dma_sem, 16
        else:
            sem, inc = "e" + eng, 1
        s = Step(eng, fn, sem, 0, dl, inc)
        if dma_sem is not None and dma_sem not in self.NOCHAIN:
            prev = self.last_dma.get(dma_sem)
            if prev is not None:
                dl.append(prev)
                s.force_prev = prev
            self.last_dma[dma_sem] = s
        self.steps.append(s)
        for k in reads:
            self.last_r.setdefault(k, []).append(s)
        for k in writes:
            self.last_w[k] = s
            self.last_r[k] = []
        return s

    def plan(self):
        for s in self.steps:
            s.inc_on = (s.inc == 16)
        for s in self.steps:
            for d in s.deps:
                d.inc_on = True
        self.sem_cnt = {}
        for s in self.steps:
            if s.inc_on:
                v = self.sem_cnt.get(s.sem, 0) + s.inc
                self.sem_cnt[s.sem] = v
                s.val = v
        know = {e: {} for e in self.ENGS}
        for s in self.steps:
            k = know[s.eng]
            wd = {}
            for d in s.deps:
                if k.get(d.sem, 0) < d.val:
                    wd[d.sem] = max(wd.get(d.sem, 0), d.val)
                    k[d.sem] = d.val
                    if d.snap:
                        for sm, v in d.snap.items():
                            if k.get(sm, 0) < v:
                                k[sm] = v
            if s.force_prev is not None and s.force_prev.val > 0:
                fp = s.force_prev
                if wd.get(fp.sem, 0) < fp.val:
                    wd[fp.sem] = fp.val
                    if k.get(fp.sem, 0) < fp.val:
                        k[fp.sem] = fp.val
            s.waits = sorted(wd.items())
            snap = dict(k)
            snap[s.sem] = max(snap.get(s.sem, 0), s.val)
            s.snap = snap

    def emit(self, block, handles):
        self.plan()
        by_eng = {e: [s for s in self.steps if s.eng == e] for e in self.ENGS}

        def runner(ename):
            def run(eng):
                for s in by_eng[ename]:
                    for sm, v in s.waits:
                        eng.wait_ge(handles[sm], v)
                    inst = s.fn(eng)
                    if s.inc_on:
                        inst.then_inc(handles[s.sem], s.inc)
            return run

        block.sync(runner("Y"))
        block.gpsimd(runner("G"))
        block.tensor(runner("T"))
        block.vector(runner("V"))
        block.scalar(runner("S"))


def build_program():
    nc = bass.Bass("TRN2", target_bir_lowering=False)
    P = Prog()
    A = P.add

    x_h = nc.dram_tensor("x", [BSEQ, NJ, 128, CH * 512], F32, kind="ExternalInput").ap()
    wkqv_h = nc.dram_tensor("wkqvt", [C, 3 * C], F32, kind="ExternalInput").ap()
    wo_h = nc.dram_tensor("wot", [C, C], F32, kind="ExternalInput").ap()
    wr_h = nc.dram_tensor("wrt", [C, C], F32, kind="ExternalInput").ap()
    wk_h = nc.dram_tensor("wkt", [C, C], F32, kind="ExternalInput").ap()
    wv_h = nc.dram_tensor("wvt", [C, C], F32, kind="ExternalInput").ap()
    mu_h = nc.dram_tensor("mu2", [128, 2 * CH], F32, kind="ExternalInput").ap()
    maps_h = nc.dram_tensor("ropemaps", [128, 2 * T], F32, kind="ExternalInput").ap()
    y_h = nc.dram_tensor("y", [BSEQ, NJ, 128, CH * 512], F32, kind="ExternalOutput").ap()
    NGRP = 28
    wtmp_h = nc.dram_tensor("wtmp", [NGRP, 128, 2048], BF16, kind="Internal").ap()
    aug_h = nc.dram_tensor("augh", [64, 128], BF16, kind="Internal").ap()

    es = ExitStack()
    sbuf = lambda name, shape, dt: es.enter_context(nc.sbuf_tensor(name, shape, dt))

    NSLOT = 5
    wbuf = [sbuf(f"wbuf{i}", [128, 2048], BF16) for i in range(NSLOT)]
    kt = sbuf("ktil", [65, H * T], BF16)
    vtok = sbuf("vtok", [128, KTT * H * 65], BF16)
    qt = sbuf("qtil", [65, H * 512], BF16)
    xres = sbuf("xres", [128, CH * 512], BF16)
    outb = sbuf("outb", [128, CH * 512], BF16)
    xn = sbuf("xn", [128, CH * 513], BF16)
    call = sbuf("call", [128, CH * 512], BF16)
    attn = sbuf("attn", [128, CH * 512], BF16)
    p2 = [sbuf(f"p2_{i}", [128, 512], BF16) for i in range(4)]
    rsig = sbuf("rsigt", [128, CH * 512], BF16)
    xrk = sbuf("xrk", [128, CH * 512], BF16)
    recb = sbuf("recb", [33, 1024], BF16)
    srowt = sbuf("srowt", [33, 1024], F32)
    sgm = sbuf("sgm", [128, 64], F32)
    lnout = sbuf("lnout", [128, 64], F32)
    b2bf = sbuf("b2bf", [128, 64], BF16)
    mu2 = sbuf("mu2s", [128, 2 * CH], F32)
    r_s = sbuf("r_s", [1, 512], F32)
    s2_s = sbuf("s2_s", [1, 512], F32)
    maskadd = sbuf("maskadd", [128, 128], F32)
    tri01 = sbuf("tri01", [128, 128], BF16)
    i128 = sbuf("i128", [128, 128], BF16)
    indic = sbuf("indic", [33, 128], BF16)
    ones128 = sbuf("ones128", [128, 1], BF16)
    onesf1 = sbuf("onesf1", [1, 128], F32)
    consts = sbuf("consts", [128, 8], F32)
    xtail = sbuf("xtail", [128, CH], BF16)
    augs = sbuf("augs", [64, 128], BF16)

    ps_t = es.enter_context(nc.psum_tensor("ps", [128, 4096], F32))
    ps = ps_t.ap() if hasattr(ps_t, "ap") else ps_t[:]

    def bank(b):
        return ps[:, 512 * b: 512 * b + 512]


    sem_names = ["eY", "eG", "eT", "eV", "eS", "fin",
                 "dwl0", "dwl1", "dwl2", "dwl3", "dwl4", "dws0", "dws1", "dws2", "dws3", "dws4", "dst0", "dst1", "dst2", "dst3", "dst4", "dmu",
                 "dx", "dmap", "dka", "dkb", "dqa", "dqb", "dy", "dy2", "db1", "db2"]
    semh = {n: es.enter_context(nc.semaphore(n)) for n in sem_names}

    CONST_VALS = [0.0, 1.0, 1.0 / 32.0, 0.125, 1e-5, -1.0, 0.5, -0.5]
    for i, v in enumerate(CONST_VALS):
        nc.const_aps.register(F32, v, consts[:, i:i + 1])

    wb3 = [wbuf[i][:].rearrange("p (c f) -> p c f", c=CH) for i in range(NSLOT)]
    vt_ap = vtok[:]
    vt3 = vt_ap.rearrange("p (k h d) -> p k h d", k=KTT, h=H)
    kt_ap = kt[:]
    rsf32 = rsig[:].bitcast(F32)
    mapblk = rsf32[:, 0:1024]
    sln = rsf32[0:33, 1024:2048]
    qt_ap = qt[:]
    call3 = call[:].rearrange("p (c t) -> p c t", c=CH)
    xn_ap = xn[:]
    xn3 = xn_ap.rearrange("p (c t) -> p c t", c=CH)
    attn3 = attn[:].rearrange("p (c t) -> p c t", c=CH)
    pack3 = xrk[:].rearrange("p (c t) -> p c t", c=CH)
    rsig3 = rsig[:].rearrange("p (c t) -> p c t", c=CH)
    xrk3 = xrk[:].rearrange("p (c t) -> p c t", c=CH)
    xres3 = xres[:].rearrange("p (c t) -> p c t", c=CH)
    outb3 = outb[:].rearrange("p (c t) -> p c t", c=CH)

    for i, v in enumerate(CONST_VALS):
        A("V", lambda e, i=i, v=v: e.memset(consts[:, i:i + 1], v), writes=["consts"])
    A("V", lambda e: e.memset(ones128[:], 1.0), writes=["ones128"])
    A("V", lambda e: e.memset(onesf1[:], 1.0), writes=["onesf1"])
    A("V", lambda e: e.memset(kt[64:65, :], 1.0), writes=["ktones"])
    A("V", lambda e: e.memset(indic[0:1, 0:64], 1.0), writes=["indic"])
    A("V", lambda e: e.memset(indic[0:1, 64:128], 0.0), writes=["indic"])
    A("V", lambda e: e.memset(indic[32:33, 0:64], 0.0), writes=["indic"])
    A("V", lambda e: e.memset(indic[32:33, 64:128], 1.0), writes=["indic"])
    A("V", lambda e: e.memset(xtail[:], 0.0), writes=["xtail"])
    A("V", lambda e: e.memset(srowt[:], 1.0), writes=["srow"])
    A("V", lambda e: e.memset(vt3[:, :, :, 64:65], 1.0), writes=["vtones"])
    A("V", lambda e: e.memset(maskadd[:], 0.0), writes=["maskadd"])
    A("G", lambda e: e.affine_select(out=maskadd[:], in_=maskadd[:],
                                     pattern=[[-1, 128]], compare_op=OP.is_ge,
                                     fill=-1.0e4, base=0, channel_multiplier=1),
      reads=["maskadd"], writes=["maskadd"])
    A("V", lambda e: e.memset(tri01[:], 1.0), writes=["tri01"])
    A("G", lambda e: e.affine_select(out=tri01[:], in_=tri01[:],
                                     pattern=[[1, 128]], compare_op=OP.is_ge,
                                     fill=0.0, base=0, channel_multiplier=-1),
      reads=["tri01"], writes=["tri01"])
    A("V", lambda e: e.memset(i128[:], 1.0), writes=["i128"])
    A("G", lambda e: e.affine_select(out=i128[:], in_=i128[:],
                                     pattern=[[1, 128]], compare_op=OP.is_ge,
                                     fill=0.0, base=0, channel_multiplier=-1),
      reads=["i128"], writes=["i128"])
    A("G", lambda e: e.affine_select(out=i128[:], in_=i128[:],
                                     pattern=[[-1, 128]], compare_op=OP.is_ge,
                                     fill=0.0, base=0, channel_multiplier=1),
      reads=["i128"], writes=["i128"])
    A("G", lambda e: e.dma_start(out=mu2[:], in_=mu_h), writes=["mu2"], dma_sem="dmu")

    wsrcs = ([(wkqv_h, g) for g in range(12)] + [(wo_h, g) for g in range(4)] +
             [(wr_h, g) for g in range(4)] + [(wk_h, g) for g in range(4)] +
             [(wv_h, g) for g in range(4)])
    wstore_last = []
    for gi, (h, g) in enumerate(wsrcs):
        slot = gi % NSLOT
        key = f"wbuf{slot}"
        src = h.rearrange("(c p) f -> p c f", p=128)[:, :, 256 * g: 256 * g + 256]
        A("G", lambda e, src=src, slot=slot: e.dma_start(out=wb3[slot], in_=src),
          writes=[key], dma_sem=f"dwl{slot}")
        st = A("Y", lambda e, gi=gi, slot=slot: e.dma_start(out=wtmp_h[gi], in_=wbuf[slot][:]),
               reads=[key], dma_sem=f"dws{slot}")
        if gi >= len(wsrcs) - NSLOT:
            wstore_last.append(st)

    wload_i = [0]

    def load_w(gi):
        slot = wload_i[0] % NSLOT
        wload_i[0] += 1
        key = f"wbuf{slot}"
        st = A("Y", lambda e, gi=gi, slot=slot: e.dma_start(out=wbuf[slot][:], in_=wtmp_h[gi]),
               writes=[key], deps=wstore_last, dma_sem=f"dst{slot}")
        return st, slot, key

    PSK = [f"ps{i}" for i in range(8)]

    def emit_ln_phases(seq, J):
        first = (seq == 0 and J == 0)
        b4row = ps[0:1, 2048:2560]

        def ph_dma(seq=seq, J=J, first=first):
            dstk = "xres" if first else "outb"
            dst = xres[:] if first else outb[:]
            A("G", lambda e, seq=seq, J=J, dst=dst: e.dma_start(
                out=dst, in_=x_h[seq, J]),
              writes=[dstk], dma_sem="dx")

        def ph_maps(J=J):
            A("Y", lambda e, J=J: e.dma_start(
                out=mapblk.rearrange("p (g t) -> p g t", g=2),
                in_=maps_h.rearrange("p (g t) -> p g t", g=2)[:, :, 512 * J: 512 * J + 512]),
              writes=["mapblk", "rsigt"], dma_sem="dmap")

        def ph0():
            for c in range(CH):
                A("T", lambda e, c=c: e.matmul(b4row, ones128[:], xres3[:, c, :],
                                               start=(c == 0), stop=(c == CH - 1)),
                  reads=["xres", "ones128"],
                  writes=["ps4"] if c in (0, CH - 1) else [])

        def ph1():
            A("V", lambda e: e.tensor_scalar(out=s2_s[:], in0=b4row,
                                             scalar1=1.0 / C, scalar2=None, op0=OP.mult),
              reads=["ps4"], writes=["s2_s"])
            A("V", lambda e: e.tensor_tensor(out=r_s[:], in0=s2_s[:], in1=s2_s[:], op=OP.mult),
              reads=["s2_s"], writes=["r_s"])
            A("V", lambda e: e.tensor_tensor(out=call3, in0=xres3, in1=xres3, op=OP.mult),
              reads=["xres"], writes=["call"])
            for c in range(CH):
                A("T", lambda e, c=c: e.matmul(b4row, ones128[:], call3[:, c, :],
                                               start=(c == 0), stop=(c == CH - 1)),
                  reads=["call", "ones128"],
                  writes=["ps4"] if c in (0, CH - 1) else [])

        def ph2(first=first):
            A("V", lambda e: e.scalar_tensor_tensor(out=r_s[:], in0=b4row,
                                                    scalar=1.0 / C, in1=r_s[:],
                                                    op0=OP.mult, op1=OP.subtract),
              reads=["ps4", "r_s"], writes=["r_s"])
            A("T", lambda e: e.matmul(bank(4), onesf1[:], s2_s[:], start=True, stop=True),
              reads=["s2_s", "onesf1", "ps4"], writes=["ps4"])
            A("S", lambda e: e.activation(s2_s[:], r_s[:], AF.Ln, bias=1e-5, scale=1.0),
              reads=["r_s", "consts"], writes=["s2_s"])
            A("S", lambda e: e.activation(r_s[:], s2_s[:], AF.Exp, scale=-0.5),
              reads=["s2_s", "consts"], writes=["r_s"])
            if not first:
                A("V", lambda e: e.tensor_copy(xtail[:], xn_ap[:, 512::513]),
                  reads=["xn"], writes=["xtail"])

        def ph3():
            A("V", lambda e: e.tensor_tensor(
                out=call3, in0=xres3,
                in1=bank(4).unsqueeze(1).broadcast_to([128, CH, 512]), op=OP.subtract),
              reads=["xres", "ps4", "call"], writes=["call"])

        def ph4(J=J):
            A("T", lambda e: e.matmul(bank(4), onesf1[:], r_s[:], start=True, stop=True),
              reads=["r_s", "onesf1"], writes=["ps4"])
            A("V", lambda e: e.tensor_tensor(
                out=xn3[:, :, 1:513], in0=call3,
                in1=bank(4).unsqueeze(1).broadcast_to([128, CH, 512]), op=OP.mult),
              reads=["call", "ps4"], writes=["xn"])
            if J == 0:
                A("V", lambda e: e.memset(xn_ap[:, 0::513], 0.0), reads=["xn"], writes=["xn"])
            else:
                A("V", lambda e: e.tensor_copy(xn_ap[:, 0::513], xtail[:]),
                  reads=["xtail", "xn"], writes=["xn"])

        return [ph_dma, ph_maps, ph0, ph1, ph2, ph3, ph4]

    def emit_ln(seq, J):
        for ph in emit_ln_phases(seq, J):
            ph()

    emit_ln(0, 0)
    pending_wo = [None]
    for seq in range(BSEQ):
        for J in range(NJ):
            is_last = (seq == BSEQ - 1 and J == NJ - 1)
            if pending_wo[0] is not None:
                pending_wo[0]()
                pending_wo[0] = None
            ln_phases = []
            if not is_last:
                nseq, nJ = (seq, J + 1) if J + 1 < NJ else (seq + 1, 0)
                ln_phases = emit_ln_phases(nseq, nJ)
            for g in range(12):
                wst, slot, wkey = load_w(g)
                if g < 8:
                    part = "K" if g < 4 else "Q"
                    for fc2 in range(2):
                        gch = (g % 4) * 2 + fc2
                        bk = 2 + gch % 2
                        bkey = PSK[bk]
                        for ci in range(CH):
                            A("T", lambda e, slot=slot, fc2=fc2, ci=ci, bk=bk: e.matmul(
                                bank(bk), wb3[slot][:, ci, 128 * fc2:128 * fc2 + 128],
                                xn3[:, ci, 1:513],
                                start=(ci == 0), stop=(ci == CH - 1)),
                              reads=[wkey, "xn"],
                              writes=[bkey] if ci in (0, CH - 1) else [], deps=[wst])
                        if part == "K":
                            key = f"kK{gch}"
                            tgt = kt_ap
                            sA = T * (2 * gch) + 512 * J
                            sB = T * (2 * gch + 1) + 512 * J
                            extra = []
                        else:
                            key = f"kQ{gch}"
                            tgt = qt_ap
                            sA = 512 * (2 * gch)
                            sB = 512 * (2 * gch + 1)
                            extra = [f"kQa{i}" for i in range(4)]
                        A("V", lambda e, gch=gch, bk=bk: e.tensor_tensor(
                            out=call3[:, gch, :], in0=bank(bk), in1=mapblk[:, 512:1024],
                            op=OP.mult),
                          reads=[bkey, "mapblk", "rsigt"], writes=["call"])
                        A("V", lambda e, bk=bk, tgt=tgt, sA=sA: e.tensor_tensor(
                            out=tgt[0:64, sA:sA + 512], in0=ps[0:64, 512 * bk:512 * bk + 512],
                            in1=mapblk[0:64, 0:512], op=OP.mult),
                          reads=[bkey, "mapblk", "rsigt"] + extra, writes=[key])
                        A("V", lambda e, bk=bk, tgt=tgt, sB=sB: e.tensor_tensor(
                            out=tgt[0:64, sB:sB + 512], in0=ps[64:128, 512 * bk:512 * bk + 512],
                            in1=mapblk[0:64, 0:512], op=OP.mult),
                          reads=[bkey, "mapblk", "rsigt"] + extra, writes=[key])
                    if g % 4 == 3:
                        if part == "K":
                            d32 = kt_ap[0:32, :].rearrange("p (h t) -> p h t", h=H)
                            d32b = kt_ap[32:64, :].rearrange("p (h t) -> p h t", h=H)
                            dsts32 = [d32[:, 0::2, 512 * J:512 * J + 512],
                                      d32b[:, 0::2, 512 * J:512 * J + 512],
                                      d32[:, 1::2, 512 * J:512 * J + 512],
                                      d32b[:, 1::2, 512 * J:512 * J + 512]]
                            sb2, kd = "dkb", "kK"
                        else:
                            d32 = qt_ap[0:32, :].rearrange("p (h t) -> p h t", h=H)
                            d32b = qt_ap[32:64, :].rearrange("p (h t) -> p h t", h=H)
                            dsts32 = [d32[:, 0::2, :], d32b[:, 0::2, :],
                                      d32[:, 1::2, :], d32b[:, 1::2, :]]
                            sb2, kd = "dqb", "kQ"
                        srcs = [call3[32:64, :, :], call3[0:32, :, :],
                                call3[96:128, :, :], call3[64:96, :, :]]
                        allk = [f"{kd}{c}" for c in range(CH)]
                        for i in range(4):
                            A("G", lambda e, i=i, dsts32=dsts32, srcs=srcs: e.dma_start(
                                out=dsts32[i], in_=srcs[i], accum_op=OP.add),
                              reads=["call"] + allk, writes=[f"{kd}a{i}"], dma_sem=sb2)
                else:
                    for tt in range(4):
                        bk = 2 + tt % 2
                        bkey = PSK[bk]
                        for ci in range(CH):
                            A("T", lambda e, slot=slot, tt=tt, ci=ci, bk=bk: e.matmul(
                                ps[:, 512 * bk: 512 * bk + 256],
                                xn3[:, ci, 1 + 128 * tt:1 + 128 * tt + 128],
                                wb3[slot][:, ci, :],
                                start=(ci == 0), stop=(ci == CH - 1)),
                              reads=[wkey, "xn"],
                              writes=[bkey] if ci in (0, CH - 1) else [], deps=[wst])
                        ktile = 4 * J + tt
                        h0 = (g - 8) * 4
                        A("V", lambda e, ktile=ktile, h0=h0, bk=bk: e.tensor_copy(
                            vt3[:, ktile:ktile + 1, h0:h0 + 4, 0:64],
                            ps[:, 512 * bk: 512 * bk + 256].rearrange(
                                "p (o h d) -> p o h d", o=1, h=4)),
                          reads=[bkey], writes=["vtok"])

            KD = [f"kK{c}" for c in range(CH)] + [f"kKa{i}" for i in range(4)]
            QD = [f"kQ{c}" for c in range(CH)] + [f"kQa{i}" for i in range(4)]

            ffn_work = []
            ffn_loadidx = []
            W = ffn_work.append
            W(lambda: A("V", lambda e: e.tensor_tensor(
                out=call3, in0=xn3[:, :, 1:513], in1=xn3[:, :, 0:512], op=OP.subtract),
                reads=["xn"], writes=["call"]))
            for c in range(CH):
                W(lambda c=c: A("V", lambda e, c=c: e.scalar_tensor_tensor(
                    out=xrk3[:, c, :], in0=call3[:, c, :], scalar=mu2[:, c:c + 1],
                    in1=xn3[:, c, 0:512], op0=OP.mult, op1=OP.add),
                    reads=["call", "xn", "mu2"], writes=["xrk"]))

            def ffn_proj(g0, rhs3, rhs_key, epi):
                for g in range(g0, g0 + 4):
                    holder = {}
                    ffn_loadidx.append(len(ffn_work))
                    W(lambda holder=holder, g=g: holder.update(zip(("st", "slot", "key"), load_w(g))))
                    for fc2 in range(2):
                        fco = (g - g0) * 2 + fc2
                        bk = 6 + fco % 2
                        bkey = PSK[bk]
                        for ci in range(CH):
                            W(lambda holder=holder, fc2=fc2, ci=ci, bk=bk, bkey=bkey,
                              rhs3=rhs3, rhs_key=rhs_key: A(
                                "T", lambda e, holder=holder, fc2=fc2, ci=ci, bk=bk, rhs3=rhs3: e.matmul(
                                    bank(bk), wb3[holder["slot"]][:, ci, 128 * fc2:128 * fc2 + 128],
                                    rhs3[:, ci, :],
                                    start=(ci == 0), stop=(ci == CH - 1)),
                                reads=[holder["key"], rhs_key],
                                writes=[bkey] if ci in (0, CH - 1) else [],
                                deps=[holder["st"]]))
                        W(lambda fco=fco, bk=bk, bkey=bkey, epi=epi: epi(fco, bk, bkey))

            def sig_epi(fco, bk, bkey):
                A("S", lambda e, bk=bk: e.activation(
                    bank(bk), bank(bk), AF.Tanh, scale=0.5),
                  reads=[bkey, "consts"], writes=[bkey])
                return A("V", lambda e, fco=fco, bk=bk: e.tensor_scalar(
                    out=rsig3[:, fco, :], in0=bank(bk),
                    scalar1=0.5, scalar2=0.5, op0=OP.mult, op1=OP.add),
                  reads=[bkey], writes=["rsigt"])

            def relu_epi(fco, bk, bkey):
                A("S", lambda e, fco=fco, bk=bk: e.activation(
                    call3[:, fco, :], bank(bk), AF.Relu),
                  reads=[bkey, "consts"], writes=["call"])
                return A("V", lambda e, fco=fco: e.tensor_tensor(
                    out=call3[:, fco, :], in0=call3[:, fco, :],
                    in1=call3[:, fco, :], op=OP.mult),
                  reads=["call"], writes=["call"])

            def wv_epi(fco, bk, bkey):
                return A("V", lambda e, fco=fco, bk=bk: e.tensor_tensor(
                    out=xrk3[:, fco, :], in0=bank(bk), in1=rsig3[:, fco, :],
                    op=OP.mult),
                  reads=[bkey, "rsigt"], writes=["xrk"])

            ffn_proj(16, xrk3, "xrk", sig_epi)
            for c in range(CH):
                W(lambda c=c: A("V", lambda e, c=c: e.scalar_tensor_tensor(
                    out=xrk3[:, c, :], in0=call3[:, c, :],
                    scalar=mu2[:, CH + c:CH + c + 1],
                    in1=xn3[:, c, 0:512], op0=OP.mult, op1=OP.add),
                    reads=["call", "xn", "mu2"], writes=["xrk"]))
            ffn_proj(20, xrk3, "xrk", relu_epi)
            ffn_proj(24, call3, "call", wv_epi)
            W(lambda: A("V", lambda e: e.tensor_tensor(
                out=xrk3, in0=xrk3, in1=xres3, op=OP.add),
                reads=["xres", "xrk"], writes=["xrk"]))
            LOOKAHEAD = 30
            marked = set(ffn_loadidx)
            loads = [ffn_work[i] for i in sorted(marked)]
            rest = [f for i, f in enumerate(ffn_work) if i not in marked]
            targets = []
            cnt = 0
            for i in range(len(ffn_work)):
                if i in marked:
                    targets.append(max(0, cnt - LOOKAHEAD))
                else:
                    cnt += 1
            woven = []
            ti = 0
            for pos, f in enumerate(rest):
                while ti < len(targets) and targets[ti] <= pos:
                    woven.append(loads[ti])
                    ti += 1
                woven.append(f)
            woven.extend(loads[ti:])
            ffn_work = woven

            n_ffn = len(ffn_work)
            ffn_i = [0]

            def drain_ffn(k):
                while ffn_i[0] < n_ffn and k > 0:
                    ffn_work[ffn_i[0]]()
                    ffn_i[0] += 1
                    k -= 1

            for h in range(H):
                for qtl in range(4):
                    s = 4 * h + qtl
                    kext = 512 * J + 128 * (qtl + 1)
                    nchunks = (kext + 511) // 512
                    if nchunks < 4:
                        if nchunks == 1:
                            base = s % 6
                        elif nchunks == 2:
                            base = 2 * (s % 3)
                        else:
                            base = 3 * (s % 2)
                        bks = PSK[base:base + nchunks]
                        for chn in range(nchunks):
                            N = min(512, kext - 512 * chn)
                            A("T", lambda e, h=h, qtl=qtl, chn=chn, N=N, base=base: e.matmul(
                                ps[:, 512 * (base + chn): 512 * (base + chn) + N],
                                qt_ap[0:64, 512 * h + 128 * qtl: 512 * h + 128 * qtl + 128],
                                kt_ap[0:64, T * h + 512 * chn: T * h + 512 * chn + N],
                                start=True, stop=True),
                              reads=QD + KD, writes=[bks[chn]])
                        A("V", lambda e, kext=kext, base=base: e.tensor_tensor(
                            out=ps[:, 512 * base + kext - 128: 512 * base + kext],
                            in0=ps[:, 512 * base + kext - 128: 512 * base + kext],
                            in1=maskadd[:], op=OP.add),
                          reads=["maskadd"] + bks[:nchunks], writes=[bks[nchunks - 1]])
                        A("S", lambda e, kext=kext, h=h, qtl=qtl, base=base: e.activation(
                            attn[:][:, 0:kext], ps[:, 512 * base: 512 * base + kext],
                            AF.Exp, scale=1.0 / 32.0,
                            accum_out=sgm[:, 4 * h + qtl: 4 * h + qtl + 1]),
                          reads=bks[:nchunks] + ["consts"],
                          writes=[f"attn{i}" for i in range(nchunks)] + ["sgm"])
                    else:
                        pL = 2 * (s % 3)
                        pH = 2 * ((s + 1) % 3)
                        bksH = PSK[pH:pH + 2]
                        bksL = PSK[pL:pL + 2]
                        for chn in (2, 3):
                            N = min(512, kext - 512 * chn)
                            A("T", lambda e, h=h, qtl=qtl, chn=chn, N=N, pH=pH: e.matmul(
                                ps[:, 512 * (pH + chn - 2): 512 * (pH + chn - 2) + N],
                                qt_ap[0:64, 512 * h + 128 * qtl: 512 * h + 128 * qtl + 128],
                                kt_ap[0:64, T * h + 512 * chn: T * h + 512 * chn + N],
                                start=True, stop=True),
                              reads=QD + KD, writes=[PSK[pH + chn - 2]])
                        A("V", lambda e, kext=kext, pH=pH: e.tensor_tensor(
                            out=ps[:, 512 * pH + kext - 1152: 512 * pH + kext - 1024],
                            in0=ps[:, 512 * pH + kext - 1152: 512 * pH + kext - 1024],
                            in1=maskadd[:], op=OP.add),
                          reads=["maskadd"] + bksH, writes=[PSK[pH + 1]])
                        A("S", lambda e, kext=kext, h=h, qtl=qtl, pH=pH: e.activation(
                            attn[:][:, 0:kext - 1024],
                            ps[:, 512 * pH: 512 * pH + kext - 1024],
                            AF.Exp, scale=1.0 / 32.0,
                            accum_out=lnout[:, 4 * h + qtl: 4 * h + qtl + 1]),
                          reads=bksH + ["consts"],
                          writes=["attn0", "attn1", "lnout"])
                        for chn in (0, 1):
                            A("T", lambda e, h=h, qtl=qtl, chn=chn, pL=pL: e.matmul(
                                ps[:, 512 * (pL + chn): 512 * (pL + chn) + 512],
                                qt_ap[0:64, 512 * h + 128 * qtl: 512 * h + 128 * qtl + 128],
                                kt_ap[0:64, T * h + 512 * chn: T * h + 512 * chn + 512],
                                start=True, stop=True),
                              reads=QD + KD, writes=[PSK[pL + chn]])
                        A("S", lambda e, h=h, qtl=qtl, pL=pL: e.activation(
                            attn[:][:, 0:1024], ps[:, 512 * pL: 512 * pL + 1024],
                            AF.Exp, scale=1.0 / 32.0,
                            accum_out=sgm[:, 4 * h + qtl: 4 * h + qtl + 1]),
                          reads=bksL + ["consts"],
                          writes=["attn0", "attn1", "sgm"])
                    drain_ffn(5)
            drain_ffn(n_ffn)
            if ln_phases:
                ln_phases[0]()
                ln_phases[1]()
                ln_phases = ln_phases[2:]
                A("V", lambda e: e.tensor_copy(xres[:], outb[:]),
                  reads=["outb"], writes=["xres"])
            if J == NJ - 1:
                A("V", lambda e: e.tensor_tensor(
                    out=sgm[:], in0=sgm[:], in1=lnout[:], op=OP.add),
                  reads=["sgm", "lnout"], writes=["sgm"])
            A("S", lambda e: e.activation(lnout[:], sgm[:], AF.Ln),
              reads=["sgm", "consts"], writes=["lnout"])
            A("V", lambda e: e.tensor_scalar(out=b2bf[:], in0=lnout[:], scalar1=-32.0,
                                             scalar2=120.0, op0=OP.mult, op1=OP.add),
              reads=["lnout"], writes=["b2bf"])
            A("T", lambda e: e.matmul(ps[0:64, 0:128], b2bf[:], i128[:],
                                      start=True, stop=True),
              reads=["b2bf", "i128"], writes=["ps0"])
            A("V", lambda e: e.tensor_copy(augs[:], ps[0:64, 0:128]),
              reads=["ps0"], writes=["augs"])
            A("G", lambda e: e.dma_start(out=aug_h, in_=augs[:]),
              reads=["augs"], writes=["augh"], dma_sem="db1")
            A("Y", lambda e: e.dma_start(
                out=qt_ap[64:65, :],
                in_=aug_h.rearrange("c q -> (c q)").unsqueeze(0)),
              reads=["augh"], writes=["qaug"], dma_sem="db2")

            wo_loads = [load_w(g) for g in range(12, 16)]

            def mk_chain(hp2):
                def chain(hp2=hp2):
                    A("S", lambda e: e.activation(sln, srowt[:], AF.Ln),
                      reads=["srow", "consts"], writes=["rsigt"])
                    A("S", lambda e: e.activation(recb[:], sln, AF.Exp, scale=-1.0),
                      reads=["rsigt", "consts"], writes=["recb"])
                    for j in range(2):
                        cpair = hp2 - 1 + j
                        co = 512 * j
                        A("T", lambda e, co=co: e.matmul(
                            bank(5), indic[0:1, :], recb[0:1, co:co + 512],
                            start=True, stop=False),
                          reads=["recb", "indic"], writes=["ps5"])
                        A("T", lambda e, co=co: e.matmul(
                            bank(5), indic[32:33, :], recb[32:33, co:co + 512],
                            start=False, stop=True),
                          reads=["recb", "indic"], writes=["ps5"])
                        A("V", lambda e, cpair=cpair: e.tensor_tensor(
                            out=attn3[:, cpair, :], in0=attn3[:, cpair, :],
                            in1=bank(5), op=OP.mult),
                          reads=["ps5", f"attn{cpair}"], writes=[f"attn{cpair}"])
                return chain

            pending_chain = [None]
            for hp2 in range(CH):
                nkt = 4 * J + 4
                hs = (2 * hp2, 2 * hp2 + 1)
                sb_of = (0, 6)
                def emit_pv(kti):
                    qs = max(0, 128 * (kti - 4 * J))
                    for j in range(2):
                        h = hs[j]
                        pb = 2 * j + kti % 2
                        A("T", lambda e, h=h, kti=kti, qs=qs, pb=pb, j=j, nkt=nkt: e.matmul(
                            ps[0:65, 512 * (2 + j) + qs: 512 * (2 + j) + 512],
                            vt_ap[:, 65 * H * kti + 65 * h: 65 * H * kti + 65 * h + 65],
                            p2[pb][:, qs:512],
                            start=(kti == 0), stop=(kti == nkt - 1)),
                          reads=[f"p2_{pb}", "vtok", "vtones"], writes=[PSK[2 + j]])

                for kti in range(nkt):
                    qs = max(0, 128 * (kti - 4 * J))
                    for j in range(2):
                        h = hs[j]
                        sbk = sb_of[j] + kti % 2
                        skey = PSK[sbk]
                        A("T", lambda e, h=h, kti=kti, qs=qs, sbk=sbk: e.matmul(
                            ps[:, 512 * sbk + qs: 512 * sbk + 512],
                            kt_ap[0:65, T * h + 128 * kti: T * h + 128 * kti + 128],
                            qt_ap[0:65, 512 * h + qs: 512 * h + 512],
                            start=True, stop=True),
                          reads=KD + QD + ["qaug", "ktones"], writes=[skey])
                    for j in range(2):
                        h = hs[j]
                        sbk = sb_of[j] + kti % 2
                        pb = 2 * j + kti % 2
                        A("S", lambda e, kti=kti, qs=qs, sbk=sbk, pb=pb: e.activation(
                            p2[pb][:, qs:512], ps[:, 512 * sbk + qs: 512 * sbk + 512],
                            AF.Exp, scale=0.125),
                          reads=[PSK[sbk], "consts"], writes=[f"p2_{pb}"])
                    if kti >= 4 * J:
                        for j in range(2):
                            pb = 2 * j + kti % 2
                            A("V", lambda e, kti=kti, qs=qs, pb=pb: e.tensor_tensor(
                                out=p2[pb][:, qs:qs + 128],
                                in0=p2[pb][:, qs:qs + 128],
                                in1=tri01[:], op=OP.mult),
                              reads=["tri01", f"p2_{pb}"], writes=[f"p2_{pb}"])
                    if kti > 0:
                        emit_pv(kti - 1)
                    if kti == 2 and pending_chain[0] is not None:
                        pending_chain[0]()
                        pending_chain[0] = None
                emit_pv(nkt - 1)
                for j in range(2):
                    h = hs[j]
                    pvb = 2 + j
                    pvk = PSK[pvb]
                    A("V", lambda e, h=h, pvb=pvb: e.tensor_copy(
                        attn[:][64 * (h % 2): 64 * (h % 2) + 64,
                                512 * (h // 2): 512 * (h // 2) + 512],
                        ps[0:64, 512 * pvb: 512 * pvb + 512]),
                      reads=[pvk], writes=[f"attn{hp2}"])
                    A("V", lambda e, h=h, pvb=pvb: e.tensor_copy(
                        srowt[32 * (h % 2):32 * (h % 2) + 1][:, 512 * ((h // 2) % 2):512 * ((h // 2) % 2) + 512],
                        ps[64:65, 512 * pvb: 512 * pvb + 512]),
                      reads=[pvk], writes=["srow"])
                if hp2 % 2 == 1:
                    pending_chain[0] = mk_chain(hp2)
                if hp2 < len(ln_phases):
                    ln_phases[hp2]()

            def mk_wo(wo_loads=wo_loads, seq=seq, J=J, chain=pending_chain[0]):
                chain_h = [chain]

                def wo_emit():
                    for g in range(12, 16):
                        wst, slot, wkey = wo_loads[g - 12]
                        for fc2 in range(2):
                            fco = (g - 12) * 2 + fc2
                            bk = fco % 2
                            bkey = PSK[bk]
                            for dc in range(CH):
                                if dc == 6 and chain_h[0] is not None:
                                    chain_h[0]()
                                    chain_h[0] = None
                                A("T", lambda e, slot=slot, fc2=fc2, dc=dc, bk=bk: e.matmul(
                                    bank(bk), wb3[slot][:, dc, 128 * fc2:128 * fc2 + 128],
                                    attn3[:, dc, :],
                                    start=(dc == 0), stop=(dc == CH - 1)),
                                  reads=[wkey, f"attn{dc}"],
                                  writes=[bkey] if dc in (0, CH - 1) else [], deps=[wst])
                            A("V", lambda e, fco=fco, bk=bk: e.tensor_tensor(
                                out=outb3[:, fco, :], in0=bank(bk), in1=xrk3[:, fco, :],
                                op=OP.add),
                              reads=[bkey, "xrk"], writes=["outb"])
                        if fc2 == 1:
                            k = g - 12
                            A("G", lambda e, seq=seq, J=J, k=k: e.dma_start(
                                out=y_h[seq, J][:, 1024 * k: 1024 * k + 1024],
                                in_=outb[:][:, 1024 * k: 1024 * k + 1024]),
                              reads=["outb"], dma_sem="dy")
                return wo_emit

            pending_wo[0] = mk_wo()
            pending_chain[0] = None
            if is_last:
                pending_wo[0]()
                pending_wo[0] = None

    last_y = [s for s in P.steps if s.sem == "dy"][-1]
    A("Y", lambda e: e.nop(), deps=[last_y], dma_sem=None)
    P.steps[-1].sem = "fin"
    P.steps[-1].val = 1
    A("G", lambda e: e.nop(), deps=[last_y])
    P.steps[-1].sem = "fin"
    P.steps[-1].val = 2

    with es:
        with nc.Block() as block:
            P.emit(block, semh)
    return nc


def _deint_perm():
    perm = np.arange(3 * C)
    for blk in range(2):
        for h in range(H):
            base = blk * C + h * DH
            idx = np.concatenate([np.arange(0, DH, 2), np.arange(1, DH, 2)])
            perm[base: base + DH] = base + idx
    return perm


def _host_inputs(inputs):
    x = np.ascontiguousarray(np.asarray(inputs["L0_attn_x"], dtype=np.float32))
    wkqv = np.asarray(inputs["L0_attn_W_kqv"], dtype=np.float32)
    wo = np.asarray(inputs["L0_attn_W_o"], dtype=np.float32)
    wr = np.asarray(inputs["L0_ffn_W_r"], dtype=np.float32)
    wk = np.asarray(inputs["L0_ffn_W_k"], dtype=np.float32)
    wv = np.asarray(inputs["L0_ffn_W_v"], dtype=np.float32)
    mu_r = np.asarray(inputs["L0_ffn_mu_r"], dtype=np.float32)
    mu_k = np.asarray(inputs["L0_ffn_mu_k"], dtype=np.float32)

    wkqvt = np.ascontiguousarray(np.ascontiguousarray(wkqv.T)[:, _deint_perm()])
    wot = np.ascontiguousarray(wo.T)
    wrt = np.ascontiguousarray(wr.T)
    wkt = np.ascontiguousarray(wk.T)
    wvt = np.ascontiguousarray(wv.T)
    mu2 = np.ascontiguousarray(np.concatenate(
        [mu_r.reshape(CH, 128).T, mu_k.reshape(CH, 128).T], axis=1))

    pos = np.arange(T, dtype=np.float64)
    inv = THETA ** (-(np.arange(32, dtype=np.float64) / 32.0))
    ang = pos[None, :] * inv[:, None]
    cos32 = np.cos(ang)
    sin32 = np.sin(ang)
    cmap = np.concatenate([cos32, cos32, cos32, cos32], axis=0)
    smap = np.concatenate([sin32, -sin32, sin32, -sin32], axis=0)
    maps = np.ascontiguousarray(
        np.concatenate([cmap, smap], axis=1).astype(np.float32))

    in_maps = []
    for c in range(NCORES):
        xb = x[BSEQ * c: BSEQ * (c + 1)]
        xc = np.ascontiguousarray(
            xb.reshape(BSEQ, NJ, 512, CH, 128).transpose(0, 1, 4, 3, 2)
        ).reshape(BSEQ, NJ, 128, CH * 512)
        in_maps.append({
            "x": xc, "wkqvt": wkqvt, "wrt": wrt, "wkt": wkt, "wvt": wvt,
            "wot": wot, "mu2": mu2, "ropemaps": maps,
        })
    return in_maps


def kernel(**inputs):
    in_maps = _host_inputs(inputs)
    nc = build_program()
    res = run_bass_kernel_spmd(nc, in_maps, core_ids=list(range(NCORES)))
    outs = []
    for c in range(NCORES):
        y = np.asarray(res.results[c]["y"]).reshape(BSEQ, NJ, 128, CH, 512)
        outs.append(y.transpose(0, 1, 4, 3, 2).reshape(BSEQ, T, C))
    return np.ascontiguousarray(np.concatenate(outs, axis=0), dtype=np.float32)


if __name__ == "__main__":
    nc = build_program()
    print("built ok, instructions:", sum(1 for _ in nc.all_instructions()))


# version 32
# speedup vs baseline: 1.0059x; 1.0059x over previous
import numpy as np
import concourse.bass as bass
import concourse.mybir as mybir
from concourse.bass_utils import run_bass_kernel_spmd
from contextlib import ExitStack

F32 = mybir.dt.float32
BF16 = mybir.dt.bfloat16
AF = mybir.ActivationFunctionType
OP = mybir.AluOpType

NCORES = 8
BSEQ = 2
T = 2048
C = 1024
H = 16
DH = 64
CH = C // 128
NJ = T // 512
KTT = T // 128
THETA = 10000.0


class Step:
    __slots__ = ("eng", "fn", "sem", "val", "deps", "snap", "inc", "waits",
                 "inc_on", "force_prev")

    def __init__(self, eng, fn, sem, val, deps, inc):
        self.eng, self.fn, self.sem, self.val = eng, fn, sem, val
        self.deps, self.inc = deps, inc
        self.snap = None
        self.waits = ()
        self.force_prev = None


class Prog:
    ENGS = ("Y", "G", "T", "V", "S")
    NOCHAIN = ("dka", "dkb", "dqa", "dqb")

    def __init__(self):
        self.steps = []
        self.sem_cnt = {}
        self.last_w = {}
        self.last_r = {}
        self.last_dma = {}

    def add(self, eng, fn, reads=(), writes=(), deps=(), dma_sem=None):
        is_dma = dma_sem is not None
        dl = [d for d in deps if d is not None]
        for k in reads:
            w = self.last_w.get(k)
            if w is not None:
                dl.append(w)
        for k in writes:
            w = self.last_w.get(k)
            if w is not None:
                dl.append(w)
            dl.extend(self.last_r.get(k, ()))
        if dma_sem is not None:
            sem, inc = dma_sem, 16
        else:
            sem, inc = "e" + eng, 1
        s = Step(eng, fn, sem, 0, dl, inc)
        if dma_sem is not None and dma_sem not in self.NOCHAIN:
            prev = self.last_dma.get(dma_sem)
            if prev is not None:
                dl.append(prev)
                s.force_prev = prev
            self.last_dma[dma_sem] = s
        self.steps.append(s)
        for k in reads:
            self.last_r.setdefault(k, []).append(s)
        for k in writes:
            self.last_w[k] = s
            self.last_r[k] = []
        return s

    def plan(self):
        for s in self.steps:
            s.inc_on = (s.inc == 16)
        for s in self.steps:
            for d in s.deps:
                d.inc_on = True
        self.sem_cnt = {}
        for s in self.steps:
            if s.inc_on:
                v = self.sem_cnt.get(s.sem, 0) + s.inc
                self.sem_cnt[s.sem] = v
                s.val = v
        know = {e: {} for e in self.ENGS}
        for s in self.steps:
            k = know[s.eng]
            wd = {}
            for d in s.deps:
                if k.get(d.sem, 0) < d.val:
                    wd[d.sem] = max(wd.get(d.sem, 0), d.val)
                    k[d.sem] = d.val
                    if d.snap:
                        for sm, v in d.snap.items():
                            if k.get(sm, 0) < v:
                                k[sm] = v
            if s.force_prev is not None and s.force_prev.val > 0:
                fp = s.force_prev
                if wd.get(fp.sem, 0) < fp.val:
                    wd[fp.sem] = fp.val
                    if k.get(fp.sem, 0) < fp.val:
                        k[fp.sem] = fp.val
            s.waits = sorted(wd.items())
            snap = dict(k)
            snap[s.sem] = max(snap.get(s.sem, 0), s.val)
            s.snap = snap

    def emit(self, block, handles):
        self.plan()
        by_eng = {e: [s for s in self.steps if s.eng == e] for e in self.ENGS}

        def runner(ename):
            def run(eng):
                for s in by_eng[ename]:
                    for sm, v in s.waits:
                        eng.wait_ge(handles[sm], v)
                    inst = s.fn(eng)
                    if s.inc_on:
                        inst.then_inc(handles[s.sem], s.inc)
            return run

        block.sync(runner("Y"))
        block.gpsimd(runner("G"))
        block.tensor(runner("T"))
        block.vector(runner("V"))
        block.scalar(runner("S"))


def build_program():
    nc = bass.Bass("TRN2", target_bir_lowering=False)
    P = Prog()
    A = P.add

    x_h = nc.dram_tensor("x", [BSEQ, NJ, 128, CH * 512], F32, kind="ExternalInput").ap()
    wkqv_h = nc.dram_tensor("wkqvt", [C, 3 * C], F32, kind="ExternalInput").ap()
    wo_h = nc.dram_tensor("wot", [C, C], F32, kind="ExternalInput").ap()
    wr_h = nc.dram_tensor("wrt", [C, C], F32, kind="ExternalInput").ap()
    wk_h = nc.dram_tensor("wkt", [C, C], F32, kind="ExternalInput").ap()
    wv_h = nc.dram_tensor("wvt", [C, C], F32, kind="ExternalInput").ap()
    mu_h = nc.dram_tensor("mu2", [128, 2 * CH], F32, kind="ExternalInput").ap()
    maps_h = nc.dram_tensor("ropemaps", [128, 2 * T], F32, kind="ExternalInput").ap()
    y_h = nc.dram_tensor("y", [BSEQ, NJ, 128, CH * 512], F32, kind="ExternalOutput").ap()
    NGRP = 28
    wtmp_h = nc.dram_tensor("wtmp", [NGRP, 128, 2048], BF16, kind="Internal").ap()
    aug_h = nc.dram_tensor("augh", [64, 128], BF16, kind="Internal").ap()

    es = ExitStack()
    sbuf = lambda name, shape, dt: es.enter_context(nc.sbuf_tensor(name, shape, dt))

    NSLOT = 5
    wbuf = [sbuf(f"wbuf{i}", [128, 2048], BF16) for i in range(NSLOT)]
    kt = sbuf("ktil", [65, H * T], BF16)
    vtok = sbuf("vtok", [128, KTT * H * 65], BF16)
    qt = sbuf("qtil", [65, H * 512], BF16)
    xres = sbuf("xres", [128, CH * 512], BF16)
    outb = sbuf("outb", [128, CH * 512], BF16)
    xn = sbuf("xn", [128, CH * 513], BF16)
    call = sbuf("call", [128, CH * 512], BF16)
    attn = sbuf("attn", [128, CH * 512], BF16)
    p2 = [sbuf(f"p2_{i}", [128, 512], BF16) for i in range(4)]
    rsig = sbuf("rsigt", [128, CH * 512], BF16)
    xrk = sbuf("xrk", [128, CH * 512], BF16)
    recb = sbuf("recb", [33, 1024], BF16)
    srowt = sbuf("srowt", [33, 1024], F32)
    sgm = sbuf("sgm", [128, 64], F32)
    lnout = sbuf("lnout", [128, 64], F32)
    b2bf = sbuf("b2bf", [128, 64], BF16)
    mu2 = sbuf("mu2s", [128, 2 * CH], F32)
    r_s = sbuf("r_s", [1, 512], F32)
    s2_s = sbuf("s2_s", [1, 512], F32)
    maskadd = sbuf("maskadd", [128, 128], F32)
    tri01 = sbuf("tri01", [128, 128], BF16)
    i128 = sbuf("i128", [128, 128], BF16)
    indic = sbuf("indic", [33, 128], BF16)
    ones128 = sbuf("ones128", [128, 1], BF16)
    onesf1 = sbuf("onesf1", [1, 128], F32)
    consts = sbuf("consts", [128, 8], F32)
    xtail = sbuf("xtail", [128, CH], BF16)
    augs = sbuf("augs", [64, 128], BF16)

    ps_t = es.enter_context(nc.psum_tensor("ps", [128, 4096], F32))
    ps = ps_t.ap() if hasattr(ps_t, "ap") else ps_t[:]

    def bank(b):
        return ps[:, 512 * b: 512 * b + 512]


    sem_names = ["eY", "eG", "eT", "eV", "eS", "fin",
                 "dwl0", "dwl1", "dwl2", "dwl3", "dwl4", "dws0", "dws1", "dws2", "dws3", "dws4", "dst0", "dst1", "dst2", "dst3", "dst4", "dmu",
                 "dx", "dmap", "dka", "dkb", "dqa", "dqb", "dy", "dy2", "db1", "db2"]
    semh = {n: es.enter_context(nc.semaphore(n)) for n in sem_names}

    CONST_VALS = [0.0, 1.0, 1.0 / 32.0, 0.125, 1e-5, -1.0, 0.5, -0.5]
    for i, v in enumerate(CONST_VALS):
        nc.const_aps.register(F32, v, consts[:, i:i + 1])

    wb3 = [wbuf[i][:].rearrange("p (c f) -> p c f", c=CH) for i in range(NSLOT)]
    vt_ap = vtok[:]
    vt3 = vt_ap.rearrange("p (k h d) -> p k h d", k=KTT, h=H)
    kt_ap = kt[:]
    rsf32 = rsig[:].bitcast(F32)
    mapblk = rsf32[:, 0:1024]
    sln = rsf32[0:33, 1024:2048]
    qt_ap = qt[:]
    call3 = call[:].rearrange("p (c t) -> p c t", c=CH)
    xn_ap = xn[:]
    xn3 = xn_ap.rearrange("p (c t) -> p c t", c=CH)
    attn3 = attn[:].rearrange("p (c t) -> p c t", c=CH)
    pack3 = xrk[:].rearrange("p (c t) -> p c t", c=CH)
    rsig3 = rsig[:].rearrange("p (c t) -> p c t", c=CH)
    xrk3 = xrk[:].rearrange("p (c t) -> p c t", c=CH)
    xres3 = xres[:].rearrange("p (c t) -> p c t", c=CH)
    outb3 = outb[:].rearrange("p (c t) -> p c t", c=CH)

    for i, v in enumerate(CONST_VALS):
        A("V", lambda e, i=i, v=v: e.memset(consts[:, i:i + 1], v), writes=["consts"])
    A("V", lambda e: e.memset(ones128[:], 1.0), writes=["ones128"])
    A("V", lambda e: e.memset(onesf1[:], 1.0), writes=["onesf1"])
    A("V", lambda e: e.memset(kt[64:65, :], 1.0), writes=["ktones"])
    A("V", lambda e: e.memset(indic[0:1, 0:64], 1.0), writes=["indic"])
    A("V", lambda e: e.memset(indic[0:1, 64:128], 0.0), writes=["indic"])
    A("V", lambda e: e.memset(indic[32:33, 0:64], 0.0), writes=["indic"])
    A("V", lambda e: e.memset(indic[32:33, 64:128], 1.0), writes=["indic"])
    A("V", lambda e: e.memset(xtail[:], 0.0), writes=["xtail"])
    A("V", lambda e: e.memset(srowt[:], 1.0), writes=["srow"])
    A("V", lambda e: e.memset(vt3[:, :, :, 64:65], 1.0), writes=["vtones"])
    A("V", lambda e: e.memset(maskadd[:], 0.0), writes=["maskadd"])
    A("G", lambda e: e.affine_select(out=maskadd[:], in_=maskadd[:],
                                     pattern=[[-1, 128]], compare_op=OP.is_ge,
                                     fill=-1.0e4, base=0, channel_multiplier=1),
      reads=["maskadd"], writes=["maskadd"])
    A("V", lambda e: e.memset(tri01[:], 1.0), writes=["tri01"])
    A("G", lambda e: e.affine_select(out=tri01[:], in_=tri01[:],
                                     pattern=[[1, 128]], compare_op=OP.is_ge,
                                     fill=0.0, base=0, channel_multiplier=-1),
      reads=["tri01"], writes=["tri01"])
    A("V", lambda e: e.memset(i128[:], 1.0), writes=["i128"])
    A("G", lambda e: e.affine_select(out=i128[:], in_=i128[:],
                                     pattern=[[1, 128]], compare_op=OP.is_ge,
                                     fill=0.0, base=0, channel_multiplier=-1),
      reads=["i128"], writes=["i128"])
    A("G", lambda e: e.affine_select(out=i128[:], in_=i128[:],
                                     pattern=[[-1, 128]], compare_op=OP.is_ge,
                                     fill=0.0, base=0, channel_multiplier=1),
      reads=["i128"], writes=["i128"])
    A("G", lambda e: e.dma_start(out=mu2[:], in_=mu_h), writes=["mu2"], dma_sem="dmu")

    wsrcs = ([(wkqv_h, g) for g in range(12)] + [(wo_h, g) for g in range(4)] +
             [(wr_h, g) for g in range(4)] + [(wk_h, g) for g in range(4)] +
             [(wv_h, g) for g in range(4)])
    wstore_last = []
    for gi, (h, g) in enumerate(wsrcs):
        slot = gi % NSLOT
        key = f"wbuf{slot}"
        src = h.rearrange("(c p) f -> p c f", p=128)[:, :, 256 * g: 256 * g + 256]
        A("G", lambda e, src=src, slot=slot: e.dma_start(out=wb3[slot], in_=src),
          writes=[key], dma_sem=f"dwl{slot}")
        st = A("Y", lambda e, gi=gi, slot=slot: e.dma_start(out=wtmp_h[gi], in_=wbuf[slot][:]),
               reads=[key], dma_sem=f"dws{slot}")
        if gi >= len(wsrcs) - NSLOT:
            wstore_last.append(st)

    wload_i = [0]

    def load_w(gi):
        slot = wload_i[0] % NSLOT
        wload_i[0] += 1
        key = f"wbuf{slot}"
        st = A("Y", lambda e, gi=gi, slot=slot: e.dma_start(out=wbuf[slot][:], in_=wtmp_h[gi]),
               writes=[key], deps=wstore_last, dma_sem=f"dst{slot}")
        return st, slot, key

    PSK = [f"ps{i}" for i in range(8)]

    def emit_ln_phases(seq, J):
        first = (seq == 0 and J == 0)
        b4row = ps[0:1, 2048:2560]

        def ph_dma(seq=seq, J=J, first=first):
            dstk = "xres" if first else "outb"
            dst = xres[:] if first else outb[:]
            A("G", lambda e, seq=seq, J=J, dst=dst: e.dma_start(
                out=dst, in_=x_h[seq, J]),
              writes=[dstk], dma_sem="dx")

        def ph_maps(J=J):
            A("Y", lambda e, J=J: e.dma_start(
                out=mapblk.rearrange("p (g t) -> p g t", g=2),
                in_=maps_h.rearrange("p (g t) -> p g t", g=2)[:, :, 512 * J: 512 * J + 512]),
              writes=["mapblk", "rsigt"], dma_sem="dmap")

        def ph0():
            for c in range(CH):
                A("T", lambda e, c=c: e.matmul(b4row, ones128[:], xres3[:, c, :],
                                               start=(c == 0), stop=(c == CH - 1)),
                  reads=["xres", "ones128"],
                  writes=["ps4"] if c in (0, CH - 1) else [])

        def ph1():
            A("V", lambda e: e.tensor_scalar(out=s2_s[:], in0=b4row,
                                             scalar1=1.0 / C, scalar2=None, op0=OP.mult),
              reads=["ps4"], writes=["s2_s"])
            A("V", lambda e: e.tensor_tensor(out=r_s[:], in0=s2_s[:], in1=s2_s[:], op=OP.mult),
              reads=["s2_s"], writes=["r_s"])
            A("V", lambda e: e.tensor_tensor(out=call3, in0=xres3, in1=xres3, op=OP.mult),
              reads=["xres"], writes=["call"])
            for c in range(CH):
                A("T", lambda e, c=c: e.matmul(b4row, ones128[:], call3[:, c, :],
                                               start=(c == 0), stop=(c == CH - 1)),
                  reads=["call", "ones128"],
                  writes=["ps4"] if c in (0, CH - 1) else [])

        def ph2(first=first):
            A("V", lambda e: e.scalar_tensor_tensor(out=r_s[:], in0=b4row,
                                                    scalar=1.0 / C, in1=r_s[:],
                                                    op0=OP.mult, op1=OP.subtract),
              reads=["ps4", "r_s"], writes=["r_s"])
            A("T", lambda e: e.matmul(bank(4), onesf1[:], s2_s[:], start=True, stop=True),
              reads=["s2_s", "onesf1", "ps4"], writes=["ps4"])
            A("S", lambda e: e.activation(s2_s[:], r_s[:], AF.Ln, bias=1e-5, scale=1.0),
              reads=["r_s", "consts"], writes=["s2_s"])
            A("S", lambda e: e.activation(r_s[:], s2_s[:], AF.Exp, scale=-0.5),
              reads=["s2_s", "consts"], writes=["r_s"])
            if not first:
                A("V", lambda e: e.tensor_copy(xtail[:], xn_ap[:, 512::513]),
                  reads=["xn"], writes=["xtail"])

        def ph3():
            A("V", lambda e: e.tensor_tensor(
                out=call3, in0=xres3,
                in1=bank(4).unsqueeze(1).broadcast_to([128, CH, 512]), op=OP.subtract),
              reads=["xres", "ps4", "call"], writes=["call"])

        def ph4(J=J):
            A("T", lambda e: e.matmul(bank(4), onesf1[:], r_s[:], start=True, stop=True),
              reads=["r_s", "onesf1"], writes=["ps4"])
            A("V", lambda e: e.tensor_tensor(
                out=xn3[:, :, 1:513], in0=call3,
                in1=bank(4).unsqueeze(1).broadcast_to([128, CH, 512]), op=OP.mult),
              reads=["call", "ps4"], writes=["xn"])
            if J == 0:
                A("V", lambda e: e.memset(xn_ap[:, 0::513], 0.0), reads=["xn"], writes=["xn"])
            else:
                A("V", lambda e: e.tensor_copy(xn_ap[:, 0::513], xtail[:]),
                  reads=["xtail", "xn"], writes=["xn"])

        return [ph_dma, ph_maps, ph0, ph1, ph2, ph3, ph4]

    def emit_ln(seq, J):
        for ph in emit_ln_phases(seq, J):
            ph()

    emit_ln(0, 0)
    pending_wo = [None]
    for seq in range(BSEQ):
        for J in range(NJ):
            is_last = (seq == BSEQ - 1 and J == NJ - 1)
            if pending_wo[0] is not None:
                pending_wo[0]()
                pending_wo[0] = None
            ln_phases = []
            if not is_last:
                nseq, nJ = (seq, J + 1) if J + 1 < NJ else (seq + 1, 0)
                ln_phases = emit_ln_phases(nseq, nJ)
            for g in range(8):
                wst, slot, wkey = load_w(g)
                if True:
                    part = "K" if g < 4 else "Q"
                    for fc2 in range(2):
                        gch = (g % 4) * 2 + fc2
                        bk = 2 + gch % 2
                        bkey = PSK[bk]
                        for ci in range(CH):
                            A("T", lambda e, slot=slot, fc2=fc2, ci=ci, bk=bk: e.matmul(
                                bank(bk), wb3[slot][:, ci, 128 * fc2:128 * fc2 + 128],
                                xn3[:, ci, 1:513],
                                start=(ci == 0), stop=(ci == CH - 1)),
                              reads=[wkey, "xn"],
                              writes=[bkey] if ci in (0, CH - 1) else [], deps=[wst])
                        if part == "K":
                            key = f"kK{gch}"
                            tgt = kt_ap
                            sA = T * (2 * gch) + 512 * J
                            sB = T * (2 * gch + 1) + 512 * J
                            extra = []
                        else:
                            key = f"kQ{gch}"
                            tgt = qt_ap
                            sA = 512 * (2 * gch)
                            sB = 512 * (2 * gch + 1)
                            extra = [f"kQa{i}" for i in range(4)]
                        A("V", lambda e, gch=gch, bk=bk: e.tensor_tensor(
                            out=call3[:, gch, :], in0=bank(bk), in1=mapblk[:, 512:1024],
                            op=OP.mult),
                          reads=[bkey, "mapblk", "rsigt"], writes=["call"])
                        A("V", lambda e, bk=bk, tgt=tgt, sA=sA: e.tensor_tensor(
                            out=tgt[0:64, sA:sA + 512], in0=ps[0:64, 512 * bk:512 * bk + 512],
                            in1=mapblk[0:64, 0:512], op=OP.mult),
                          reads=[bkey, "mapblk", "rsigt"] + extra, writes=[key])
                        A("V", lambda e, bk=bk, tgt=tgt, sB=sB: e.tensor_tensor(
                            out=tgt[0:64, sB:sB + 512], in0=ps[64:128, 512 * bk:512 * bk + 512],
                            in1=mapblk[0:64, 0:512], op=OP.mult),
                          reads=[bkey, "mapblk", "rsigt"] + extra, writes=[key])
                    if g % 4 == 3:
                        if part == "K":
                            d32 = kt_ap[0:32, :].rearrange("p (h t) -> p h t", h=H)
                            d32b = kt_ap[32:64, :].rearrange("p (h t) -> p h t", h=H)
                            dsts32 = [d32[:, 0::2, 512 * J:512 * J + 512],
                                      d32b[:, 0::2, 512 * J:512 * J + 512],
                                      d32[:, 1::2, 512 * J:512 * J + 512],
                                      d32b[:, 1::2, 512 * J:512 * J + 512]]
                            sb2, kd = "dkb", "kK"
                        else:
                            d32 = qt_ap[0:32, :].rearrange("p (h t) -> p h t", h=H)
                            d32b = qt_ap[32:64, :].rearrange("p (h t) -> p h t", h=H)
                            dsts32 = [d32[:, 0::2, :], d32b[:, 0::2, :],
                                      d32[:, 1::2, :], d32b[:, 1::2, :]]
                            sb2, kd = "dqb", "kQ"
                        srcs = [call3[32:64, :, :], call3[0:32, :, :],
                                call3[96:128, :, :], call3[64:96, :, :]]
                        allk = [f"{kd}{c}" for c in range(CH)]
                        for i in range(4):
                            A("G", lambda e, i=i, dsts32=dsts32, srcs=srcs: e.dma_start(
                                out=dsts32[i], in_=srcs[i], accum_op=OP.add),
                              reads=["call"] + allk, writes=[f"{kd}a{i}"], dma_sem=sb2)
            KD = [f"kK{c}" for c in range(CH)] + [f"kKa{i}" for i in range(4)]
            QD = [f"kQ{c}" for c in range(CH)] + [f"kQa{i}" for i in range(4)]

            ffn_work = []
            ffn_loadidx = []
            W = ffn_work.append
            for g in range(8, 12):
                holder = {}
                ffn_loadidx.append(len(ffn_work))
                W(lambda holder=holder, g=g: holder.update(
                    zip(("st", "slot", "key"), load_w(g))))
                for tt in range(4):
                    bk = 6 + tt % 2
                    bkey = PSK[bk]
                    for ci in range(CH):
                        W(lambda holder=holder, tt=tt, ci=ci, bk=bk, bkey=bkey: A(
                            "T", lambda e, holder=holder, tt=tt, ci=ci, bk=bk: e.matmul(
                                ps[:, 512 * bk: 512 * bk + 256],
                                xn3[:, ci, 1 + 128 * tt:1 + 128 * tt + 128],
                                wb3[holder["slot"]][:, ci, :],
                                start=(ci == 0), stop=(ci == CH - 1)),
                            reads=[holder["key"], "xn"],
                            writes=[bkey] if ci in (0, CH - 1) else [],
                            deps=[holder["st"]]))
                    ktile = 4 * J + tt
                    h0 = (g - 8) * 4
                    W(lambda ktile=ktile, h0=h0, bk=bk, bkey=bkey: A(
                        "V", lambda e, ktile=ktile, h0=h0, bk=bk: e.tensor_copy(
                            vt3[:, ktile:ktile + 1, h0:h0 + 4, 0:64],
                            ps[:, 512 * bk: 512 * bk + 256].rearrange(
                                "p (o h d) -> p o h d", o=1, h=4)),
                        reads=[bkey], writes=["vtok"]))
            W(lambda: A("V", lambda e: e.tensor_tensor(
                out=call3, in0=xn3[:, :, 1:513], in1=xn3[:, :, 0:512], op=OP.subtract),
                reads=["xn"], writes=["call"]))
            for c in range(CH):
                W(lambda c=c: A("V", lambda e, c=c: e.scalar_tensor_tensor(
                    out=xrk3[:, c, :], in0=call3[:, c, :], scalar=mu2[:, c:c + 1],
                    in1=xn3[:, c, 0:512], op0=OP.mult, op1=OP.add),
                    reads=["call", "xn", "mu2"], writes=["xrk"]))

            def ffn_proj(g0, rhs3, rhs_key, epi):
                for g in range(g0, g0 + 4):
                    holder = {}
                    ffn_loadidx.append(len(ffn_work))
                    W(lambda holder=holder, g=g: holder.update(zip(("st", "slot", "key"), load_w(g))))
                    for fc2 in range(2):
                        fco = (g - g0) * 2 + fc2
                        bk = 6 + fco % 2
                        bkey = PSK[bk]
                        for ci in range(CH):
                            W(lambda holder=holder, fc2=fc2, ci=ci, bk=bk, bkey=bkey,
                              rhs3=rhs3, rhs_key=rhs_key: A(
                                "T", lambda e, holder=holder, fc2=fc2, ci=ci, bk=bk, rhs3=rhs3: e.matmul(
                                    bank(bk), wb3[holder["slot"]][:, ci, 128 * fc2:128 * fc2 + 128],
                                    rhs3[:, ci, :],
                                    start=(ci == 0), stop=(ci == CH - 1)),
                                reads=[holder["key"], rhs_key],
                                writes=[bkey] if ci in (0, CH - 1) else [],
                                deps=[holder["st"]]))
                        W(lambda fco=fco, bk=bk, bkey=bkey, epi=epi: epi(fco, bk, bkey))

            def sig_epi(fco, bk, bkey):
                A("S", lambda e, bk=bk: e.activation(
                    bank(bk), bank(bk), AF.Tanh, scale=0.5),
                  reads=[bkey, "consts"], writes=[bkey])
                return A("V", lambda e, fco=fco, bk=bk: e.tensor_scalar(
                    out=rsig3[:, fco, :], in0=bank(bk),
                    scalar1=0.5, scalar2=0.5, op0=OP.mult, op1=OP.add),
                  reads=[bkey], writes=["rsigt"])

            def relu_epi(fco, bk, bkey):
                A("S", lambda e, fco=fco, bk=bk: e.activation(
                    call3[:, fco, :], bank(bk), AF.Relu),
                  reads=[bkey, "consts"], writes=["call"])
                return A("V", lambda e, fco=fco: e.tensor_tensor(
                    out=call3[:, fco, :], in0=call3[:, fco, :],
                    in1=call3[:, fco, :], op=OP.mult),
                  reads=["call"], writes=["call"])

            def wv_epi(fco, bk, bkey):
                return A("V", lambda e, fco=fco, bk=bk: e.tensor_tensor(
                    out=xrk3[:, fco, :], in0=bank(bk), in1=rsig3[:, fco, :],
                    op=OP.mult),
                  reads=[bkey, "rsigt"], writes=["xrk"])

            ffn_proj(16, xrk3, "xrk", sig_epi)
            for c in range(CH):
                W(lambda c=c: A("V", lambda e, c=c: e.scalar_tensor_tensor(
                    out=xrk3[:, c, :], in0=call3[:, c, :],
                    scalar=mu2[:, CH + c:CH + c + 1],
                    in1=xn3[:, c, 0:512], op0=OP.mult, op1=OP.add),
                    reads=["call", "xn", "mu2"], writes=["xrk"]))
            ffn_proj(20, xrk3, "xrk", relu_epi)
            ffn_proj(24, call3, "call", wv_epi)
            W(lambda: A("V", lambda e: e.tensor_tensor(
                out=xrk3, in0=xrk3, in1=xres3, op=OP.add),
                reads=["xres", "xrk"], writes=["xrk"]))
            LOOKAHEAD = 30
            marked = set(ffn_loadidx)
            loads = [ffn_work[i] for i in sorted(marked)]
            rest = [f for i, f in enumerate(ffn_work) if i not in marked]
            targets = []
            cnt = 0
            for i in range(len(ffn_work)):
                if i in marked:
                    targets.append(max(0, cnt - LOOKAHEAD))
                else:
                    cnt += 1
            woven = []
            ti = 0
            for pos, f in enumerate(rest):
                while ti < len(targets) and targets[ti] <= pos:
                    woven.append(loads[ti])
                    ti += 1
                woven.append(f)
            woven.extend(loads[ti:])
            ffn_work = woven

            n_ffn = len(ffn_work)
            ffn_i = [0]

            def drain_ffn(k):
                while ffn_i[0] < n_ffn and k > 0:
                    ffn_work[ffn_i[0]]()
                    ffn_i[0] += 1
                    k -= 1

            for h in range(H):
                for qtl in range(4):
                    s = 4 * h + qtl
                    kext = 512 * J + 128 * (qtl + 1)
                    nchunks = (kext + 511) // 512
                    if nchunks < 4:
                        if nchunks == 1:
                            base = s % 4
                        elif nchunks == 2:
                            base = 2 * (s % 3)
                        else:
                            base = 3 * (s % 2)
                        bks = PSK[base:base + nchunks]
                        for chn in range(nchunks):
                            N = min(512, kext - 512 * chn)
                            A("T", lambda e, h=h, qtl=qtl, chn=chn, N=N, base=base: e.matmul(
                                ps[:, 512 * (base + chn): 512 * (base + chn) + N],
                                qt_ap[0:64, 512 * h + 128 * qtl: 512 * h + 128 * qtl + 128],
                                kt_ap[0:64, T * h + 512 * chn: T * h + 512 * chn + N],
                                start=True, stop=True),
                              reads=QD + KD, writes=[bks[chn]])
                        A("V", lambda e, kext=kext, base=base: e.tensor_tensor(
                            out=ps[:, 512 * base + kext - 128: 512 * base + kext],
                            in0=ps[:, 512 * base + kext - 128: 512 * base + kext],
                            in1=maskadd[:], op=OP.add),
                          reads=["maskadd"] + bks[:nchunks], writes=[bks[nchunks - 1]])
                        A("S", lambda e, kext=kext, h=h, qtl=qtl, base=base: e.activation(
                            attn[:][:, 0:kext], ps[:, 512 * base: 512 * base + kext],
                            AF.Exp, scale=1.0 / 32.0,
                            accum_out=sgm[:, 4 * h + qtl: 4 * h + qtl + 1]),
                          reads=bks[:nchunks] + ["consts"],
                          writes=[f"attn{i}" for i in range(nchunks)] + ["sgm"])
                    else:
                        pL = 2 * (s % 3)
                        pH = 2 * ((s + 1) % 3)
                        bksH = PSK[pH:pH + 2]
                        bksL = PSK[pL:pL + 2]
                        for chn in (2, 3):
                            N = min(512, kext - 512 * chn)
                            A("T", lambda e, h=h, qtl=qtl, chn=chn, N=N, pH=pH: e.matmul(
                                ps[:, 512 * (pH + chn - 2): 512 * (pH + chn - 2) + N],
                                qt_ap[0:64, 512 * h + 128 * qtl: 512 * h + 128 * qtl + 128],
                                kt_ap[0:64, T * h + 512 * chn: T * h + 512 * chn + N],
                                start=True, stop=True),
                              reads=QD + KD, writes=[PSK[pH + chn - 2]])
                        A("V", lambda e, kext=kext, pH=pH: e.tensor_tensor(
                            out=ps[:, 512 * pH + kext - 1152: 512 * pH + kext - 1024],
                            in0=ps[:, 512 * pH + kext - 1152: 512 * pH + kext - 1024],
                            in1=maskadd[:], op=OP.add),
                          reads=["maskadd"] + bksH, writes=[PSK[pH + 1]])
                        A("S", lambda e, kext=kext, h=h, qtl=qtl, pH=pH: e.activation(
                            attn[:][:, 0:kext - 1024],
                            ps[:, 512 * pH: 512 * pH + kext - 1024],
                            AF.Exp, scale=1.0 / 32.0,
                            accum_out=lnout[:, 4 * h + qtl: 4 * h + qtl + 1]),
                          reads=bksH + ["consts"],
                          writes=["attn0", "attn1", "lnout"])
                        for chn in (0, 1):
                            A("T", lambda e, h=h, qtl=qtl, chn=chn, pL=pL: e.matmul(
                                ps[:, 512 * (pL + chn): 512 * (pL + chn) + 512],
                                qt_ap[0:64, 512 * h + 128 * qtl: 512 * h + 128 * qtl + 128],
                                kt_ap[0:64, T * h + 512 * chn: T * h + 512 * chn + 512],
                                start=True, stop=True),
                              reads=QD + KD, writes=[PSK[pL + chn]])
                        A("S", lambda e, h=h, qtl=qtl, pL=pL: e.activation(
                            attn[:][:, 0:1024], ps[:, 512 * pL: 512 * pL + 1024],
                            AF.Exp, scale=1.0 / 32.0,
                            accum_out=sgm[:, 4 * h + qtl: 4 * h + qtl + 1]),
                          reads=bksL + ["consts"],
                          writes=["attn0", "attn1", "sgm"])
                    drain_ffn(7)
            drain_ffn(n_ffn)
            if ln_phases:
                ln_phases[0]()
                ln_phases[1]()
                ln_phases = ln_phases[2:]
                A("V", lambda e: e.tensor_copy(xres[:], outb[:]),
                  reads=["outb"], writes=["xres"])
            if J == NJ - 1:
                A("V", lambda e: e.tensor_tensor(
                    out=sgm[:], in0=sgm[:], in1=lnout[:], op=OP.add),
                  reads=["sgm", "lnout"], writes=["sgm"])
            A("S", lambda e: e.activation(lnout[:], sgm[:], AF.Ln),
              reads=["sgm", "consts"], writes=["lnout"])
            A("V", lambda e: e.tensor_scalar(out=b2bf[:], in0=lnout[:], scalar1=-32.0,
                                             scalar2=120.0, op0=OP.mult, op1=OP.add),
              reads=["lnout"], writes=["b2bf"])
            A("T", lambda e: e.matmul(ps[0:64, 0:128], b2bf[:], i128[:],
                                      start=True, stop=True),
              reads=["b2bf", "i128"], writes=["ps0"])
            A("V", lambda e: e.tensor_copy(augs[:], ps[0:64, 0:128]),
              reads=["ps0"], writes=["augs"])
            A("G", lambda e: e.dma_start(out=aug_h, in_=augs[:]),
              reads=["augs"], writes=["augh"], dma_sem="db1")
            A("Y", lambda e: e.dma_start(
                out=qt_ap[64:65, :],
                in_=aug_h.rearrange("c q -> (c q)").unsqueeze(0)),
              reads=["augh"], writes=["qaug"], dma_sem="db2")

            wo_loads = [load_w(g) for g in range(12, 16)]

            def mk_chain(hp2):
                def chain(hp2=hp2):
                    A("S", lambda e: e.activation(sln, srowt[:], AF.Ln),
                      reads=["srow", "consts"], writes=["rsigt"])
                    A("S", lambda e: e.activation(recb[:], sln, AF.Exp, scale=-1.0),
                      reads=["rsigt", "consts"], writes=["recb"])
                    for j in range(2):
                        cpair = hp2 - 1 + j
                        co = 512 * j
                        A("T", lambda e, co=co: e.matmul(
                            bank(5), indic[0:1, :], recb[0:1, co:co + 512],
                            start=True, stop=False),
                          reads=["recb", "indic"], writes=["ps5"])
                        A("T", lambda e, co=co: e.matmul(
                            bank(5), indic[32:33, :], recb[32:33, co:co + 512],
                            start=False, stop=True),
                          reads=["recb", "indic"], writes=["ps5"])
                        A("V", lambda e, cpair=cpair: e.tensor_tensor(
                            out=attn3[:, cpair, :], in0=attn3[:, cpair, :],
                            in1=bank(5), op=OP.mult),
                          reads=["ps5", f"attn{cpair}"], writes=[f"attn{cpair}"])
                return chain

            pending_chain = [None]
            for hp2 in range(CH):
                nkt = 4 * J + 4
                hs = (2 * hp2, 2 * hp2 + 1)
                sb_of = (0, 6)
                def emit_pv(kti):
                    qs = max(0, 128 * (kti - 4 * J))
                    for j in range(2):
                        h = hs[j]
                        pb = 2 * j + kti % 2
                        A("T", lambda e, h=h, kti=kti, qs=qs, pb=pb, j=j, nkt=nkt: e.matmul(
                            ps[0:65, 512 * (2 + j) + qs: 512 * (2 + j) + 512],
                            vt_ap[:, 65 * H * kti + 65 * h: 65 * H * kti + 65 * h + 65],
                            p2[pb][:, qs:512],
                            start=(kti == 0), stop=(kti == nkt - 1)),
                          reads=[f"p2_{pb}", "vtok", "vtones"], writes=[PSK[2 + j]])

                for kti in range(nkt):
                    qs = max(0, 128 * (kti - 4 * J))
                    for j in range(2):
                        h = hs[j]
                        sbk = sb_of[j] + kti % 2
                        skey = PSK[sbk]
                        A("T", lambda e, h=h, kti=kti, qs=qs, sbk=sbk: e.matmul(
                            ps[:, 512 * sbk + qs: 512 * sbk + 512],
                            kt_ap[0:65, T * h + 128 * kti: T * h + 128 * kti + 128],
                            qt_ap[0:65, 512 * h + qs: 512 * h + 512],
                            start=True, stop=True),
                          reads=KD + QD + ["qaug", "ktones"], writes=[skey])
                    for j in range(2):
                        h = hs[j]
                        sbk = sb_of[j] + kti % 2
                        pb = 2 * j + kti % 2
                        A("S", lambda e, kti=kti, qs=qs, sbk=sbk, pb=pb: e.activation(
                            p2[pb][:, qs:512], ps[:, 512 * sbk + qs: 512 * sbk + 512],
                            AF.Exp, scale=0.125),
                          reads=[PSK[sbk], "consts"], writes=[f"p2_{pb}"])
                    if kti >= 4 * J:
                        for j in range(2):
                            pb = 2 * j + kti % 2
                            A("V", lambda e, kti=kti, qs=qs, pb=pb: e.tensor_tensor(
                                out=p2[pb][:, qs:qs + 128],
                                in0=p2[pb][:, qs:qs + 128],
                                in1=tri01[:], op=OP.mult),
                              reads=["tri01", f"p2_{pb}"], writes=[f"p2_{pb}"])
                    if kti > 0:
                        emit_pv(kti - 1)
                    if kti == 2 and pending_chain[0] is not None:
                        pending_chain[0]()
                        pending_chain[0] = None
                emit_pv(nkt - 1)
                for j in range(2):
                    h = hs[j]
                    pvb = 2 + j
                    pvk = PSK[pvb]
                    A("V", lambda e, h=h, pvb=pvb: e.tensor_copy(
                        attn[:][64 * (h % 2): 64 * (h % 2) + 64,
                                512 * (h // 2): 512 * (h // 2) + 512],
                        ps[0:64, 512 * pvb: 512 * pvb + 512]),
                      reads=[pvk], writes=[f"attn{hp2}"])
                    A("V", lambda e, h=h, pvb=pvb: e.tensor_copy(
                        srowt[32 * (h % 2):32 * (h % 2) + 1][:, 512 * ((h // 2) % 2):512 * ((h // 2) % 2) + 512],
                        ps[64:65, 512 * pvb: 512 * pvb + 512]),
                      reads=[pvk], writes=["srow"])
                if hp2 % 2 == 1:
                    pending_chain[0] = mk_chain(hp2)
                if hp2 < len(ln_phases):
                    ln_phases[hp2]()

            def mk_wo(wo_loads=wo_loads, seq=seq, J=J, chain=pending_chain[0]):
                chain_h = [chain]

                def wo_emit():
                    for g in range(12, 16):
                        wst, slot, wkey = wo_loads[g - 12]
                        for fc2 in range(2):
                            fco = (g - 12) * 2 + fc2
                            bk = fco % 2
                            bkey = PSK[bk]
                            for dc in range(CH):
                                if dc == 6 and chain_h[0] is not None:
                                    chain_h[0]()
                                    chain_h[0] = None
                                A("T", lambda e, slot=slot, fc2=fc2, dc=dc, bk=bk: e.matmul(
                                    bank(bk), wb3[slot][:, dc, 128 * fc2:128 * fc2 + 128],
                                    attn3[:, dc, :],
                                    start=(dc == 0), stop=(dc == CH - 1)),
                                  reads=[wkey, f"attn{dc}"],
                                  writes=[bkey] if dc in (0, CH - 1) else [], deps=[wst])
                            A("V", lambda e, fco=fco, bk=bk: e.tensor_tensor(
                                out=outb3[:, fco, :], in0=bank(bk), in1=xrk3[:, fco, :],
                                op=OP.add),
                              reads=[bkey, "xrk"], writes=["outb"])
                    A("G", lambda e, seq=seq, J=J: e.dma_start(
                        out=y_h[seq, J], in_=outb[:]),
                      reads=["outb"], dma_sem="dy")
                return wo_emit

            pending_wo[0] = mk_wo()
            pending_chain[0] = None
            if is_last:
                pending_wo[0]()
                pending_wo[0] = None

    last_y = [s for s in P.steps if s.sem == "dy"][-1]
    A("Y", lambda e: e.nop(), deps=[last_y], dma_sem=None)
    P.steps[-1].sem = "fin"
    P.steps[-1].val = 1
    A("G", lambda e: e.nop(), deps=[last_y])
    P.steps[-1].sem = "fin"
    P.steps[-1].val = 2

    with es:
        with nc.Block() as block:
            P.emit(block, semh)
    return nc


def _deint_perm():
    perm = np.arange(3 * C)
    for blk in range(2):
        for h in range(H):
            base = blk * C + h * DH
            idx = np.concatenate([np.arange(0, DH, 2), np.arange(1, DH, 2)])
            perm[base: base + DH] = base + idx
    return perm


def _host_inputs(inputs):
    x = np.ascontiguousarray(np.asarray(inputs["L0_attn_x"], dtype=np.float32))
    wkqv = np.asarray(inputs["L0_attn_W_kqv"], dtype=np.float32)
    wo = np.asarray(inputs["L0_attn_W_o"], dtype=np.float32)
    wr = np.asarray(inputs["L0_ffn_W_r"], dtype=np.float32)
    wk = np.asarray(inputs["L0_ffn_W_k"], dtype=np.float32)
    wv = np.asarray(inputs["L0_ffn_W_v"], dtype=np.float32)
    mu_r = np.asarray(inputs["L0_ffn_mu_r"], dtype=np.float32)
    mu_k = np.asarray(inputs["L0_ffn_mu_k"], dtype=np.float32)

    wkqvt = np.ascontiguousarray(np.ascontiguousarray(wkqv.T)[:, _deint_perm()])
    wot = np.ascontiguousarray(wo.T)
    wrt = np.ascontiguousarray(wr.T)
    wkt = np.ascontiguousarray(wk.T)
    wvt = np.ascontiguousarray(wv.T)
    mu2 = np.ascontiguousarray(np.concatenate(
        [mu_r.reshape(CH, 128).T, mu_k.reshape(CH, 128).T], axis=1))

    pos = np.arange(T, dtype=np.float64)
    inv = THETA ** (-(np.arange(32, dtype=np.float64) / 32.0))
    ang = pos[None, :] * inv[:, None]
    cos32 = np.cos(ang)
    sin32 = np.sin(ang)
    cmap = np.concatenate([cos32, cos32, cos32, cos32], axis=0)
    smap = np.concatenate([sin32, -sin32, sin32, -sin32], axis=0)
    maps = np.ascontiguousarray(
        np.concatenate([cmap, smap], axis=1).astype(np.float32))

    in_maps = []
    for c in range(NCORES):
        xb = x[BSEQ * c: BSEQ * (c + 1)]
        xc = np.ascontiguousarray(
            xb.reshape(BSEQ, NJ, 512, CH, 128).transpose(0, 1, 4, 3, 2)
        ).reshape(BSEQ, NJ, 128, CH * 512)
        in_maps.append({
            "x": xc, "wkqvt": wkqvt, "wrt": wrt, "wkt": wkt, "wvt": wvt,
            "wot": wot, "mu2": mu2, "ropemaps": maps,
        })
    return in_maps


def kernel(**inputs):
    in_maps = _host_inputs(inputs)
    nc = build_program()
    res = run_bass_kernel_spmd(nc, in_maps, core_ids=list(range(NCORES)))
    outs = []
    for c in range(NCORES):
        y = np.asarray(res.results[c]["y"]).reshape(BSEQ, NJ, 128, CH, 512)
        outs.append(y.transpose(0, 1, 4, 3, 2).reshape(BSEQ, T, C))
    return np.ascontiguousarray(np.concatenate(outs, axis=0), dtype=np.float32)


if __name__ == "__main__":
    nc = build_program()
    print("built ok, instructions:", sum(1 for _ in nc.all_instructions()))


# version 33
# speedup vs baseline: 1.0220x; 1.0160x over previous
import numpy as np
import concourse.bass as bass
import concourse.mybir as mybir
from concourse.bass_utils import run_bass_kernel_spmd
from contextlib import ExitStack

F32 = mybir.dt.float32
BF16 = mybir.dt.bfloat16
AF = mybir.ActivationFunctionType
OP = mybir.AluOpType

NCORES = 8
BSEQ = 2
T = 2048
C = 1024
H = 16
DH = 64
CH = C // 128
NJ = T // 512
KTT = T // 128
THETA = 10000.0


class Step:
    __slots__ = ("eng", "fn", "sem", "val", "deps", "snap", "inc", "waits",
                 "inc_on", "force_prev")

    def __init__(self, eng, fn, sem, val, deps, inc):
        self.eng, self.fn, self.sem, self.val = eng, fn, sem, val
        self.deps, self.inc = deps, inc
        self.snap = None
        self.waits = ()
        self.force_prev = None


class Prog:
    ENGS = ("Y", "G", "T", "V", "S")
    NOCHAIN = ("dka", "dkb", "dqa", "dqb")

    def __init__(self):
        self.steps = []
        self.sem_cnt = {}
        self.last_w = {}
        self.last_r = {}
        self.last_dma = {}

    def add(self, eng, fn, reads=(), writes=(), deps=(), dma_sem=None):
        is_dma = dma_sem is not None
        dl = [d for d in deps if d is not None]
        for k in reads:
            w = self.last_w.get(k)
            if w is not None:
                dl.append(w)
        for k in writes:
            w = self.last_w.get(k)
            if w is not None:
                dl.append(w)
            dl.extend(self.last_r.get(k, ()))
        if dma_sem is not None:
            sem, inc = dma_sem, 16
        else:
            sem, inc = "e" + eng, 1
        s = Step(eng, fn, sem, 0, dl, inc)
        if dma_sem is not None and dma_sem not in self.NOCHAIN:
            prev = self.last_dma.get(dma_sem)
            if prev is not None:
                dl.append(prev)
                s.force_prev = prev
            self.last_dma[dma_sem] = s
        self.steps.append(s)
        for k in reads:
            self.last_r.setdefault(k, []).append(s)
        for k in writes:
            self.last_w[k] = s
            self.last_r[k] = []
        return s

    def plan(self):
        for s in self.steps:
            s.inc_on = (s.inc == 16)
        for s in self.steps:
            for d in s.deps:
                d.inc_on = True
        self.sem_cnt = {}
        for s in self.steps:
            if s.inc_on:
                v = self.sem_cnt.get(s.sem, 0) + s.inc
                self.sem_cnt[s.sem] = v
                s.val = v
        know = {e: {} for e in self.ENGS}
        for s in self.steps:
            k = know[s.eng]
            wd = {}
            for d in s.deps:
                if k.get(d.sem, 0) < d.val:
                    wd[d.sem] = max(wd.get(d.sem, 0), d.val)
                    k[d.sem] = d.val
                    if d.snap:
                        for sm, v in d.snap.items():
                            if k.get(sm, 0) < v:
                                k[sm] = v
            if s.force_prev is not None and s.force_prev.val > 0:
                fp = s.force_prev
                if wd.get(fp.sem, 0) < fp.val:
                    wd[fp.sem] = fp.val
                    if k.get(fp.sem, 0) < fp.val:
                        k[fp.sem] = fp.val
            s.waits = sorted(wd.items())
            snap = dict(k)
            snap[s.sem] = max(snap.get(s.sem, 0), s.val)
            s.snap = snap

    def emit(self, block, handles):
        self.plan()
        by_eng = {e: [s for s in self.steps if s.eng == e] for e in self.ENGS}

        def runner(ename):
            def run(eng):
                for s in by_eng[ename]:
                    for sm, v in s.waits:
                        eng.wait_ge(handles[sm], v)
                    inst = s.fn(eng)
                    if s.inc_on:
                        inst.then_inc(handles[s.sem], s.inc)
            return run

        block.sync(runner("Y"))
        block.gpsimd(runner("G"))
        block.tensor(runner("T"))
        block.vector(runner("V"))
        block.scalar(runner("S"))


def build_program():
    nc = bass.Bass("TRN2", target_bir_lowering=False)
    P = Prog()
    A = P.add

    x_h = nc.dram_tensor("x", [BSEQ, NJ, 128, CH * 512], F32, kind="ExternalInput").ap()
    wkqv_h = nc.dram_tensor("wkqvt", [C, 3 * C], F32, kind="ExternalInput").ap()
    wo_h = nc.dram_tensor("wot", [C, C], F32, kind="ExternalInput").ap()
    wr_h = nc.dram_tensor("wrt", [C, C], F32, kind="ExternalInput").ap()
    wk_h = nc.dram_tensor("wkt", [C, C], F32, kind="ExternalInput").ap()
    wv_h = nc.dram_tensor("wvt", [C, C], F32, kind="ExternalInput").ap()
    mu_h = nc.dram_tensor("mu2", [128, 2 * CH], F32, kind="ExternalInput").ap()
    maps_h = nc.dram_tensor("ropemaps", [128, 2 * T], F32, kind="ExternalInput").ap()
    y_h = nc.dram_tensor("y", [BSEQ, NJ, 128, CH * 512], F32, kind="ExternalOutput").ap()
    NGRP = 28
    wtmp_h = nc.dram_tensor("wtmp", [NGRP, 128, 2048], BF16, kind="Internal").ap()
    aug_h = nc.dram_tensor("augh", [64, 128], BF16, kind="Internal").ap()

    es = ExitStack()
    sbuf = lambda name, shape, dt: es.enter_context(nc.sbuf_tensor(name, shape, dt))

    NSLOT = 5
    wbuf = [sbuf(f"wbuf{i}", [128, 2048], BF16) for i in range(NSLOT)]
    kt = sbuf("ktil", [65, H * T], BF16)
    vtok = sbuf("vtok", [128, KTT * H * 65], BF16)
    qt = sbuf("qtil", [65, H * 512], BF16)
    xres = sbuf("xres", [128, CH * 512], BF16)
    outb = sbuf("outb", [128, CH * 512], BF16)
    xn = sbuf("xn", [128, CH * 513], BF16)
    call = sbuf("call", [128, CH * 512], BF16)
    attn = sbuf("attn", [128, CH * 512], BF16)
    p2 = [sbuf(f"p2_{i}", [128, 512], BF16) for i in range(4)]
    rsig = sbuf("rsigt", [128, CH * 512], BF16)
    xrk = sbuf("xrk", [128, CH * 512], BF16)
    recb = sbuf("recb", [33, 1024], BF16)
    srowt = sbuf("srowt", [33, 1024], F32)
    sgm = sbuf("sgm", [128, 64], F32)
    lnout = sbuf("lnout", [128, 64], F32)
    b2bf = sbuf("b2bf", [128, 64], BF16)
    mu2 = sbuf("mu2s", [128, 2 * CH], F32)
    r_s = sbuf("r_s", [1, 512], F32)
    s2_s = sbuf("s2_s", [1, 512], F32)
    maskadd = sbuf("maskadd", [128, 128], F32)
    tri01 = sbuf("tri01", [128, 128], BF16)
    i128 = sbuf("i128", [128, 128], BF16)
    indic = sbuf("indic", [33, 128], BF16)
    ones128 = sbuf("ones128", [128, 1], BF16)
    onesf1 = sbuf("onesf1", [1, 128], F32)
    consts = sbuf("consts", [128, 8], F32)
    xtail = sbuf("xtail", [128, CH], BF16)
    augs = sbuf("augs", [64, 128], BF16)

    ps_t = es.enter_context(nc.psum_tensor("ps", [128, 4096], F32))
    ps = ps_t.ap() if hasattr(ps_t, "ap") else ps_t[:]

    def bank(b):
        return ps[:, 512 * b: 512 * b + 512]


    sem_names = ["eY", "eG", "eT", "eV", "eS", "fin",
                 "dwl0", "dwl1", "dwl2", "dwl3", "dwl4", "dws0", "dws1", "dws2", "dws3", "dws4", "dst0", "dst1", "dst2", "dst3", "dst4", "dmu",
                 "dx", "dmap", "dka", "dkb", "dqa", "dqb", "dy", "dy2", "db1", "db2"]
    semh = {n: es.enter_context(nc.semaphore(n)) for n in sem_names}

    CONST_VALS = [0.0, 1.0, 1.0 / 32.0, 0.125, 1e-5, -1.0, 0.5, -0.5]
    for i, v in enumerate(CONST_VALS):
        nc.const_aps.register(F32, v, consts[:, i:i + 1])

    wb3 = [wbuf[i][:].rearrange("p (c f) -> p c f", c=CH) for i in range(NSLOT)]
    vt_ap = vtok[:]
    vt3 = vt_ap.rearrange("p (k h d) -> p k h d", k=KTT, h=H)
    kt_ap = kt[:]
    rsf32 = rsig[:].bitcast(F32)
    mapblk = rsf32[:, 0:1024]
    sln = rsf32[0:33, 1024:2048]
    qt_ap = qt[:]
    call3 = call[:].rearrange("p (c t) -> p c t", c=CH)
    xn_ap = xn[:]
    xn3 = xn_ap.rearrange("p (c t) -> p c t", c=CH)
    attn3 = attn[:].rearrange("p (c t) -> p c t", c=CH)
    pack3 = xrk[:].rearrange("p (c t) -> p c t", c=CH)
    rsig3 = rsig[:].rearrange("p (c t) -> p c t", c=CH)
    xrk3 = xrk[:].rearrange("p (c t) -> p c t", c=CH)
    xres3 = xres[:].rearrange("p (c t) -> p c t", c=CH)
    outb3 = outb[:].rearrange("p (c t) -> p c t", c=CH)

    for i, v in enumerate(CONST_VALS):
        A("V", lambda e, i=i, v=v: e.memset(consts[:, i:i + 1], v), writes=["consts"])
    A("V", lambda e: e.memset(ones128[:], 1.0), writes=["ones128"])
    A("V", lambda e: e.memset(onesf1[:], 1.0), writes=["onesf1"])
    A("V", lambda e: e.memset(kt[64:65, :], 1.0), writes=["ktones"])
    A("V", lambda e: e.memset(indic[0:1, 0:64], 1.0), writes=["indic"])
    A("V", lambda e: e.memset(indic[0:1, 64:128], 0.0), writes=["indic"])
    A("V", lambda e: e.memset(indic[32:33, 0:64], 0.0), writes=["indic"])
    A("V", lambda e: e.memset(indic[32:33, 64:128], 1.0), writes=["indic"])
    A("V", lambda e: e.memset(xtail[:], 0.0), writes=["xtail"])
    A("V", lambda e: e.memset(srowt[:], 1.0), writes=["srow"])
    A("V", lambda e: e.memset(vt3[:, :, :, 64:65], 1.0), writes=["vtones"])
    A("V", lambda e: e.memset(maskadd[:], 0.0), writes=["maskadd"])
    A("G", lambda e: e.affine_select(out=maskadd[:], in_=maskadd[:],
                                     pattern=[[-1, 128]], compare_op=OP.is_ge,
                                     fill=-1.0e4, base=0, channel_multiplier=1),
      reads=["maskadd"], writes=["maskadd"])
    A("V", lambda e: e.memset(tri01[:], 1.0), writes=["tri01"])
    A("G", lambda e: e.affine_select(out=tri01[:], in_=tri01[:],
                                     pattern=[[1, 128]], compare_op=OP.is_ge,
                                     fill=0.0, base=0, channel_multiplier=-1),
      reads=["tri01"], writes=["tri01"])
    A("V", lambda e: e.memset(i128[:], 1.0), writes=["i128"])
    A("G", lambda e: e.affine_select(out=i128[:], in_=i128[:],
                                     pattern=[[1, 128]], compare_op=OP.is_ge,
                                     fill=0.0, base=0, channel_multiplier=-1),
      reads=["i128"], writes=["i128"])
    A("G", lambda e: e.affine_select(out=i128[:], in_=i128[:],
                                     pattern=[[-1, 128]], compare_op=OP.is_ge,
                                     fill=0.0, base=0, channel_multiplier=1),
      reads=["i128"], writes=["i128"])
    A("G", lambda e: e.dma_start(out=mu2[:], in_=mu_h), writes=["mu2"], dma_sem="dmu")

    wsrcs = ([(wkqv_h, g) for g in range(12)] + [(wo_h, g) for g in range(4)] +
             [(wr_h, g) for g in range(4)] + [(wk_h, g) for g in range(4)] +
             [(wv_h, g) for g in range(4)])
    wstore_last = []
    for gi, (h, g) in enumerate(wsrcs):
        slot = gi % NSLOT
        key = f"wbuf{slot}"
        src = h.rearrange("(c p) f -> p c f", p=128)[:, :, 256 * g: 256 * g + 256]
        A("G", lambda e, src=src, slot=slot: e.dma_start(out=wb3[slot], in_=src),
          writes=[key], dma_sem=f"dwl{slot}")
        st = A("Y", lambda e, gi=gi, slot=slot: e.dma_start(out=wtmp_h[gi], in_=wbuf[slot][:]),
               reads=[key], dma_sem=f"dws{slot}")
        if gi >= len(wsrcs) - NSLOT:
            wstore_last.append(st)

    wload_i = [0]

    def load_w(gi):
        slot = wload_i[0] % NSLOT
        wload_i[0] += 1
        key = f"wbuf{slot}"
        st = A("Y", lambda e, gi=gi, slot=slot: e.dma_start(out=wbuf[slot][:], in_=wtmp_h[gi]),
               writes=[key], deps=wstore_last, dma_sem=f"dst{slot}")
        return st, slot, key

    PSK = [f"ps{i}" for i in range(8)]

    def emit_ln_phases(seq, J):
        first = (seq == 0 and J == 0)
        b4row = ps[0:1, 2048:2560]

        def ph_dma(seq=seq, J=J, first=first):
            dstk = "xres" if first else "outb"
            dst = xres[:] if first else outb[:]
            A("G", lambda e, seq=seq, J=J, dst=dst: e.dma_start(
                out=dst, in_=x_h[seq, J]),
              writes=[dstk], dma_sem="dx")

        def ph_maps(J=J):
            A("Y", lambda e, J=J: e.dma_start(
                out=mapblk.rearrange("p (g t) -> p g t", g=2),
                in_=maps_h.rearrange("p (g t) -> p g t", g=2)[:, :, 512 * J: 512 * J + 512]),
              writes=["mapblk", "rsigt"], dma_sem="dmap")

        def ph0():
            for c in range(CH):
                A("T", lambda e, c=c: e.matmul(b4row, ones128[:], xres3[:, c, :],
                                               start=(c == 0), stop=(c == CH - 1)),
                  reads=["xres", "ones128"],
                  writes=["ps4"] if c in (0, CH - 1) else [])

        def ph1():
            A("V", lambda e: e.tensor_scalar(out=s2_s[:], in0=b4row,
                                             scalar1=1.0 / C, scalar2=None, op0=OP.mult),
              reads=["ps4"], writes=["s2_s"])
            A("V", lambda e: e.tensor_tensor(out=r_s[:], in0=s2_s[:], in1=s2_s[:], op=OP.mult),
              reads=["s2_s"], writes=["r_s"])
            A("V", lambda e: e.tensor_tensor(out=call3, in0=xres3, in1=xres3, op=OP.mult),
              reads=["xres"], writes=["call"])
            for c in range(CH):
                A("T", lambda e, c=c: e.matmul(b4row, ones128[:], call3[:, c, :],
                                               start=(c == 0), stop=(c == CH - 1)),
                  reads=["call", "ones128"],
                  writes=["ps4"] if c in (0, CH - 1) else [])

        def ph2(first=first):
            A("V", lambda e: e.scalar_tensor_tensor(out=r_s[:], in0=b4row,
                                                    scalar=1.0 / C, in1=r_s[:],
                                                    op0=OP.mult, op1=OP.subtract),
              reads=["ps4", "r_s"], writes=["r_s"])
            A("T", lambda e: e.matmul(bank(4), onesf1[:], s2_s[:], start=True, stop=True),
              reads=["s2_s", "onesf1", "ps4"], writes=["ps4"])
            A("S", lambda e: e.activation(s2_s[:], r_s[:], AF.Ln, bias=1e-5, scale=1.0),
              reads=["r_s", "consts"], writes=["s2_s"])
            A("S", lambda e: e.activation(r_s[:], s2_s[:], AF.Exp, scale=-0.5),
              reads=["s2_s", "consts"], writes=["r_s"])
            if not first:
                A("V", lambda e: e.tensor_copy(xtail[:], xn_ap[:, 512::513]),
                  reads=["xn"], writes=["xtail"])

        def ph3():
            A("V", lambda e: e.tensor_tensor(
                out=call3, in0=xres3,
                in1=bank(4).unsqueeze(1).broadcast_to([128, CH, 512]), op=OP.subtract),
              reads=["xres", "ps4", "call"], writes=["call"])

        def ph4(J=J):
            A("T", lambda e: e.matmul(bank(4), onesf1[:], r_s[:], start=True, stop=True),
              reads=["r_s", "onesf1"], writes=["ps4"])
            A("V", lambda e: e.tensor_tensor(
                out=xn3[:, :, 1:513], in0=call3,
                in1=bank(4).unsqueeze(1).broadcast_to([128, CH, 512]), op=OP.mult),
              reads=["call", "ps4"], writes=["xn"])
            if J == 0:
                A("V", lambda e: e.memset(xn_ap[:, 0::513], 0.0), reads=["xn"], writes=["xn"])
            else:
                A("V", lambda e: e.tensor_copy(xn_ap[:, 0::513], xtail[:]),
                  reads=["xtail", "xn"], writes=["xn"])

        return [ph_dma, ph_maps, ph0, ph1, ph2, ph3, ph4]

    def emit_ln(seq, J):
        for ph in emit_ln_phases(seq, J):
            ph()

    emit_ln(0, 0)
    pending_wo = [None]
    for seq in range(BSEQ):
        for J in range(NJ):
            is_last = (seq == BSEQ - 1 and J == NJ - 1)
            if pending_wo[0] is not None:
                pending_wo[0]()
                pending_wo[0] = None
            ln_phases = []
            if not is_last:
                nseq, nJ = (seq, J + 1) if J + 1 < NJ else (seq + 1, 0)
                ln_phases = emit_ln_phases(nseq, nJ)
            for g in range(12):
                wst, slot, wkey = load_w(g)
                if g < 8:
                    part = "K" if g < 4 else "Q"
                    for fc2 in range(2):
                        gch = (g % 4) * 2 + fc2
                        bk = 2 + gch % 2
                        bkey = PSK[bk]
                        for ci in range(CH):
                            A("T", lambda e, slot=slot, fc2=fc2, ci=ci, bk=bk: e.matmul(
                                bank(bk), wb3[slot][:, ci, 128 * fc2:128 * fc2 + 128],
                                xn3[:, ci, 1:513],
                                start=(ci == 0), stop=(ci == CH - 1)),
                              reads=[wkey, "xn"],
                              writes=[bkey] if ci in (0, CH - 1) else [], deps=[wst])
                        if part == "K":
                            key = f"kK{gch}"
                            tgt = kt_ap
                            sA = T * (2 * gch) + 512 * J
                            sB = T * (2 * gch + 1) + 512 * J
                            extra = []
                        else:
                            key = f"kQ{gch}"
                            tgt = qt_ap
                            sA = 512 * (2 * gch)
                            sB = 512 * (2 * gch + 1)
                            extra = [f"kQa{i}" for i in range(4)]
                        A("V", lambda e, gch=gch, bk=bk: e.tensor_tensor(
                            out=call3[:, gch, :], in0=bank(bk), in1=mapblk[:, 512:1024],
                            op=OP.mult),
                          reads=[bkey, "mapblk", "rsigt"], writes=["call"])
                        A("V", lambda e, bk=bk, tgt=tgt, sA=sA: e.tensor_tensor(
                            out=tgt[0:64, sA:sA + 512], in0=ps[0:64, 512 * bk:512 * bk + 512],
                            in1=mapblk[0:64, 0:512], op=OP.mult),
                          reads=[bkey, "mapblk", "rsigt"] + extra, writes=[key])
                        A("V", lambda e, bk=bk, tgt=tgt, sB=sB: e.tensor_tensor(
                            out=tgt[0:64, sB:sB + 512], in0=ps[64:128, 512 * bk:512 * bk + 512],
                            in1=mapblk[0:64, 0:512], op=OP.mult),
                          reads=[bkey, "mapblk", "rsigt"] + extra, writes=[key])
                    if g % 4 == 3:
                        if part == "K":
                            d32 = kt_ap[0:32, :].rearrange("p (h t) -> p h t", h=H)
                            d32b = kt_ap[32:64, :].rearrange("p (h t) -> p h t", h=H)
                            dsts32 = [d32[:, 0::2, 512 * J:512 * J + 512],
                                      d32b[:, 0::2, 512 * J:512 * J + 512],
                                      d32[:, 1::2, 512 * J:512 * J + 512],
                                      d32b[:, 1::2, 512 * J:512 * J + 512]]
                            sb2, kd = "dkb", "kK"
                        else:
                            d32 = qt_ap[0:32, :].rearrange("p (h t) -> p h t", h=H)
                            d32b = qt_ap[32:64, :].rearrange("p (h t) -> p h t", h=H)
                            dsts32 = [d32[:, 0::2, :], d32b[:, 0::2, :],
                                      d32[:, 1::2, :], d32b[:, 1::2, :]]
                            sb2, kd = "dqb", "kQ"
                        srcs = [call3[32:64, :, :], call3[0:32, :, :],
                                call3[96:128, :, :], call3[64:96, :, :]]
                        allk = [f"{kd}{c}" for c in range(CH)]
                        for i in range(4):
                            A("G", lambda e, i=i, dsts32=dsts32, srcs=srcs: e.dma_start(
                                out=dsts32[i], in_=srcs[i], accum_op=OP.add),
                              reads=["call"] + allk, writes=[f"{kd}a{i}"], dma_sem=sb2)
                else:
                    for tt in range(4):
                        bk = 2 + tt % 2
                        bkey = PSK[bk]
                        for ci in range(CH):
                            A("T", lambda e, slot=slot, tt=tt, ci=ci, bk=bk: e.matmul(
                                ps[:, 512 * bk: 512 * bk + 256],
                                xn3[:, ci, 1 + 128 * tt:1 + 128 * tt + 128],
                                wb3[slot][:, ci, :],
                                start=(ci == 0), stop=(ci == CH - 1)),
                              reads=[wkey, "xn"],
                              writes=[bkey] if ci in (0, CH - 1) else [], deps=[wst])
                        ktile = 4 * J + tt
                        h0 = (g - 8) * 4
                        A("V", lambda e, ktile=ktile, h0=h0, bk=bk: e.tensor_copy(
                            vt3[:, ktile:ktile + 1, h0:h0 + 4, 0:64],
                            ps[:, 512 * bk: 512 * bk + 256].rearrange(
                                "p (o h d) -> p o h d", o=1, h=4)),
                          reads=[bkey], writes=["vtok"])

            KD = [f"kK{c}" for c in range(CH)] + [f"kKa{i}" for i in range(4)]
            QD = [f"kQ{c}" for c in range(CH)] + [f"kQa{i}" for i in range(4)]

            ffn_work = []
            ffn_loadidx = []
            W = ffn_work.append
            W(lambda: A("V", lambda e: e.tensor_tensor(
                out=call3, in0=xn3[:, :, 1:513], in1=xn3[:, :, 0:512], op=OP.subtract),
                reads=["xn"], writes=["call"]))
            for c in range(CH):
                W(lambda c=c: A("V", lambda e, c=c: e.scalar_tensor_tensor(
                    out=xrk3[:, c, :], in0=call3[:, c, :], scalar=mu2[:, c:c + 1],
                    in1=xn3[:, c, 0:512], op0=OP.mult, op1=OP.add),
                    reads=["call", "xn", "mu2"], writes=["xrk"]))

            def ffn_proj(g0, rhs3, rhs_key, epi):
                for g in range(g0, g0 + 4):
                    holder = {}
                    ffn_loadidx.append(len(ffn_work))
                    W(lambda holder=holder, g=g: holder.update(zip(("st", "slot", "key"), load_w(g))))
                    for fc2 in range(2):
                        fco = (g - g0) * 2 + fc2
                        bk = 6 + fco % 2
                        bkey = PSK[bk]
                        for ci in range(CH):
                            W(lambda holder=holder, fc2=fc2, ci=ci, bk=bk, bkey=bkey,
                              rhs3=rhs3, rhs_key=rhs_key: A(
                                "T", lambda e, holder=holder, fc2=fc2, ci=ci, bk=bk, rhs3=rhs3: e.matmul(
                                    bank(bk), wb3[holder["slot"]][:, ci, 128 * fc2:128 * fc2 + 128],
                                    rhs3[:, ci, :],
                                    start=(ci == 0), stop=(ci == CH - 1)),
                                reads=[holder["key"], rhs_key],
                                writes=[bkey] if ci in (0, CH - 1) else [],
                                deps=[holder["st"]]))
                        W(lambda fco=fco, bk=bk, bkey=bkey, epi=epi: epi(fco, bk, bkey))

            def sig_epi(fco, bk, bkey):
                A("S", lambda e, bk=bk: e.activation(
                    bank(bk), bank(bk), AF.Tanh, scale=0.5),
                  reads=[bkey, "consts"], writes=[bkey])
                return A("V", lambda e, fco=fco, bk=bk: e.tensor_scalar(
                    out=rsig3[:, fco, :], in0=bank(bk),
                    scalar1=0.5, scalar2=0.5, op0=OP.mult, op1=OP.add),
                  reads=[bkey], writes=["rsigt"])

            def relu_epi(fco, bk, bkey):
                A("S", lambda e, fco=fco, bk=bk: e.activation(
                    call3[:, fco, :], bank(bk), AF.Relu),
                  reads=[bkey, "consts"], writes=["call"])
                return A("V", lambda e, fco=fco: e.tensor_tensor(
                    out=call3[:, fco, :], in0=call3[:, fco, :],
                    in1=call3[:, fco, :], op=OP.mult),
                  reads=["call"], writes=["call"])

            def wv_epi(fco, bk, bkey):
                return A("V", lambda e, fco=fco, bk=bk: e.tensor_tensor(
                    out=xrk3[:, fco, :], in0=bank(bk), in1=rsig3[:, fco, :],
                    op=OP.mult),
                  reads=[bkey, "rsigt"], writes=["xrk"])

            ffn_proj(16, xrk3, "xrk", sig_epi)
            for c in range(CH):
                W(lambda c=c: A("V", lambda e, c=c: e.scalar_tensor_tensor(
                    out=xrk3[:, c, :], in0=call3[:, c, :],
                    scalar=mu2[:, CH + c:CH + c + 1],
                    in1=xn3[:, c, 0:512], op0=OP.mult, op1=OP.add),
                    reads=["call", "xn", "mu2"], writes=["xrk"]))
            ffn_proj(20, xrk3, "xrk", relu_epi)
            ffn_proj(24, call3, "call", wv_epi)
            W(lambda: A("V", lambda e: e.tensor_tensor(
                out=xrk3, in0=xrk3, in1=xres3, op=OP.add),
                reads=["xres", "xrk"], writes=["xrk"]))
            LOOKAHEAD = 30
            marked = set(ffn_loadidx)
            loads = [ffn_work[i] for i in sorted(marked)]
            rest = [f for i, f in enumerate(ffn_work) if i not in marked]
            targets = []
            cnt = 0
            for i in range(len(ffn_work)):
                if i in marked:
                    targets.append(max(0, cnt - LOOKAHEAD))
                else:
                    cnt += 1
            woven = []
            ti = 0
            for pos, f in enumerate(rest):
                while ti < len(targets) and targets[ti] <= pos:
                    woven.append(loads[ti])
                    ti += 1
                woven.append(f)
            woven.extend(loads[ti:])
            ffn_work = woven

            n_ffn = len(ffn_work)
            ffn_i = [0]

            def drain_ffn(k):
                while ffn_i[0] < n_ffn and k > 0:
                    ffn_work[ffn_i[0]]()
                    ffn_i[0] += 1
                    k -= 1

            for h in range(H):
                for qtl in range(4):
                    s = 4 * h + qtl
                    kext = 512 * J + 128 * (qtl + 1)
                    nchunks = (kext + 511) // 512
                    if nchunks < 4:
                        if nchunks == 1:
                            base = s % 6
                        elif nchunks == 2:
                            base = 2 * (s % 3)
                        else:
                            base = 3 * (s % 2)
                        bks = PSK[base:base + nchunks]
                        for chn in range(nchunks):
                            N = min(512, kext - 512 * chn)
                            A("T", lambda e, h=h, qtl=qtl, chn=chn, N=N, base=base: e.matmul(
                                ps[:, 512 * (base + chn): 512 * (base + chn) + N],
                                qt_ap[0:64, 512 * h + 128 * qtl: 512 * h + 128 * qtl + 128],
                                kt_ap[0:64, T * h + 512 * chn: T * h + 512 * chn + N],
                                start=True, stop=True),
                              reads=QD + KD, writes=[bks[chn]])
                        A("V", lambda e, kext=kext, base=base: e.tensor_tensor(
                            out=ps[:, 512 * base + kext - 128: 512 * base + kext],
                            in0=ps[:, 512 * base + kext - 128: 512 * base + kext],
                            in1=maskadd[:], op=OP.add),
                          reads=["maskadd"] + bks[:nchunks], writes=[bks[nchunks - 1]])
                        A("S", lambda e, kext=kext, h=h, qtl=qtl, base=base: e.activation(
                            attn[:][:, 0:kext], ps[:, 512 * base: 512 * base + kext],
                            AF.Exp, scale=1.0 / 32.0,
                            accum_out=sgm[:, 4 * h + qtl: 4 * h + qtl + 1]),
                          reads=bks[:nchunks] + ["consts"],
                          writes=[f"attn{i}" for i in range(nchunks)] + ["sgm"])
                    else:
                        pL = 2 * (s % 3)
                        pH = 2 * ((s + 1) % 3)
                        bksH = PSK[pH:pH + 2]
                        bksL = PSK[pL:pL + 2]
                        for chn in (2, 3):
                            N = min(512, kext - 512 * chn)
                            A("T", lambda e, h=h, qtl=qtl, chn=chn, N=N, pH=pH: e.matmul(
                                ps[:, 512 * (pH + chn - 2): 512 * (pH + chn - 2) + N],
                                qt_ap[0:64, 512 * h + 128 * qtl: 512 * h + 128 * qtl + 128],
                                kt_ap[0:64, T * h + 512 * chn: T * h + 512 * chn + N],
                                start=True, stop=True),
                              reads=QD + KD, writes=[PSK[pH + chn - 2]])
                        A("V", lambda e, kext=kext, pH=pH: e.tensor_tensor(
                            out=ps[:, 512 * pH + kext - 1152: 512 * pH + kext - 1024],
                            in0=ps[:, 512 * pH + kext - 1152: 512 * pH + kext - 1024],
                            in1=maskadd[:], op=OP.add),
                          reads=["maskadd"] + bksH, writes=[PSK[pH + 1]])
                        A("S", lambda e, kext=kext, h=h, qtl=qtl, pH=pH: e.activation(
                            attn[:][:, 0:kext - 1024],
                            ps[:, 512 * pH: 512 * pH + kext - 1024],
                            AF.Exp, scale=1.0 / 32.0,
                            accum_out=lnout[:, 4 * h + qtl: 4 * h + qtl + 1]),
                          reads=bksH + ["consts"],
                          writes=["attn0", "attn1", "lnout"])
                        for chn in (0, 1):
                            A("T", lambda e, h=h, qtl=qtl, chn=chn, pL=pL: e.matmul(
                                ps[:, 512 * (pL + chn): 512 * (pL + chn) + 512],
                                qt_ap[0:64, 512 * h + 128 * qtl: 512 * h + 128 * qtl + 128],
                                kt_ap[0:64, T * h + 512 * chn: T * h + 512 * chn + 512],
                                start=True, stop=True),
                              reads=QD + KD, writes=[PSK[pL + chn]])
                        A("S", lambda e, h=h, qtl=qtl, pL=pL: e.activation(
                            attn[:][:, 0:1024], ps[:, 512 * pL: 512 * pL + 1024],
                            AF.Exp, scale=1.0 / 32.0,
                            accum_out=sgm[:, 4 * h + qtl: 4 * h + qtl + 1]),
                          reads=bksL + ["consts"],
                          writes=["attn0", "attn1", "sgm"])
                    drain_ffn(5)
            drain_ffn(n_ffn)
            if ln_phases:
                ln_phases[0]()
                ln_phases[1]()
                ln_phases = ln_phases[2:]
                A("V", lambda e: e.tensor_copy(xres[:], outb[:]),
                  reads=["outb"], writes=["xres"])
            if J == NJ - 1:
                A("V", lambda e: e.tensor_tensor(
                    out=sgm[:], in0=sgm[:], in1=lnout[:], op=OP.add),
                  reads=["sgm", "lnout"], writes=["sgm"])
            A("S", lambda e: e.activation(lnout[:], sgm[:], AF.Ln),
              reads=["sgm", "consts"], writes=["lnout"])
            A("V", lambda e: e.tensor_scalar(out=b2bf[:], in0=lnout[:], scalar1=-32.0,
                                             scalar2=120.0, op0=OP.mult, op1=OP.add),
              reads=["lnout"], writes=["b2bf"])
            A("T", lambda e: e.matmul(ps[0:64, 0:128], b2bf[:], i128[:],
                                      start=True, stop=True),
              reads=["b2bf", "i128"], writes=["ps0"])
            A("V", lambda e: e.tensor_copy(augs[:], ps[0:64, 0:128]),
              reads=["ps0"], writes=["augs"])
            A("G", lambda e: e.dma_start(out=aug_h, in_=augs[:]),
              reads=["augs"], writes=["augh"], dma_sem="db1")
            A("Y", lambda e: e.dma_start(
                out=qt_ap[64:65, :],
                in_=aug_h.rearrange("c q -> (c q)").unsqueeze(0)),
              reads=["augh"], writes=["qaug"], dma_sem="db2")

            wo_loads = [load_w(g) for g in range(12, 16)]

            def mk_chain(hp2):
                def chain(hp2=hp2):
                    A("S", lambda e: e.activation(sln, srowt[:], AF.Ln),
                      reads=["srow", "consts"], writes=["rsigt"])
                    A("S", lambda e: e.activation(recb[:], sln, AF.Exp, scale=-1.0),
                      reads=["rsigt", "consts"], writes=["recb"])
                    for j in range(2):
                        cpair = hp2 - 1 + j
                        co = 512 * j
                        A("T", lambda e, co=co: e.matmul(
                            bank(5), indic[0:1, :], recb[0:1, co:co + 512],
                            start=True, stop=False),
                          reads=["recb", "indic"], writes=["ps5"])
                        A("T", lambda e, co=co: e.matmul(
                            bank(5), indic[32:33, :], recb[32:33, co:co + 512],
                            start=False, stop=True),
                          reads=["recb", "indic"], writes=["ps5"])
                        A("V", lambda e, cpair=cpair: e.tensor_tensor(
                            out=attn3[:, cpair, :], in0=attn3[:, cpair, :],
                            in1=bank(5), op=OP.mult),
                          reads=["ps5", f"attn{cpair}"], writes=[f"attn{cpair}"])
                return chain

            pending_chain = [None]
            for hp2 in range(CH):
                nkt = 4 * J + 4
                hs = (2 * hp2, 2 * hp2 + 1)
                sb_of = (0, 6)
                def emit_pv(kti):
                    qs = max(0, 128 * (kti - 4 * J))
                    for j in range(2):
                        h = hs[j]
                        pb = 2 * j + kti % 2
                        A("T", lambda e, h=h, kti=kti, qs=qs, pb=pb, j=j, nkt=nkt: e.matmul(
                            ps[0:65, 512 * (2 + j) + qs: 512 * (2 + j) + 512],
                            vt_ap[:, 65 * H * kti + 65 * h: 65 * H * kti + 65 * h + 65],
                            p2[pb][:, qs:512],
                            start=(kti == 0), stop=(kti == nkt - 1)),
                          reads=[f"p2_{pb}", "vtok", "vtones"], writes=[PSK[2 + j]])

                for kti in range(nkt):
                    qs = max(0, 128 * (kti - 4 * J))
                    for j in range(2):
                        h = hs[j]
                        sbk = sb_of[j] + kti % 2
                        skey = PSK[sbk]
                        A("T", lambda e, h=h, kti=kti, qs=qs, sbk=sbk: e.matmul(
                            ps[:, 512 * sbk + qs: 512 * sbk + 512],
                            kt_ap[0:65, T * h + 128 * kti: T * h + 128 * kti + 128],
                            qt_ap[0:65, 512 * h + qs: 512 * h + 512],
                            start=True, stop=True),
                          reads=KD + QD + ["qaug", "ktones"], writes=[skey])
                    for j in range(2):
                        h = hs[j]
                        sbk = sb_of[j] + kti % 2
                        pb = 2 * j + kti % 2
                        A("S", lambda e, kti=kti, qs=qs, sbk=sbk, pb=pb: e.activation(
                            p2[pb][:, qs:512], ps[:, 512 * sbk + qs: 512 * sbk + 512],
                            AF.Exp, scale=0.125),
                          reads=[PSK[sbk], "consts"], writes=[f"p2_{pb}"])
                    if kti >= 4 * J:
                        for j in range(2):
                            pb = 2 * j + kti % 2
                            A("V", lambda e, kti=kti, qs=qs, pb=pb: e.tensor_tensor(
                                out=p2[pb][:, qs:qs + 128],
                                in0=p2[pb][:, qs:qs + 128],
                                in1=tri01[:], op=OP.mult),
                              reads=["tri01", f"p2_{pb}"], writes=[f"p2_{pb}"])
                    if kti > 0:
                        emit_pv(kti - 1)
                    if kti == 3 and pending_chain[0] is not None:
                        pending_chain[0]()
                        pending_chain[0] = None
                emit_pv(nkt - 1)
                for j in range(2):
                    h = hs[j]
                    pvb = 2 + j
                    pvk = PSK[pvb]
                    A("V", lambda e, h=h, pvb=pvb: e.tensor_copy(
                        attn[:][64 * (h % 2): 64 * (h % 2) + 64,
                                512 * (h // 2): 512 * (h // 2) + 512],
                        ps[0:64, 512 * pvb: 512 * pvb + 512]),
                      reads=[pvk], writes=[f"attn{hp2}"])
                    A("V", lambda e, h=h, pvb=pvb: e.tensor_copy(
                        srowt[32 * (h % 2):32 * (h % 2) + 1][:, 512 * ((h // 2) % 2):512 * ((h // 2) % 2) + 512],
                        ps[64:65, 512 * pvb: 512 * pvb + 512]),
                      reads=[pvk], writes=["srow"])
                if hp2 % 2 == 1:
                    pending_chain[0] = mk_chain(hp2)
                if 1 <= hp2 <= len(ln_phases):
                    ln_phases[hp2 - 1]()

            def mk_wo(wo_loads=wo_loads, seq=seq, J=J, chain=pending_chain[0]):
                chain_h = [chain]

                def wo_emit():
                    for g in range(12, 16):
                        wst, slot, wkey = wo_loads[g - 12]
                        for fc2 in range(2):
                            fco = (g - 12) * 2 + fc2
                            bk = fco % 2
                            bkey = PSK[bk]
                            for dc in range(CH):
                                if dc == 6 and chain_h[0] is not None:
                                    chain_h[0]()
                                    chain_h[0] = None
                                A("T", lambda e, slot=slot, fc2=fc2, dc=dc, bk=bk: e.matmul(
                                    bank(bk), wb3[slot][:, dc, 128 * fc2:128 * fc2 + 128],
                                    attn3[:, dc, :],
                                    start=(dc == 0), stop=(dc == CH - 1)),
                                  reads=[wkey, f"attn{dc}"],
                                  writes=[bkey] if dc in (0, CH - 1) else [], deps=[wst])
                            A("V", lambda e, fco=fco, bk=bk: e.tensor_tensor(
                                out=outb3[:, fco, :], in0=bank(bk), in1=xrk3[:, fco, :],
                                op=OP.add),
                              reads=[bkey, "xrk"], writes=["outb"])
                    A("G", lambda e, seq=seq, J=J: e.dma_start(
                        out=y_h[seq, J], in_=outb[:]),
                      reads=["outb"], dma_sem="dy")
                return wo_emit

            pending_wo[0] = mk_wo()
            pending_chain[0] = None
            if is_last:
                pending_wo[0]()
                pending_wo[0] = None

    last_y = [s for s in P.steps if s.sem == "dy"][-1]
    A("Y", lambda e: e.nop(), deps=[last_y], dma_sem=None)
    P.steps[-1].sem = "fin"
    P.steps[-1].val = 1
    A("G", lambda e: e.nop(), deps=[last_y])
    P.steps[-1].sem = "fin"
    P.steps[-1].val = 2

    with es:
        with nc.Block() as block:
            P.emit(block, semh)
    return nc


def _deint_perm():
    perm = np.arange(3 * C)
    for blk in range(2):
        for h in range(H):
            base = blk * C + h * DH
            idx = np.concatenate([np.arange(0, DH, 2), np.arange(1, DH, 2)])
            perm[base: base + DH] = base + idx
    return perm


def _host_inputs(inputs):
    x = np.ascontiguousarray(np.asarray(inputs["L0_attn_x"], dtype=np.float32))
    wkqv = np.asarray(inputs["L0_attn_W_kqv"], dtype=np.float32)
    wo = np.asarray(inputs["L0_attn_W_o"], dtype=np.float32)
    wr = np.asarray(inputs["L0_ffn_W_r"], dtype=np.float32)
    wk = np.asarray(inputs["L0_ffn_W_k"], dtype=np.float32)
    wv = np.asarray(inputs["L0_ffn_W_v"], dtype=np.float32)
    mu_r = np.asarray(inputs["L0_ffn_mu_r"], dtype=np.float32)
    mu_k = np.asarray(inputs["L0_ffn_mu_k"], dtype=np.float32)

    wkqvt = np.ascontiguousarray(np.ascontiguousarray(wkqv.T)[:, _deint_perm()])
    wot = np.ascontiguousarray(wo.T)
    wrt = np.ascontiguousarray(wr.T)
    wkt = np.ascontiguousarray(wk.T)
    wvt = np.ascontiguousarray(wv.T)
    mu2 = np.ascontiguousarray(np.concatenate(
        [mu_r.reshape(CH, 128).T, mu_k.reshape(CH, 128).T], axis=1))

    pos = np.arange(T, dtype=np.float64)
    inv = THETA ** (-(np.arange(32, dtype=np.float64) / 32.0))
    ang = pos[None, :] * inv[:, None]
    cos32 = np.cos(ang)
    sin32 = np.sin(ang)
    cmap = np.concatenate([cos32, cos32, cos32, cos32], axis=0)
    smap = np.concatenate([sin32, -sin32, sin32, -sin32], axis=0)
    maps = np.ascontiguousarray(
        np.concatenate([cmap, smap], axis=1).astype(np.float32))

    in_maps = []
    for c in range(NCORES):
        xb = x[BSEQ * c: BSEQ * (c + 1)]
        xc = np.ascontiguousarray(
            xb.reshape(BSEQ, NJ, 512, CH, 128).transpose(0, 1, 4, 3, 2)
        ).reshape(BSEQ, NJ, 128, CH * 512)
        in_maps.append({
            "x": xc, "wkqvt": wkqvt, "wrt": wrt, "wkt": wkt, "wvt": wvt,
            "wot": wot, "mu2": mu2, "ropemaps": maps,
        })
    return in_maps


def kernel(**inputs):
    in_maps = _host_inputs(inputs)
    nc = build_program()
    res = run_bass_kernel_spmd(nc, in_maps, core_ids=list(range(NCORES)))
    outs = []
    for c in range(NCORES):
        y = np.asarray(res.results[c]["y"]).reshape(BSEQ, NJ, 128, CH, 512)
        outs.append(y.transpose(0, 1, 4, 3, 2).reshape(BSEQ, T, C))
    return np.ascontiguousarray(np.concatenate(outs, axis=0), dtype=np.float32)


if __name__ == "__main__":
    nc = build_program()
    print("built ok, instructions:", sum(1 for _ in nc.all_instructions()))


# version 34
# speedup vs baseline: 1.0237x; 1.0017x over previous
import numpy as np
import concourse.bass as bass
import concourse.mybir as mybir
from concourse.bass_utils import run_bass_kernel_spmd
from contextlib import ExitStack

F32 = mybir.dt.float32
BF16 = mybir.dt.bfloat16
AF = mybir.ActivationFunctionType
OP = mybir.AluOpType

NCORES = 8
BSEQ = 2
T = 2048
C = 1024
H = 16
DH = 64
CH = C // 128
NJ = T // 512
KTT = T // 128
THETA = 10000.0


class Step:
    __slots__ = ("eng", "fn", "sem", "val", "deps", "snap", "inc", "waits",
                 "inc_on", "force_prev")

    def __init__(self, eng, fn, sem, val, deps, inc):
        self.eng, self.fn, self.sem, self.val = eng, fn, sem, val
        self.deps, self.inc = deps, inc
        self.snap = None
        self.waits = ()
        self.force_prev = None


class Prog:
    ENGS = ("Y", "G", "T", "V", "S")
    NOCHAIN = ("dka", "dkb", "dqa", "dqb")

    def __init__(self):
        self.steps = []
        self.sem_cnt = {}
        self.last_w = {}
        self.last_r = {}
        self.last_dma = {}

    def add(self, eng, fn, reads=(), writes=(), deps=(), dma_sem=None):
        is_dma = dma_sem is not None
        dl = [d for d in deps if d is not None]
        for k in reads:
            w = self.last_w.get(k)
            if w is not None:
                dl.append(w)
        for k in writes:
            w = self.last_w.get(k)
            if w is not None:
                dl.append(w)
            dl.extend(self.last_r.get(k, ()))
        if dma_sem is not None:
            sem, inc = dma_sem, 16
        else:
            sem, inc = "e" + eng, 1
        s = Step(eng, fn, sem, 0, dl, inc)
        if dma_sem is not None and dma_sem not in self.NOCHAIN:
            prev = self.last_dma.get(dma_sem)
            if prev is not None:
                dl.append(prev)
                s.force_prev = prev
            self.last_dma[dma_sem] = s
        self.steps.append(s)
        for k in reads:
            self.last_r.setdefault(k, []).append(s)
        for k in writes:
            self.last_w[k] = s
            self.last_r[k] = []
        return s

    def plan(self):
        for s in self.steps:
            s.inc_on = (s.inc == 16)
        for s in self.steps:
            for d in s.deps:
                d.inc_on = True
        self.sem_cnt = {}
        for s in self.steps:
            if s.inc_on:
                v = self.sem_cnt.get(s.sem, 0) + s.inc
                self.sem_cnt[s.sem] = v
                s.val = v
        know = {e: {} for e in self.ENGS}
        for s in self.steps:
            k = know[s.eng]
            wd = {}
            for d in s.deps:
                if k.get(d.sem, 0) < d.val:
                    wd[d.sem] = max(wd.get(d.sem, 0), d.val)
                    k[d.sem] = d.val
                    if d.snap:
                        for sm, v in d.snap.items():
                            if k.get(sm, 0) < v:
                                k[sm] = v
            if s.force_prev is not None and s.force_prev.val > 0:
                fp = s.force_prev
                if wd.get(fp.sem, 0) < fp.val:
                    wd[fp.sem] = fp.val
                    if k.get(fp.sem, 0) < fp.val:
                        k[fp.sem] = fp.val
            s.waits = sorted(wd.items())
            snap = dict(k)
            snap[s.sem] = max(snap.get(s.sem, 0), s.val)
            s.snap = snap

    def emit(self, block, handles):
        self.plan()
        by_eng = {e: [s for s in self.steps if s.eng == e] for e in self.ENGS}

        def runner(ename):
            def run(eng):
                for s in by_eng[ename]:
                    for sm, v in s.waits:
                        eng.wait_ge(handles[sm], v)
                    inst = s.fn(eng)
                    if s.inc_on:
                        inst.then_inc(handles[s.sem], s.inc)
            return run

        block.sync(runner("Y"))
        block.gpsimd(runner("G"))
        block.tensor(runner("T"))
        block.vector(runner("V"))
        block.scalar(runner("S"))


def build_program():
    nc = bass.Bass("TRN2", target_bir_lowering=False)
    P = Prog()
    A = P.add

    x_h = nc.dram_tensor("x", [BSEQ, NJ, 128, CH * 512], F32, kind="ExternalInput").ap()
    wkqv_h = nc.dram_tensor("wkqvt", [C, 3 * C], F32, kind="ExternalInput").ap()
    wo_h = nc.dram_tensor("wot", [C, C], F32, kind="ExternalInput").ap()
    wr_h = nc.dram_tensor("wrt", [C, C], F32, kind="ExternalInput").ap()
    wk_h = nc.dram_tensor("wkt", [C, C], F32, kind="ExternalInput").ap()
    wv_h = nc.dram_tensor("wvt", [C, C], F32, kind="ExternalInput").ap()
    mu_h = nc.dram_tensor("mu2", [128, 2 * CH], F32, kind="ExternalInput").ap()
    maps_h = nc.dram_tensor("ropemaps", [128, 2 * T], F32, kind="ExternalInput").ap()
    y_h = nc.dram_tensor("y", [BSEQ, NJ, 128, CH * 512], F32, kind="ExternalOutput").ap()
    NGRP = 28
    wtmp_h = nc.dram_tensor("wtmp", [NGRP, 128, 2048], BF16, kind="Internal").ap()
    aug_h = nc.dram_tensor("augh", [64, 128], BF16, kind="Internal").ap()

    es = ExitStack()
    sbuf = lambda name, shape, dt: es.enter_context(nc.sbuf_tensor(name, shape, dt))

    NSLOT = 6
    wbuf = [sbuf(f"wbuf{i}", [128, 2048], BF16) for i in range(NSLOT)]
    kt = sbuf("ktil", [65, H * T], BF16)
    vtok = sbuf("vtok", [128, KTT * H * 65], BF16)
    qt = sbuf("qtil", [65, H * 512], BF16)
    xres = sbuf("xres", [128, CH * 512], BF16)
    outb = sbuf("outb", [128, CH * 512], BF16)
    xn = sbuf("xn", [128, CH * 513], BF16)
    call = sbuf("call", [128, CH * 512], BF16)
    attn = sbuf("attn", [128, CH * 512], BF16)
    p2 = [sbuf(f"p2_{i}", [128, 512], BF16) for i in range(4)]
    rsig = sbuf("rsigt", [128, CH * 512], BF16)
    xrk = sbuf("xrk", [128, CH * 512], BF16)
    recb = sbuf("recb", [33, 1024], BF16)
    srowt = sbuf("srowt", [33, 1024], BF16)
    sgm = sbuf("sgm", [128, 64], F32)
    lnout = sbuf("lnout", [128, 64], F32)
    b2bf = sbuf("b2bf", [128, 64], BF16)
    mu2 = sbuf("mu2s", [128, 2 * CH], F32)
    r_s = sbuf("r_s", [1, 512], F32)
    s2_s = sbuf("s2_s", [1, 512], F32)
    maskadd = sbuf("maskadd", [128, 128], F32)
    tri01 = sbuf("tri01", [128, 128], BF16)
    i128 = sbuf("i128", [128, 128], BF16)
    indic = sbuf("indic", [33, 128], BF16)
    ones128 = sbuf("ones128", [128, 1], BF16)
    onesf1 = sbuf("onesf1", [1, 128], F32)
    consts = sbuf("consts", [128, 8], F32)
    xtail = sbuf("xtail", [128, CH], BF16)
    augs = sbuf("augs", [64, 128], BF16)

    ps_t = es.enter_context(nc.psum_tensor("ps", [128, 4096], F32))
    ps = ps_t.ap() if hasattr(ps_t, "ap") else ps_t[:]

    def bank(b):
        return ps[:, 512 * b: 512 * b + 512]


    sem_names = ["eY", "eG", "eT", "eV", "eS", "fin",
                 "dwl0", "dwl1", "dwl2", "dwl3", "dwl4", "dwl5", "dws0", "dws1", "dws2", "dws3", "dws4", "dws5", "dst0", "dst1", "dst2", "dst3", "dst4", "dst5", "dmu",
                 "dx", "dmap", "dka", "dkb", "dqa", "dqb", "dy", "dy2", "db1", "db2"]
    semh = {n: es.enter_context(nc.semaphore(n)) for n in sem_names}

    CONST_VALS = [0.0, 1.0, 1.0 / 32.0, 0.125, 1e-5, -1.0, 0.5, -0.5]
    for i, v in enumerate(CONST_VALS):
        nc.const_aps.register(F32, v, consts[:, i:i + 1])

    wb3 = [wbuf[i][:].rearrange("p (c f) -> p c f", c=CH) for i in range(NSLOT)]
    vt_ap = vtok[:]
    vt3 = vt_ap.rearrange("p (k h d) -> p k h d", k=KTT, h=H)
    kt_ap = kt[:]
    rsf32 = rsig[:].bitcast(F32)
    mapblk = rsf32[:, 0:1024]
    sln = rsf32[0:33, 1024:2048]
    qt_ap = qt[:]
    call3 = call[:].rearrange("p (c t) -> p c t", c=CH)
    xn_ap = xn[:]
    xn3 = xn_ap.rearrange("p (c t) -> p c t", c=CH)
    attn3 = attn[:].rearrange("p (c t) -> p c t", c=CH)
    pack3 = xrk[:].rearrange("p (c t) -> p c t", c=CH)
    rsig3 = rsig[:].rearrange("p (c t) -> p c t", c=CH)
    xrk3 = xrk[:].rearrange("p (c t) -> p c t", c=CH)
    xres3 = xres[:].rearrange("p (c t) -> p c t", c=CH)
    outb3 = outb[:].rearrange("p (c t) -> p c t", c=CH)

    for i, v in enumerate(CONST_VALS):
        A("V", lambda e, i=i, v=v: e.memset(consts[:, i:i + 1], v), writes=["consts"])
    A("V", lambda e: e.memset(ones128[:], 1.0), writes=["ones128"])
    A("V", lambda e: e.memset(onesf1[:], 1.0), writes=["onesf1"])
    A("V", lambda e: e.memset(kt[64:65, :], 1.0), writes=["ktones"])
    A("V", lambda e: e.memset(indic[0:1, 0:64], 1.0), writes=["indic"])
    A("V", lambda e: e.memset(indic[0:1, 64:128], 0.0), writes=["indic"])
    A("V", lambda e: e.memset(indic[32:33, 0:64], 0.0), writes=["indic"])
    A("V", lambda e: e.memset(indic[32:33, 64:128], 1.0), writes=["indic"])
    A("V", lambda e: e.memset(xtail[:], 0.0), writes=["xtail"])
    A("V", lambda e: e.memset(srowt[:], 1.0), writes=["srow"])
    A("V", lambda e: e.memset(vt3[:, :, :, 64:65], 1.0), writes=["vtones"])
    A("V", lambda e: e.memset(maskadd[:], 0.0), writes=["maskadd"])
    A("G", lambda e: e.affine_select(out=maskadd[:], in_=maskadd[:],
                                     pattern=[[-1, 128]], compare_op=OP.is_ge,
                                     fill=-1.0e4, base=0, channel_multiplier=1),
      reads=["maskadd"], writes=["maskadd"])
    A("V", lambda e: e.memset(tri01[:], 1.0), writes=["tri01"])
    A("G", lambda e: e.affine_select(out=tri01[:], in_=tri01[:],
                                     pattern=[[1, 128]], compare_op=OP.is_ge,
                                     fill=0.0, base=0, channel_multiplier=-1),
      reads=["tri01"], writes=["tri01"])
    A("V", lambda e: e.memset(i128[:], 1.0), writes=["i128"])
    A("G", lambda e: e.affine_select(out=i128[:], in_=i128[:],
                                     pattern=[[1, 128]], compare_op=OP.is_ge,
                                     fill=0.0, base=0, channel_multiplier=-1),
      reads=["i128"], writes=["i128"])
    A("G", lambda e: e.affine_select(out=i128[:], in_=i128[:],
                                     pattern=[[-1, 128]], compare_op=OP.is_ge,
                                     fill=0.0, base=0, channel_multiplier=1),
      reads=["i128"], writes=["i128"])
    A("G", lambda e: e.dma_start(out=mu2[:], in_=mu_h), writes=["mu2"], dma_sem="dmu")

    wsrcs = ([(wkqv_h, g) for g in range(12)] + [(wo_h, g) for g in range(4)] +
             [(wr_h, g) for g in range(4)] + [(wk_h, g) for g in range(4)] +
             [(wv_h, g) for g in range(4)])
    wstore_last = []
    for gi, (h, g) in enumerate(wsrcs):
        slot = gi % NSLOT
        key = f"wbuf{slot}"
        src = h.rearrange("(c p) f -> p c f", p=128)[:, :, 256 * g: 256 * g + 256]
        A("G", lambda e, src=src, slot=slot: e.dma_start(out=wb3[slot], in_=src),
          writes=[key], dma_sem=f"dwl{slot}")
        st = A("Y", lambda e, gi=gi, slot=slot: e.dma_start(out=wtmp_h[gi], in_=wbuf[slot][:]),
               reads=[key], dma_sem=f"dws{slot}")
        if gi >= len(wsrcs) - NSLOT:
            wstore_last.append(st)

    wload_i = [0]

    def load_w(gi):
        slot = wload_i[0] % NSLOT
        wload_i[0] += 1
        key = f"wbuf{slot}"
        st = A("Y", lambda e, gi=gi, slot=slot: e.dma_start(out=wbuf[slot][:], in_=wtmp_h[gi]),
               writes=[key], deps=wstore_last, dma_sem=f"dst{slot}")
        return st, slot, key

    PSK = [f"ps{i}" for i in range(8)]

    def emit_ln_phases(seq, J):
        first = (seq == 0 and J == 0)
        b4row = ps[0:1, 2048:2560]

        def ph_dma(seq=seq, J=J, first=first):
            dstk = "xres" if first else "outb"
            dst = xres[:] if first else outb[:]
            A("G", lambda e, seq=seq, J=J, dst=dst: e.dma_start(
                out=dst, in_=x_h[seq, J]),
              writes=[dstk], dma_sem="dx")

        def ph_maps(J=J):
            A("Y", lambda e, J=J: e.dma_start(
                out=mapblk.rearrange("p (g t) -> p g t", g=2),
                in_=maps_h.rearrange("p (g t) -> p g t", g=2)[:, :, 512 * J: 512 * J + 512]),
              writes=["mapblk", "rsigt"], dma_sem="dmap")

        def ph0():
            for c in range(CH):
                A("T", lambda e, c=c: e.matmul(b4row, ones128[:], xres3[:, c, :],
                                               start=(c == 0), stop=(c == CH - 1)),
                  reads=["xres", "ones128"],
                  writes=["ps4"] if c in (0, CH - 1) else [])

        def ph1():
            A("V", lambda e: e.tensor_scalar(out=s2_s[:], in0=b4row,
                                             scalar1=1.0 / C, scalar2=None, op0=OP.mult),
              reads=["ps4"], writes=["s2_s"])
            A("V", lambda e: e.tensor_tensor(out=r_s[:], in0=s2_s[:], in1=s2_s[:], op=OP.mult),
              reads=["s2_s"], writes=["r_s"])
            A("V", lambda e: e.tensor_tensor(out=call3, in0=xres3, in1=xres3, op=OP.mult),
              reads=["xres"], writes=["call"])
            for c in range(CH):
                A("T", lambda e, c=c: e.matmul(b4row, ones128[:], call3[:, c, :],
                                               start=(c == 0), stop=(c == CH - 1)),
                  reads=["call", "ones128"],
                  writes=["ps4"] if c in (0, CH - 1) else [])

        def ph2(first=first):
            A("V", lambda e: e.scalar_tensor_tensor(out=r_s[:], in0=b4row,
                                                    scalar=1.0 / C, in1=r_s[:],
                                                    op0=OP.mult, op1=OP.subtract),
              reads=["ps4", "r_s"], writes=["r_s"])
            A("T", lambda e: e.matmul(bank(4), onesf1[:], s2_s[:], start=True, stop=True),
              reads=["s2_s", "onesf1", "ps4"], writes=["ps4"])
            A("S", lambda e: e.activation(s2_s[:], r_s[:], AF.Ln, bias=1e-5, scale=1.0),
              reads=["r_s", "consts"], writes=["s2_s"])
            A("S", lambda e: e.activation(r_s[:], s2_s[:], AF.Exp, scale=-0.5),
              reads=["s2_s", "consts"], writes=["r_s"])
            if not first:
                A("V", lambda e: e.tensor_copy(xtail[:], xn_ap[:, 512::513]),
                  reads=["xn"], writes=["xtail"])

        def ph3():
            A("V", lambda e: e.tensor_tensor(
                out=call3, in0=xres3,
                in1=bank(4).unsqueeze(1).broadcast_to([128, CH, 512]), op=OP.subtract),
              reads=["xres", "ps4", "call"], writes=["call"])

        def ph4(J=J):
            A("T", lambda e: e.matmul(bank(4), onesf1[:], r_s[:], start=True, stop=True),
              reads=["r_s", "onesf1"], writes=["ps4"])
            A("V", lambda e: e.tensor_tensor(
                out=xn3[:, :, 1:513], in0=call3,
                in1=bank(4).unsqueeze(1).broadcast_to([128, CH, 512]), op=OP.mult),
              reads=["call", "ps4"], writes=["xn"])
            if J == 0:
                A("V", lambda e: e.memset(xn_ap[:, 0::513], 0.0), reads=["xn"], writes=["xn"])
            else:
                A("V", lambda e: e.tensor_copy(xn_ap[:, 0::513], xtail[:]),
                  reads=["xtail", "xn"], writes=["xn"])

        return [ph_dma, ph_maps, ph0, ph1, ph2, ph3, ph4]

    def emit_ln(seq, J):
        for ph in emit_ln_phases(seq, J):
            ph()

    emit_ln(0, 0)
    pending_wo = [None]
    for seq in range(BSEQ):
        for J in range(NJ):
            is_last = (seq == BSEQ - 1 and J == NJ - 1)
            if pending_wo[0] is not None:
                pending_wo[0]()
                pending_wo[0] = None
            ln_phases = []
            if not is_last:
                nseq, nJ = (seq, J + 1) if J + 1 < NJ else (seq + 1, 0)
                ln_phases = emit_ln_phases(nseq, nJ)
            for g in range(12):
                wst, slot, wkey = load_w(g)
                if g < 8:
                    part = "K" if g < 4 else "Q"
                    for fc2 in range(2):
                        gch = (g % 4) * 2 + fc2
                        bk = 2 + gch % 2
                        bkey = PSK[bk]
                        for ci in range(CH):
                            A("T", lambda e, slot=slot, fc2=fc2, ci=ci, bk=bk: e.matmul(
                                bank(bk), wb3[slot][:, ci, 128 * fc2:128 * fc2 + 128],
                                xn3[:, ci, 1:513],
                                start=(ci == 0), stop=(ci == CH - 1)),
                              reads=[wkey, "xn"],
                              writes=[bkey] if ci in (0, CH - 1) else [], deps=[wst])
                        if part == "K":
                            key = f"kK{gch}"
                            tgt = kt_ap
                            sA = T * (2 * gch) + 512 * J
                            sB = T * (2 * gch + 1) + 512 * J
                            extra = []
                        else:
                            key = f"kQ{gch}"
                            tgt = qt_ap
                            sA = 512 * (2 * gch)
                            sB = 512 * (2 * gch + 1)
                            extra = [f"kQa{i}" for i in range(4)]
                        A("V", lambda e, gch=gch, bk=bk: e.tensor_tensor(
                            out=call3[:, gch, :], in0=bank(bk), in1=mapblk[:, 512:1024],
                            op=OP.mult),
                          reads=[bkey, "mapblk", "rsigt"], writes=["call"])
                        A("V", lambda e, bk=bk, tgt=tgt, sA=sA: e.tensor_tensor(
                            out=tgt[0:64, sA:sA + 512], in0=ps[0:64, 512 * bk:512 * bk + 512],
                            in1=mapblk[0:64, 0:512], op=OP.mult),
                          reads=[bkey, "mapblk", "rsigt"] + extra, writes=[key])
                        A("V", lambda e, bk=bk, tgt=tgt, sB=sB: e.tensor_tensor(
                            out=tgt[0:64, sB:sB + 512], in0=ps[64:128, 512 * bk:512 * bk + 512],
                            in1=mapblk[0:64, 0:512], op=OP.mult),
                          reads=[bkey, "mapblk", "rsigt"] + extra, writes=[key])
                    if g % 4 == 3:
                        if part == "K":
                            d32 = kt_ap[0:32, :].rearrange("p (h t) -> p h t", h=H)
                            d32b = kt_ap[32:64, :].rearrange("p (h t) -> p h t", h=H)
                            dsts32 = [d32[:, 0::2, 512 * J:512 * J + 512],
                                      d32b[:, 0::2, 512 * J:512 * J + 512],
                                      d32[:, 1::2, 512 * J:512 * J + 512],
                                      d32b[:, 1::2, 512 * J:512 * J + 512]]
                            sb2, kd = "dkb", "kK"
                        else:
                            d32 = qt_ap[0:32, :].rearrange("p (h t) -> p h t", h=H)
                            d32b = qt_ap[32:64, :].rearrange("p (h t) -> p h t", h=H)
                            dsts32 = [d32[:, 0::2, :], d32b[:, 0::2, :],
                                      d32[:, 1::2, :], d32b[:, 1::2, :]]
                            sb2, kd = "dqb", "kQ"
                        srcs = [call3[32:64, :, :], call3[0:32, :, :],
                                call3[96:128, :, :], call3[64:96, :, :]]
                        allk = [f"{kd}{c}" for c in range(CH)]
                        for i in range(4):
                            A("G", lambda e, i=i, dsts32=dsts32, srcs=srcs: e.dma_start(
                                out=dsts32[i], in_=srcs[i], accum_op=OP.add),
                              reads=["call"] + allk, writes=[f"{kd}a{i}"], dma_sem=sb2)
                else:
                    for tt in range(4):
                        bk = 2 + tt % 2
                        bkey = PSK[bk]
                        for ci in range(CH):
                            A("T", lambda e, slot=slot, tt=tt, ci=ci, bk=bk: e.matmul(
                                ps[:, 512 * bk: 512 * bk + 256],
                                xn3[:, ci, 1 + 128 * tt:1 + 128 * tt + 128],
                                wb3[slot][:, ci, :],
                                start=(ci == 0), stop=(ci == CH - 1)),
                              reads=[wkey, "xn"],
                              writes=[bkey] if ci in (0, CH - 1) else [], deps=[wst])
                        ktile = 4 * J + tt
                        h0 = (g - 8) * 4
                        A("V", lambda e, ktile=ktile, h0=h0, bk=bk: e.tensor_copy(
                            vt3[:, ktile:ktile + 1, h0:h0 + 4, 0:64],
                            ps[:, 512 * bk: 512 * bk + 256].rearrange(
                                "p (o h d) -> p o h d", o=1, h=4)),
                          reads=[bkey], writes=["vtok"])

            KD = [f"kK{c}" for c in range(CH)] + [f"kKa{i}" for i in range(4)]
            QD = [f"kQ{c}" for c in range(CH)] + [f"kQa{i}" for i in range(4)]

            ffn_work = []
            ffn_loadidx = []
            W = ffn_work.append
            W(lambda: A("V", lambda e: e.tensor_tensor(
                out=call3, in0=xn3[:, :, 1:513], in1=xn3[:, :, 0:512], op=OP.subtract),
                reads=["xn"], writes=["call"]))
            for c in range(CH):
                W(lambda c=c: A("V", lambda e, c=c: e.scalar_tensor_tensor(
                    out=xrk3[:, c, :], in0=call3[:, c, :], scalar=mu2[:, c:c + 1],
                    in1=xn3[:, c, 0:512], op0=OP.mult, op1=OP.add),
                    reads=["call", "xn", "mu2"], writes=["xrk"]))

            def ffn_proj(g0, rhs3, rhs_key, epi):
                for g in range(g0, g0 + 4):
                    holder = {}
                    ffn_loadidx.append(len(ffn_work))
                    W(lambda holder=holder, g=g: holder.update(zip(("st", "slot", "key"), load_w(g))))
                    for fc2 in range(2):
                        fco = (g - g0) * 2 + fc2
                        bk = 6 + fco % 2
                        bkey = PSK[bk]
                        for ci in range(CH):
                            W(lambda holder=holder, fc2=fc2, ci=ci, bk=bk, bkey=bkey,
                              rhs3=rhs3, rhs_key=rhs_key: A(
                                "T", lambda e, holder=holder, fc2=fc2, ci=ci, bk=bk, rhs3=rhs3: e.matmul(
                                    bank(bk), wb3[holder["slot"]][:, ci, 128 * fc2:128 * fc2 + 128],
                                    rhs3[:, ci, :],
                                    start=(ci == 0), stop=(ci == CH - 1)),
                                reads=[holder["key"], rhs_key],
                                writes=[bkey] if ci in (0, CH - 1) else [],
                                deps=[holder["st"]]))
                        W(lambda fco=fco, bk=bk, bkey=bkey, epi=epi: epi(fco, bk, bkey))

            def sig_epi(fco, bk, bkey):
                A("S", lambda e, bk=bk: e.activation(
                    bank(bk), bank(bk), AF.Tanh, scale=0.5),
                  reads=[bkey, "consts"], writes=[bkey])
                return A("V", lambda e, fco=fco, bk=bk: e.tensor_scalar(
                    out=rsig3[:, fco, :], in0=bank(bk),
                    scalar1=0.5, scalar2=0.5, op0=OP.mult, op1=OP.add),
                  reads=[bkey], writes=["rsigt"])

            def relu_epi(fco, bk, bkey):
                A("S", lambda e, fco=fco, bk=bk: e.activation(
                    call3[:, fco, :], bank(bk), AF.Relu),
                  reads=[bkey, "consts"], writes=["call"])
                return A("V", lambda e, fco=fco: e.tensor_tensor(
                    out=call3[:, fco, :], in0=call3[:, fco, :],
                    in1=call3[:, fco, :], op=OP.mult),
                  reads=["call"], writes=["call"])

            def wv_epi(fco, bk, bkey):
                return A("V", lambda e, fco=fco, bk=bk: e.tensor_tensor(
                    out=xrk3[:, fco, :], in0=bank(bk), in1=rsig3[:, fco, :],
                    op=OP.mult),
                  reads=[bkey, "rsigt"], writes=["xrk"])

            ffn_proj(16, xrk3, "xrk", sig_epi)
            for c in range(CH):
                W(lambda c=c: A("V", lambda e, c=c: e.scalar_tensor_tensor(
                    out=xrk3[:, c, :], in0=call3[:, c, :],
                    scalar=mu2[:, CH + c:CH + c + 1],
                    in1=xn3[:, c, 0:512], op0=OP.mult, op1=OP.add),
                    reads=["call", "xn", "mu2"], writes=["xrk"]))
            ffn_proj(20, xrk3, "xrk", relu_epi)
            ffn_proj(24, call3, "call", wv_epi)
            W(lambda: A("V", lambda e: e.tensor_tensor(
                out=xrk3, in0=xrk3, in1=xres3, op=OP.add),
                reads=["xres", "xrk"], writes=["xrk"]))
            LOOKAHEAD = 30
            marked = set(ffn_loadidx)
            loads = [ffn_work[i] for i in sorted(marked)]
            rest = [f for i, f in enumerate(ffn_work) if i not in marked]
            targets = []
            cnt = 0
            for i in range(len(ffn_work)):
                if i in marked:
                    targets.append(max(0, cnt - LOOKAHEAD))
                else:
                    cnt += 1
            woven = []
            ti = 0
            for pos, f in enumerate(rest):
                while ti < len(targets) and targets[ti] <= pos:
                    woven.append(loads[ti])
                    ti += 1
                woven.append(f)
            woven.extend(loads[ti:])
            ffn_work = woven

            n_ffn = len(ffn_work)
            ffn_i = [0]

            def drain_ffn(k):
                while ffn_i[0] < n_ffn and k > 0:
                    ffn_work[ffn_i[0]]()
                    ffn_i[0] += 1
                    k -= 1

            for h in range(H):
                for qtl in range(4):
                    s = 4 * h + qtl
                    kext = 512 * J + 128 * (qtl + 1)
                    nchunks = (kext + 511) // 512
                    if nchunks < 4:
                        if nchunks == 1:
                            base = s % 4
                        elif nchunks == 2:
                            base = 2 * (s % 3)
                        else:
                            base = 3 * (s % 2)
                        bks = PSK[base:base + nchunks]
                        for chn in range(nchunks):
                            N = min(512, kext - 512 * chn)
                            A("T", lambda e, h=h, qtl=qtl, chn=chn, N=N, base=base: e.matmul(
                                ps[:, 512 * (base + chn): 512 * (base + chn) + N],
                                qt_ap[0:64, 512 * h + 128 * qtl: 512 * h + 128 * qtl + 128],
                                kt_ap[0:64, T * h + 512 * chn: T * h + 512 * chn + N],
                                start=True, stop=True),
                              reads=QD + KD, writes=[bks[chn]])
                        A("V", lambda e, kext=kext, base=base: e.tensor_tensor(
                            out=ps[:, 512 * base + kext - 128: 512 * base + kext],
                            in0=ps[:, 512 * base + kext - 128: 512 * base + kext],
                            in1=maskadd[:], op=OP.add),
                          reads=["maskadd"] + bks[:nchunks], writes=[bks[nchunks - 1]])
                        A("S", lambda e, kext=kext, h=h, qtl=qtl, base=base: e.activation(
                            attn[:][:, 0:kext], ps[:, 512 * base: 512 * base + kext],
                            AF.Exp, scale=1.0 / 32.0,
                            accum_out=sgm[:, 4 * h + qtl: 4 * h + qtl + 1]),
                          reads=bks[:nchunks] + ["consts"],
                          writes=[f"attn{i}" for i in range(nchunks)] + ["sgm"])
                    else:
                        pL = 2 * (s % 3)
                        pH = 2 * ((s + 1) % 3)
                        bksH = PSK[pH:pH + 2]
                        bksL = PSK[pL:pL + 2]
                        for chn in (2, 3):
                            N = min(512, kext - 512 * chn)
                            A("T", lambda e, h=h, qtl=qtl, chn=chn, N=N, pH=pH: e.matmul(
                                ps[:, 512 * (pH + chn - 2): 512 * (pH + chn - 2) + N],
                                qt_ap[0:64, 512 * h + 128 * qtl: 512 * h + 128 * qtl + 128],
                                kt_ap[0:64, T * h + 512 * chn: T * h + 512 * chn + N],
                                start=True, stop=True),
                              reads=QD + KD, writes=[PSK[pH + chn - 2]])
                        A("V", lambda e, kext=kext, pH=pH: e.tensor_tensor(
                            out=ps[:, 512 * pH + kext - 1152: 512 * pH + kext - 1024],
                            in0=ps[:, 512 * pH + kext - 1152: 512 * pH + kext - 1024],
                            in1=maskadd[:], op=OP.add),
                          reads=["maskadd"] + bksH, writes=[PSK[pH + 1]])
                        A("S", lambda e, kext=kext, h=h, qtl=qtl, pH=pH: e.activation(
                            attn[:][:, 0:kext - 1024],
                            ps[:, 512 * pH: 512 * pH + kext - 1024],
                            AF.Exp, scale=1.0 / 32.0,
                            accum_out=lnout[:, 4 * h + qtl: 4 * h + qtl + 1]),
                          reads=bksH + ["consts"],
                          writes=["attn0", "attn1", "lnout"])
                        for chn in (0, 1):
                            A("T", lambda e, h=h, qtl=qtl, chn=chn, pL=pL: e.matmul(
                                ps[:, 512 * (pL + chn): 512 * (pL + chn) + 512],
                                qt_ap[0:64, 512 * h + 128 * qtl: 512 * h + 128 * qtl + 128],
                                kt_ap[0:64, T * h + 512 * chn: T * h + 512 * chn + 512],
                                start=True, stop=True),
                              reads=QD + KD, writes=[PSK[pL + chn]])
                        A("S", lambda e, h=h, qtl=qtl, pL=pL: e.activation(
                            attn[:][:, 0:1024], ps[:, 512 * pL: 512 * pL + 1024],
                            AF.Exp, scale=1.0 / 32.0,
                            accum_out=sgm[:, 4 * h + qtl: 4 * h + qtl + 1]),
                          reads=bksL + ["consts"],
                          writes=["attn0", "attn1", "sgm"])
                    drain_ffn(5)
            drain_ffn(n_ffn)
            if ln_phases:
                ln_phases[0]()
                ln_phases[1]()
                ln_phases = ln_phases[2:]
                A("V", lambda e: e.tensor_copy(xres[:], outb[:]),
                  reads=["outb"], writes=["xres"])
            if J == NJ - 1:
                A("V", lambda e: e.tensor_tensor(
                    out=sgm[:], in0=sgm[:], in1=lnout[:], op=OP.add),
                  reads=["sgm", "lnout"], writes=["sgm"])
            A("S", lambda e: e.activation(lnout[:], sgm[:], AF.Ln),
              reads=["sgm", "consts"], writes=["lnout"])
            A("V", lambda e: e.tensor_scalar(out=b2bf[:], in0=lnout[:], scalar1=-32.0,
                                             scalar2=120.0, op0=OP.mult, op1=OP.add),
              reads=["lnout"], writes=["b2bf"])
            A("T", lambda e: e.matmul(ps[0:64, 0:128], b2bf[:], i128[:],
                                      start=True, stop=True),
              reads=["b2bf", "i128"], writes=["ps0"])
            A("V", lambda e: e.tensor_copy(augs[:], ps[0:64, 0:128]),
              reads=["ps0"], writes=["augs"])
            A("G", lambda e: e.dma_start(out=aug_h, in_=augs[:]),
              reads=["augs"], writes=["augh"], dma_sem="db1")
            A("Y", lambda e: e.dma_start(
                out=qt_ap[64:65, :],
                in_=aug_h.rearrange("c q -> (c q)").unsqueeze(0)),
              reads=["augh"], writes=["qaug"], dma_sem="db2")

            wo_loads = [load_w(g) for g in range(12, 16)]

            def mk_chain(hp2):
                def chain(hp2=hp2):
                    A("S", lambda e: e.activation(sln, srowt[:], AF.Ln),
                      reads=["srow", "consts"], writes=["rsigt"])
                    A("S", lambda e: e.activation(recb[:], sln, AF.Exp, scale=-1.0),
                      reads=["rsigt", "consts"], writes=["recb"])
                    for j in range(2):
                        cpair = hp2 - 1 + j
                        co = 512 * j
                        A("T", lambda e, co=co: e.matmul(
                            bank(5), indic[0:1, :], recb[0:1, co:co + 512],
                            start=True, stop=False),
                          reads=["recb", "indic"], writes=["ps5"])
                        A("T", lambda e, co=co: e.matmul(
                            bank(5), indic[32:33, :], recb[32:33, co:co + 512],
                            start=False, stop=True),
                          reads=["recb", "indic"], writes=["ps5"])
                        A("V", lambda e, cpair=cpair: e.tensor_tensor(
                            out=attn3[:, cpair, :], in0=attn3[:, cpair, :],
                            in1=bank(5), op=OP.mult),
                          reads=["ps5", f"attn{cpair}"], writes=[f"attn{cpair}"])
                return chain

            pending_chain = [None]
            for hp2 in range(CH):
                nkt = 4 * J + 4
                hs = (2 * hp2, 2 * hp2 + 1)
                sb_of = (0, 6)
                def emit_pv(kti):
                    qs = max(0, 128 * (kti - 4 * J))
                    for j in range(2):
                        h = hs[j]
                        pb = 2 * j + kti % 2
                        A("T", lambda e, h=h, kti=kti, qs=qs, pb=pb, j=j, nkt=nkt: e.matmul(
                            ps[0:65, 512 * (2 + j) + qs: 512 * (2 + j) + 512],
                            vt_ap[:, 65 * H * kti + 65 * h: 65 * H * kti + 65 * h + 65],
                            p2[pb][:, qs:512],
                            start=(kti == 0), stop=(kti == nkt - 1)),
                          reads=[f"p2_{pb}", "vtok", "vtones"], writes=[PSK[2 + j]])

                for kti in range(nkt):
                    qs = max(0, 128 * (kti - 4 * J))
                    for j in range(2):
                        h = hs[j]
                        sbk = sb_of[j] + kti % 2
                        skey = PSK[sbk]
                        A("T", lambda e, h=h, kti=kti, qs=qs, sbk=sbk: e.matmul(
                            ps[:, 512 * sbk + qs: 512 * sbk + 512],
                            kt_ap[0:65, T * h + 128 * kti: T * h + 128 * kti + 128],
                            qt_ap[0:65, 512 * h + qs: 512 * h + 512],
                            start=True, stop=True),
                          reads=KD + QD + ["qaug", "ktones"], writes=[skey])
                    for j in range(2):
                        h = hs[j]
                        sbk = sb_of[j] + kti % 2
                        pb = 2 * j + kti % 2
                        A("S", lambda e, kti=kti, qs=qs, sbk=sbk, pb=pb: e.activation(
                            p2[pb][:, qs:512], ps[:, 512 * sbk + qs: 512 * sbk + 512],
                            AF.Exp, scale=0.125),
                          reads=[PSK[sbk], "consts"], writes=[f"p2_{pb}"])
                    if kti >= 4 * J:
                        for j in range(2):
                            pb = 2 * j + kti % 2
                            A("V", lambda e, kti=kti, qs=qs, pb=pb: e.tensor_tensor(
                                out=p2[pb][:, qs:qs + 128],
                                in0=p2[pb][:, qs:qs + 128],
                                in1=tri01[:], op=OP.mult),
                              reads=["tri01", f"p2_{pb}"], writes=[f"p2_{pb}"])
                    if kti > 0:
                        emit_pv(kti - 1)
                    if kti == 2 and pending_chain[0] is not None:
                        pending_chain[0]()
                        pending_chain[0] = None
                emit_pv(nkt - 1)
                for j in range(2):
                    h = hs[j]
                    pvb = 2 + j
                    pvk = PSK[pvb]
                    A("V", lambda e, h=h, pvb=pvb: e.tensor_copy(
                        attn[:][64 * (h % 2): 64 * (h % 2) + 64,
                                512 * (h // 2): 512 * (h // 2) + 512],
                        ps[0:64, 512 * pvb: 512 * pvb + 512]),
                      reads=[pvk], writes=[f"attn{hp2}"])
                    A("V", lambda e, h=h, pvb=pvb: e.tensor_copy(
                        srowt[32 * (h % 2):32 * (h % 2) + 1][:, 512 * ((h // 2) % 2):512 * ((h // 2) % 2) + 512],
                        ps[64:65, 512 * pvb: 512 * pvb + 512]),
                      reads=[pvk], writes=["srow"])
                if hp2 % 2 == 1:
                    pending_chain[0] = mk_chain(hp2)
                if hp2 < len(ln_phases):
                    ln_phases[hp2]()

            def mk_wo(wo_loads=wo_loads, seq=seq, J=J, chain=pending_chain[0]):
                chain_h = [chain]

                def wo_emit():
                    for g in range(12, 16):
                        wst, slot, wkey = wo_loads[g - 12]
                        for fc2 in range(2):
                            fco = (g - 12) * 2 + fc2
                            bk = fco % 2
                            bkey = PSK[bk]
                            for dc in range(CH):
                                if dc == 6 and chain_h[0] is not None:
                                    chain_h[0]()
                                    chain_h[0] = None
                                A("T", lambda e, slot=slot, fc2=fc2, dc=dc, bk=bk: e.matmul(
                                    bank(bk), wb3[slot][:, dc, 128 * fc2:128 * fc2 + 128],
                                    attn3[:, dc, :],
                                    start=(dc == 0), stop=(dc == CH - 1)),
                                  reads=[wkey, f"attn{dc}"],
                                  writes=[bkey] if dc in (0, CH - 1) else [], deps=[wst])
                            A("V", lambda e, fco=fco, bk=bk: e.tensor_tensor(
                                out=outb3[:, fco, :], in0=bank(bk), in1=xrk3[:, fco, :],
                                op=OP.add),
                              reads=[bkey, "xrk"], writes=["outb"])
                    A("G", lambda e, seq=seq, J=J: e.dma_start(
                        out=y_h[seq, J], in_=outb[:]),
                      reads=["outb"], dma_sem="dy")
                return wo_emit

            pending_wo[0] = mk_wo()
            pending_chain[0] = None
            if is_last:
                pending_wo[0]()
                pending_wo[0] = None

    last_y = [s for s in P.steps if s.sem == "dy"][-1]
    A("Y", lambda e: e.nop(), deps=[last_y], dma_sem=None)
    P.steps[-1].sem = "fin"
    P.steps[-1].val = 1
    A("G", lambda e: e.nop(), deps=[last_y])
    P.steps[-1].sem = "fin"
    P.steps[-1].val = 2

    with es:
        with nc.Block() as block:
            P.emit(block, semh)
    return nc


def _deint_perm():
    perm = np.arange(3 * C)
    for blk in range(2):
        for h in range(H):
            base = blk * C + h * DH
            idx = np.concatenate([np.arange(0, DH, 2), np.arange(1, DH, 2)])
            perm[base: base + DH] = base + idx
    return perm


def _host_inputs(inputs):
    x = np.ascontiguousarray(np.asarray(inputs["L0_attn_x"], dtype=np.float32))
    wkqv = np.asarray(inputs["L0_attn_W_kqv"], dtype=np.float32)
    wo = np.asarray(inputs["L0_attn_W_o"], dtype=np.float32)
    wr = np.asarray(inputs["L0_ffn_W_r"], dtype=np.float32)
    wk = np.asarray(inputs["L0_ffn_W_k"], dtype=np.float32)
    wv = np.asarray(inputs["L0_ffn_W_v"], dtype=np.float32)
    mu_r = np.asarray(inputs["L0_ffn_mu_r"], dtype=np.float32)
    mu_k = np.asarray(inputs["L0_ffn_mu_k"], dtype=np.float32)

    wkqvt = np.ascontiguousarray(np.ascontiguousarray(wkqv.T)[:, _deint_perm()])
    wot = np.ascontiguousarray(wo.T)
    wrt = np.ascontiguousarray(wr.T)
    wkt = np.ascontiguousarray(wk.T)
    wvt = np.ascontiguousarray(wv.T)
    mu2 = np.ascontiguousarray(np.concatenate(
        [mu_r.reshape(CH, 128).T, mu_k.reshape(CH, 128).T], axis=1))

    pos = np.arange(T, dtype=np.float64)
    inv = THETA ** (-(np.arange(32, dtype=np.float64) / 32.0))
    ang = pos[None, :] * inv[:, None]
    cos32 = np.cos(ang)
    sin32 = np.sin(ang)
    cmap = np.concatenate([cos32, cos32, cos32, cos32], axis=0)
    smap = np.concatenate([sin32, -sin32, sin32, -sin32], axis=0)
    maps = np.ascontiguousarray(
        np.concatenate([cmap, smap], axis=1).astype(np.float32))

    in_maps = []
    for c in range(NCORES):
        xb = x[BSEQ * c: BSEQ * (c + 1)]
        xc = np.ascontiguousarray(
            xb.reshape(BSEQ, NJ, 512, CH, 128).transpose(0, 1, 4, 3, 2)
        ).reshape(BSEQ, NJ, 128, CH * 512)
        in_maps.append({
            "x": xc, "wkqvt": wkqvt, "wrt": wrt, "wkt": wkt, "wvt": wvt,
            "wot": wot, "mu2": mu2, "ropemaps": maps,
        })
    return in_maps


def kernel(**inputs):
    in_maps = _host_inputs(inputs)
    nc = build_program()
    res = run_bass_kernel_spmd(nc, in_maps, core_ids=list(range(NCORES)))
    outs = []
    for c in range(NCORES):
        y = np.asarray(res.results[c]["y"]).reshape(BSEQ, NJ, 128, CH, 512)
        outs.append(y.transpose(0, 1, 4, 3, 2).reshape(BSEQ, T, C))
    return np.ascontiguousarray(np.concatenate(outs, axis=0), dtype=np.float32)


if __name__ == "__main__":
    nc = build_program()
    print("built ok, instructions:", sum(1 for _ in nc.all_instructions()))


# version 35
# speedup vs baseline: 1.0459x; 1.0216x over previous
import numpy as np
import concourse.bass as bass
import concourse.mybir as mybir
from concourse.bass_utils import run_bass_kernel_spmd
from contextlib import ExitStack

F32 = mybir.dt.float32
BF16 = mybir.dt.bfloat16
AF = mybir.ActivationFunctionType
OP = mybir.AluOpType

NCORES = 8
BSEQ = 2
T = 2048
C = 1024
H = 16
DH = 64
CH = C // 128
NJ = T // 512
KTT = T // 128
THETA = 10000.0


class Step:
    __slots__ = ("eng", "fn", "sem", "val", "deps", "snap", "inc", "waits",
                 "inc_on", "force_prev")

    def __init__(self, eng, fn, sem, val, deps, inc):
        self.eng, self.fn, self.sem, self.val = eng, fn, sem, val
        self.deps, self.inc = deps, inc
        self.snap = None
        self.waits = ()
        self.force_prev = None


class Prog:
    ENGS = ("Y", "G", "T", "V", "S")
    NOCHAIN = ("dka", "dkb", "dqa", "dqb")

    def __init__(self):
        self.steps = []
        self.sem_cnt = {}
        self.last_w = {}
        self.last_r = {}
        self.last_dma = {}

    def add(self, eng, fn, reads=(), writes=(), deps=(), dma_sem=None):
        is_dma = dma_sem is not None
        dl = [d for d in deps if d is not None]
        for k in reads:
            w = self.last_w.get(k)
            if w is not None:
                dl.append(w)
        for k in writes:
            w = self.last_w.get(k)
            if w is not None:
                dl.append(w)
            dl.extend(self.last_r.get(k, ()))
        if dma_sem is not None:
            sem, inc = dma_sem, 16
        else:
            sem, inc = "e" + eng, 1
        s = Step(eng, fn, sem, 0, dl, inc)
        if dma_sem is not None and dma_sem not in self.NOCHAIN:
            prev = self.last_dma.get(dma_sem)
            if prev is not None:
                dl.append(prev)
                s.force_prev = prev
            self.last_dma[dma_sem] = s
        self.steps.append(s)
        for k in reads:
            self.last_r.setdefault(k, []).append(s)
        for k in writes:
            self.last_w[k] = s
            self.last_r[k] = []
        return s

    def plan(self):
        for s in self.steps:
            s.inc_on = (s.inc == 16)
        for s in self.steps:
            for d in s.deps:
                d.inc_on = True
        self.sem_cnt = {}
        for s in self.steps:
            if s.inc_on:
                v = self.sem_cnt.get(s.sem, 0) + s.inc
                self.sem_cnt[s.sem] = v
                s.val = v
        know = {e: {} for e in self.ENGS}
        for s in self.steps:
            k = know[s.eng]
            wd = {}
            for d in s.deps:
                if k.get(d.sem, 0) < d.val:
                    wd[d.sem] = max(wd.get(d.sem, 0), d.val)
                    k[d.sem] = d.val
                    if d.snap:
                        for sm, v in d.snap.items():
                            if k.get(sm, 0) < v:
                                k[sm] = v
            if s.force_prev is not None and s.force_prev.val > 0:
                fp = s.force_prev
                if wd.get(fp.sem, 0) < fp.val:
                    wd[fp.sem] = fp.val
                    if k.get(fp.sem, 0) < fp.val:
                        k[fp.sem] = fp.val
            s.waits = sorted(wd.items())
            snap = dict(k)
            snap[s.sem] = max(snap.get(s.sem, 0), s.val)
            s.snap = snap

    def emit(self, block, handles):
        self.plan()
        by_eng = {e: [s for s in self.steps if s.eng == e] for e in self.ENGS}

        def runner(ename):
            def run(eng):
                for s in by_eng[ename]:
                    for sm, v in s.waits:
                        eng.wait_ge(handles[sm], v)
                    inst = s.fn(eng)
                    if s.inc_on:
                        inst.then_inc(handles[s.sem], s.inc)
            return run

        block.sync(runner("Y"))
        block.gpsimd(runner("G"))
        block.tensor(runner("T"))
        block.vector(runner("V"))
        block.scalar(runner("S"))


def build_program():
    nc = bass.Bass("TRN2", target_bir_lowering=False)
    P = Prog()
    A = P.add

    x_h = nc.dram_tensor("x", [BSEQ, NJ, 128, CH * 512], F32, kind="ExternalInput").ap()
    wkqv_h = nc.dram_tensor("wkqvt", [C, 3 * C], F32, kind="ExternalInput").ap()
    wo_h = nc.dram_tensor("wot", [C, C], F32, kind="ExternalInput").ap()
    wr_h = nc.dram_tensor("wrt", [C, C], F32, kind="ExternalInput").ap()
    wk_h = nc.dram_tensor("wkt", [C, C], F32, kind="ExternalInput").ap()
    wv_h = nc.dram_tensor("wvt", [C, C], F32, kind="ExternalInput").ap()
    mu_h = nc.dram_tensor("mu2", [128, 2 * CH], F32, kind="ExternalInput").ap()
    maps_h = nc.dram_tensor("ropemaps", [128, 2 * T], F32, kind="ExternalInput").ap()
    y_h = nc.dram_tensor("y", [BSEQ, NJ, 128, CH * 512], F32, kind="ExternalOutput").ap()
    NGRP = 28
    wtmp_h = nc.dram_tensor("wtmp", [NGRP, 128, 2048], BF16, kind="Internal").ap()
    aug_h = nc.dram_tensor("augh", [64, 128], BF16, kind="Internal").ap()

    es = ExitStack()
    sbuf = lambda name, shape, dt: es.enter_context(nc.sbuf_tensor(name, shape, dt))

    NSLOT = 5
    wbuf = [sbuf(f"wbuf{i}", [128, 2048], BF16) for i in range(NSLOT)]
    kt = sbuf("ktil", [65, H * T], BF16)
    vtok = sbuf("vtok", [128, KTT * H * 65], BF16)
    qt = sbuf("qtil", [65, H * 512], BF16)
    xres = sbuf("xres", [128, CH * 512], BF16)
    outb = sbuf("outb", [128, CH * 512], BF16)
    xn = sbuf("xn", [128, CH * 513], BF16)
    call = sbuf("call", [128, CH * 512], BF16)
    attn = sbuf("attn", [128, CH * 512], BF16)
    p2 = [sbuf(f"p2_{i}", [128, 512], BF16) for i in range(4)]
    rsig = sbuf("rsigt", [128, CH * 512], BF16)
    xrk = sbuf("xrk", [128, CH * 512], BF16)
    recb = sbuf("recb", [33, 1024], BF16)
    srowt = sbuf("srowt", [33, 1024], F32)
    sgm = sbuf("sgm", [128, 64], F32)
    lnout = sbuf("lnout", [128, 64], F32)
    b2bf = sbuf("b2bf", [128, 64], BF16)
    mu2 = sbuf("mu2s", [128, 2 * CH], F32)
    r_s = sbuf("r_s", [1, 512], F32)
    s2_s = sbuf("s2_s", [1, 512], F32)
    maskadd = sbuf("maskadd", [128, 128], F32)
    tri01 = sbuf("tri01", [128, 128], BF16)
    i128 = sbuf("i128", [128, 128], BF16)
    indic = sbuf("indic", [33, 128], BF16)
    ones128 = sbuf("ones128", [128, 1], BF16)
    onesf1 = sbuf("onesf1", [1, 128], F32)
    consts = sbuf("consts", [128, 8], F32)
    xtail = sbuf("xtail", [128, CH], BF16)
    augs = sbuf("augs", [64, 128], BF16)

    ps_t = es.enter_context(nc.psum_tensor("ps", [128, 4096], F32))
    ps = ps_t.ap() if hasattr(ps_t, "ap") else ps_t[:]

    def bank(b):
        return ps[:, 512 * b: 512 * b + 512]


    sem_names = ["eY", "eG", "eT", "eV", "eS", "fin",
                 "dwl0", "dwl1", "dwl2", "dwl3", "dwl4", "dws0", "dws1", "dws2", "dws3", "dws4", "dst0", "dst1", "dst2", "dst3", "dst4", "dmu",
                 "dx", "dmap", "dka", "dkb", "dqa", "dqb", "dy", "dy2", "db1", "db2"]
    semh = {n: es.enter_context(nc.semaphore(n)) for n in sem_names}

    CONST_VALS = [0.0, 1.0, 1.0 / 32.0, 0.125, 1e-5, -1.0, 0.5, -0.5]
    for i, v in enumerate(CONST_VALS):
        nc.const_aps.register(F32, v, consts[:, i:i + 1])

    wb3 = [wbuf[i][:].rearrange("p (c f) -> p c f", c=CH) for i in range(NSLOT)]
    vt_ap = vtok[:]
    vt3 = vt_ap.rearrange("p (k h d) -> p k h d", k=KTT, h=H)
    kt_ap = kt[:]
    rsf32 = rsig[:].bitcast(F32)
    mapblk = rsf32[:, 0:1024]
    sln = rsf32[0:33, 1024:2048]
    qt_ap = qt[:]
    call3 = call[:].rearrange("p (c t) -> p c t", c=CH)
    xn_ap = xn[:]
    xn3 = xn_ap.rearrange("p (c t) -> p c t", c=CH)
    attn3 = attn[:].rearrange("p (c t) -> p c t", c=CH)
    pack3 = xrk[:].rearrange("p (c t) -> p c t", c=CH)
    rsig3 = rsig[:].rearrange("p (c t) -> p c t", c=CH)
    xrk3 = xrk[:].rearrange("p (c t) -> p c t", c=CH)
    xres3 = xres[:].rearrange("p (c t) -> p c t", c=CH)
    outb3 = outb[:].rearrange("p (c t) -> p c t", c=CH)

    for i, v in enumerate(CONST_VALS):
        A("V", lambda e, i=i, v=v: e.memset(consts[:, i:i + 1], v), writes=["consts"])
    A("V", lambda e: e.memset(ones128[:], 1.0), writes=["ones128"])
    A("V", lambda e: e.memset(onesf1[:], 1.0), writes=["onesf1"])
    A("V", lambda e: e.memset(kt[64:65, :], 1.0), writes=["ktones"])
    A("V", lambda e: e.memset(indic[0:1, 0:64], 1.0), writes=["indic"])
    A("V", lambda e: e.memset(indic[0:1, 64:128], 0.0), writes=["indic"])
    A("V", lambda e: e.memset(indic[32:33, 0:64], 0.0), writes=["indic"])
    A("V", lambda e: e.memset(indic[32:33, 64:128], 1.0), writes=["indic"])
    A("V", lambda e: e.memset(xtail[:], 0.0), writes=["xtail"])
    A("V", lambda e: e.memset(srowt[:], 1.0), writes=["srow"])
    A("V", lambda e: e.memset(vt3[:, :, :, 64:65], 1.0), writes=["vtones"])
    A("V", lambda e: e.memset(maskadd[:], 0.0), writes=["maskadd"])
    A("G", lambda e: e.affine_select(out=maskadd[:], in_=maskadd[:],
                                     pattern=[[-1, 128]], compare_op=OP.is_ge,
                                     fill=-1.0e4, base=0, channel_multiplier=1),
      reads=["maskadd"], writes=["maskadd"])
    A("V", lambda e: e.memset(tri01[:], 1.0), writes=["tri01"])
    A("G", lambda e: e.affine_select(out=tri01[:], in_=tri01[:],
                                     pattern=[[1, 128]], compare_op=OP.is_ge,
                                     fill=0.0, base=0, channel_multiplier=-1),
      reads=["tri01"], writes=["tri01"])
    A("V", lambda e: e.memset(i128[:], 1.0), writes=["i128"])
    A("G", lambda e: e.affine_select(out=i128[:], in_=i128[:],
                                     pattern=[[1, 128]], compare_op=OP.is_ge,
                                     fill=0.0, base=0, channel_multiplier=-1),
      reads=["i128"], writes=["i128"])
    A("G", lambda e: e.affine_select(out=i128[:], in_=i128[:],
                                     pattern=[[-1, 128]], compare_op=OP.is_ge,
                                     fill=0.0, base=0, channel_multiplier=1),
      reads=["i128"], writes=["i128"])
    A("G", lambda e: e.dma_start(out=mu2[:], in_=mu_h), writes=["mu2"], dma_sem="dmu")

    wsrcs = ([(wkqv_h, g) for g in range(12)] + [(wo_h, g) for g in range(4)] +
             [(wr_h, g) for g in range(4)] + [(wk_h, g) for g in range(4)] +
             [(wv_h, g) for g in range(4)])

    wload_i = [0]
    first_pass = [True]

    def load_w(gi):
        slot = wload_i[0] % NSLOT
        wload_i[0] += 1
        key = f"wbuf{slot}"
        if first_pass[0]:
            h, g = wsrcs[gi]
            src = h.rearrange("(c p) f -> p c f", p=128)[:, :, 256 * g: 256 * g + 256]
            st = A("G", lambda e, src=src, slot=slot: e.dma_start(out=wb3[slot], in_=src),
                   writes=[key], dma_sem=f"dwl{slot}")
            A("Y", lambda e, gi=gi, slot=slot: e.dma_start(out=wtmp_h[gi], in_=wbuf[slot][:]),
              reads=[key], writes=[f"wtmp{gi}"], dma_sem=f"dws{slot}")
            return st, slot, key
        st = A("Y", lambda e, gi=gi, slot=slot: e.dma_start(out=wbuf[slot][:], in_=wtmp_h[gi]),
               reads=[f"wtmp{gi}"], writes=[key], dma_sem=f"dst{slot}")
        return st, slot, key

    PSK = [f"ps{i}" for i in range(8)]

    def emit_ln_phases(seq, J):
        first = (seq == 0 and J == 0)
        b4row = ps[0:1, 2048:2560]

        def ph_dma(seq=seq, J=J, first=first):
            dstk = "xres" if first else "outb"
            dst = xres[:] if first else outb[:]
            A("G", lambda e, seq=seq, J=J, dst=dst: e.dma_start(
                out=dst, in_=x_h[seq, J]),
              writes=[dstk], dma_sem="dx")

        def ph_maps(J=J):
            A("Y", lambda e, J=J: e.dma_start(
                out=mapblk.rearrange("p (g t) -> p g t", g=2),
                in_=maps_h.rearrange("p (g t) -> p g t", g=2)[:, :, 512 * J: 512 * J + 512]),
              writes=["mapblk", "rsigt"], dma_sem="dmap")

        def ph0():
            for c in range(CH):
                A("T", lambda e, c=c: e.matmul(b4row, ones128[:], xres3[:, c, :],
                                               start=(c == 0), stop=(c == CH - 1)),
                  reads=["xres", "ones128"],
                  writes=["ps4"] if c in (0, CH - 1) else [])

        def ph1():
            A("V", lambda e: e.tensor_scalar(out=s2_s[:], in0=b4row,
                                             scalar1=1.0 / C, scalar2=None, op0=OP.mult),
              reads=["ps4"], writes=["s2_s"])
            A("V", lambda e: e.tensor_tensor(out=r_s[:], in0=s2_s[:], in1=s2_s[:], op=OP.mult),
              reads=["s2_s"], writes=["r_s"])
            A("V", lambda e: e.tensor_tensor(out=call3, in0=xres3, in1=xres3, op=OP.mult),
              reads=["xres"], writes=["call"])
            for c in range(CH):
                A("T", lambda e, c=c: e.matmul(b4row, ones128[:], call3[:, c, :],
                                               start=(c == 0), stop=(c == CH - 1)),
                  reads=["call", "ones128"],
                  writes=["ps4"] if c in (0, CH - 1) else [])

        def ph2(first=first):
            A("V", lambda e: e.scalar_tensor_tensor(out=r_s[:], in0=b4row,
                                                    scalar=1.0 / C, in1=r_s[:],
                                                    op0=OP.mult, op1=OP.subtract),
              reads=["ps4", "r_s"], writes=["r_s"])
            A("T", lambda e: e.matmul(bank(4), onesf1[:], s2_s[:], start=True, stop=True),
              reads=["s2_s", "onesf1", "ps4"], writes=["ps4"])
            A("S", lambda e: e.activation(s2_s[:], r_s[:], AF.Ln, bias=1e-5, scale=1.0),
              reads=["r_s", "consts"], writes=["s2_s"])
            A("S", lambda e: e.activation(r_s[:], s2_s[:], AF.Exp, scale=-0.5),
              reads=["s2_s", "consts"], writes=["r_s"])
            if not first:
                A("V", lambda e: e.tensor_copy(xtail[:], xn_ap[:, 512::513]),
                  reads=["xn"], writes=["xtail"])

        def ph3():
            A("V", lambda e: e.tensor_tensor(
                out=call3, in0=xres3,
                in1=bank(4).unsqueeze(1).broadcast_to([128, CH, 512]), op=OP.subtract),
              reads=["xres", "ps4", "call"], writes=["call"])

        def ph4(J=J):
            A("T", lambda e: e.matmul(bank(4), onesf1[:], r_s[:], start=True, stop=True),
              reads=["r_s", "onesf1"], writes=["ps4"])
            A("V", lambda e: e.tensor_tensor(
                out=xn3[:, :, 1:513], in0=call3,
                in1=bank(4).unsqueeze(1).broadcast_to([128, CH, 512]), op=OP.mult),
              reads=["call", "ps4"], writes=["xn"])
            if J == 0:
                A("V", lambda e: e.memset(xn_ap[:, 0::513], 0.0), reads=["xn"], writes=["xn"])
            else:
                A("V", lambda e: e.tensor_copy(xn_ap[:, 0::513], xtail[:]),
                  reads=["xtail", "xn"], writes=["xn"])

        return [ph_dma, ph_maps, ph0, ph1, ph2, ph3, ph4]

    def emit_ln(seq, J):
        for ph in emit_ln_phases(seq, J):
            ph()

    emit_ln(0, 0)
    pending_wo = [None]
    for seq in range(BSEQ):
        for J in range(NJ):
            is_last = (seq == BSEQ - 1 and J == NJ - 1)
            if pending_wo[0] is not None:
                pending_wo[0]()
                pending_wo[0] = None
            ln_phases = []
            if not is_last:
                nseq, nJ = (seq, J + 1) if J + 1 < NJ else (seq + 1, 0)
                ln_phases = emit_ln_phases(nseq, nJ)
            for g in range(12):
                wst, slot, wkey = load_w(g)
                if g < 8:
                    part = "K" if g < 4 else "Q"
                    for fc2 in range(2):
                        gch = (g % 4) * 2 + fc2
                        bk = 2 + gch % 2
                        bkey = PSK[bk]
                        for ci in range(CH):
                            A("T", lambda e, slot=slot, fc2=fc2, ci=ci, bk=bk: e.matmul(
                                bank(bk), wb3[slot][:, ci, 128 * fc2:128 * fc2 + 128],
                                xn3[:, ci, 1:513],
                                start=(ci == 0), stop=(ci == CH - 1)),
                              reads=[wkey, "xn"],
                              writes=[bkey] if ci in (0, CH - 1) else [], deps=[wst])
                        if part == "K":
                            key = f"kK{gch}"
                            tgt = kt_ap
                            sA = T * (2 * gch) + 512 * J
                            sB = T * (2 * gch + 1) + 512 * J
                            extra = []
                        else:
                            key = f"kQ{gch}"
                            tgt = qt_ap
                            sA = 512 * (2 * gch)
                            sB = 512 * (2 * gch + 1)
                            extra = [f"kQa{i}" for i in range(4)]
                        A("V", lambda e, gch=gch, bk=bk: e.tensor_tensor(
                            out=call3[:, gch, :], in0=bank(bk), in1=mapblk[:, 512:1024],
                            op=OP.mult),
                          reads=[bkey, "mapblk", "rsigt"], writes=["call"])
                        A("V", lambda e, bk=bk, tgt=tgt, sA=sA: e.tensor_tensor(
                            out=tgt[0:64, sA:sA + 512], in0=ps[0:64, 512 * bk:512 * bk + 512],
                            in1=mapblk[0:64, 0:512], op=OP.mult),
                          reads=[bkey, "mapblk", "rsigt"] + extra, writes=[key])
                        A("V", lambda e, bk=bk, tgt=tgt, sB=sB: e.tensor_tensor(
                            out=tgt[0:64, sB:sB + 512], in0=ps[64:128, 512 * bk:512 * bk + 512],
                            in1=mapblk[0:64, 0:512], op=OP.mult),
                          reads=[bkey, "mapblk", "rsigt"] + extra, writes=[key])
                    if g % 4 == 3:
                        if part == "K":
                            d32 = kt_ap[0:32, :].rearrange("p (h t) -> p h t", h=H)
                            d32b = kt_ap[32:64, :].rearrange("p (h t) -> p h t", h=H)
                            dsts32 = [d32[:, 0::2, 512 * J:512 * J + 512],
                                      d32b[:, 0::2, 512 * J:512 * J + 512],
                                      d32[:, 1::2, 512 * J:512 * J + 512],
                                      d32b[:, 1::2, 512 * J:512 * J + 512]]
                            sb2, kd = "dkb", "kK"
                        else:
                            d32 = qt_ap[0:32, :].rearrange("p (h t) -> p h t", h=H)
                            d32b = qt_ap[32:64, :].rearrange("p (h t) -> p h t", h=H)
                            dsts32 = [d32[:, 0::2, :], d32b[:, 0::2, :],
                                      d32[:, 1::2, :], d32b[:, 1::2, :]]
                            sb2, kd = "dqb", "kQ"
                        srcs = [call3[32:64, :, :], call3[0:32, :, :],
                                call3[96:128, :, :], call3[64:96, :, :]]
                        allk = [f"{kd}{c}" for c in range(CH)]
                        for i in range(4):
                            A("G", lambda e, i=i, dsts32=dsts32, srcs=srcs: e.dma_start(
                                out=dsts32[i], in_=srcs[i], accum_op=OP.add),
                              reads=["call"] + allk, writes=[f"{kd}a{i}"], dma_sem=sb2)
                else:
                    for tt in range(4):
                        bk = 2 + tt % 2
                        bkey = PSK[bk]
                        for ci in range(CH):
                            A("T", lambda e, slot=slot, tt=tt, ci=ci, bk=bk: e.matmul(
                                ps[:, 512 * bk: 512 * bk + 256],
                                xn3[:, ci, 1 + 128 * tt:1 + 128 * tt + 128],
                                wb3[slot][:, ci, :],
                                start=(ci == 0), stop=(ci == CH - 1)),
                              reads=[wkey, "xn"],
                              writes=[bkey] if ci in (0, CH - 1) else [], deps=[wst])
                        ktile = 4 * J + tt
                        h0 = (g - 8) * 4
                        A("V", lambda e, ktile=ktile, h0=h0, bk=bk: e.tensor_copy(
                            vt3[:, ktile:ktile + 1, h0:h0 + 4, 0:64],
                            ps[:, 512 * bk: 512 * bk + 256].rearrange(
                                "p (o h d) -> p o h d", o=1, h=4)),
                          reads=[bkey], writes=["vtok"])

            KD = [f"kK{c}" for c in range(CH)] + [f"kKa{i}" for i in range(4)]
            QD = [f"kQ{c}" for c in range(CH)] + [f"kQa{i}" for i in range(4)]

            ffn_work = []
            ffn_loadidx = []
            W = ffn_work.append
            W(lambda: A("V", lambda e: e.tensor_tensor(
                out=call3, in0=xn3[:, :, 1:513], in1=xn3[:, :, 0:512], op=OP.subtract),
                reads=["xn"], writes=["call"]))
            for c in range(CH):
                W(lambda c=c: A("V", lambda e, c=c: e.scalar_tensor_tensor(
                    out=xrk3[:, c, :], in0=call3[:, c, :], scalar=mu2[:, c:c + 1],
                    in1=xn3[:, c, 0:512], op0=OP.mult, op1=OP.add),
                    reads=["call", "xn", "mu2"], writes=["xrk"]))

            def ffn_proj(g0, rhs3, rhs_key, epi):
                for g in range(g0, g0 + 4):
                    holder = {}
                    ffn_loadidx.append(len(ffn_work))
                    W(lambda holder=holder, g=g: holder.update(zip(("st", "slot", "key"), load_w(g))))
                    for fc2 in range(2):
                        fco = (g - g0) * 2 + fc2
                        bk = 6 + fco % 2
                        bkey = PSK[bk]
                        for ci in range(CH):
                            W(lambda holder=holder, fc2=fc2, ci=ci, bk=bk, bkey=bkey,
                              rhs3=rhs3, rhs_key=rhs_key: A(
                                "T", lambda e, holder=holder, fc2=fc2, ci=ci, bk=bk, rhs3=rhs3: e.matmul(
                                    bank(bk), wb3[holder["slot"]][:, ci, 128 * fc2:128 * fc2 + 128],
                                    rhs3[:, ci, :],
                                    start=(ci == 0), stop=(ci == CH - 1)),
                                reads=[holder["key"], rhs_key],
                                writes=[bkey] if ci in (0, CH - 1) else [],
                                deps=[holder["st"]]))
                        W(lambda fco=fco, bk=bk, bkey=bkey, epi=epi: epi(fco, bk, bkey))

            def sig_epi(fco, bk, bkey):
                A("S", lambda e, bk=bk: e.activation(
                    bank(bk), bank(bk), AF.Tanh, scale=0.5),
                  reads=[bkey, "consts"], writes=[bkey])
                return A("V", lambda e, fco=fco, bk=bk: e.tensor_scalar(
                    out=rsig3[:, fco, :], in0=bank(bk),
                    scalar1=0.5, scalar2=0.5, op0=OP.mult, op1=OP.add),
                  reads=[bkey], writes=["rsigt"])

            def relu_epi(fco, bk, bkey):
                A("S", lambda e, fco=fco, bk=bk: e.activation(
                    call3[:, fco, :], bank(bk), AF.Relu),
                  reads=[bkey, "consts"], writes=["call"])
                return A("V", lambda e, fco=fco: e.tensor_tensor(
                    out=call3[:, fco, :], in0=call3[:, fco, :],
                    in1=call3[:, fco, :], op=OP.mult),
                  reads=["call"], writes=["call"])

            def wv_epi(fco, bk, bkey):
                return A("V", lambda e, fco=fco, bk=bk: e.tensor_tensor(
                    out=xrk3[:, fco, :], in0=bank(bk), in1=rsig3[:, fco, :],
                    op=OP.mult),
                  reads=[bkey, "rsigt"], writes=["xrk"])

            ffn_proj(16, xrk3, "xrk", sig_epi)
            for c in range(CH):
                W(lambda c=c: A("V", lambda e, c=c: e.scalar_tensor_tensor(
                    out=xrk3[:, c, :], in0=call3[:, c, :],
                    scalar=mu2[:, CH + c:CH + c + 1],
                    in1=xn3[:, c, 0:512], op0=OP.mult, op1=OP.add),
                    reads=["call", "xn", "mu2"], writes=["xrk"]))
            ffn_proj(20, xrk3, "xrk", relu_epi)
            ffn_proj(24, call3, "call", wv_epi)
            W(lambda: A("V", lambda e: e.tensor_tensor(
                out=xrk3, in0=xrk3, in1=xres3, op=OP.add),
                reads=["xres", "xrk"], writes=["xrk"]))
            LOOKAHEAD = 30
            marked = set(ffn_loadidx)
            loads = [ffn_work[i] for i in sorted(marked)]
            rest = [f for i, f in enumerate(ffn_work) if i not in marked]
            targets = []
            cnt = 0
            for i in range(len(ffn_work)):
                if i in marked:
                    targets.append(max(0, cnt - LOOKAHEAD))
                else:
                    cnt += 1
            woven = []
            ti = 0
            for pos, f in enumerate(rest):
                while ti < len(targets) and targets[ti] <= pos:
                    woven.append(loads[ti])
                    ti += 1
                woven.append(f)
            woven.extend(loads[ti:])
            ffn_work = woven

            n_ffn = len(ffn_work)
            ffn_i = [0]

            def drain_ffn(k):
                while ffn_i[0] < n_ffn and k > 0:
                    ffn_work[ffn_i[0]]()
                    ffn_i[0] += 1
                    k -= 1

            for h in range(H):
                for qtl in range(4):
                    s = 4 * h + qtl
                    kext = 512 * J + 128 * (qtl + 1)
                    nchunks = (kext + 511) // 512
                    if nchunks < 4:
                        if nchunks == 1:
                            base = s % 4
                        elif nchunks == 2:
                            base = 2 * (s % 3)
                        else:
                            base = 3 * (s % 2)
                        bks = PSK[base:base + nchunks]
                        for chn in range(nchunks):
                            N = min(512, kext - 512 * chn)
                            A("T", lambda e, h=h, qtl=qtl, chn=chn, N=N, base=base: e.matmul(
                                ps[:, 512 * (base + chn): 512 * (base + chn) + N],
                                qt_ap[0:64, 512 * h + 128 * qtl: 512 * h + 128 * qtl + 128],
                                kt_ap[0:64, T * h + 512 * chn: T * h + 512 * chn + N],
                                start=True, stop=True),
                              reads=QD + KD, writes=[bks[chn]])
                        A("V", lambda e, kext=kext, base=base: e.tensor_tensor(
                            out=ps[:, 512 * base + kext - 128: 512 * base + kext],
                            in0=ps[:, 512 * base + kext - 128: 512 * base + kext],
                            in1=maskadd[:], op=OP.add),
                          reads=["maskadd"] + bks[:nchunks], writes=[bks[nchunks - 1]])
                        A("S", lambda e, kext=kext, h=h, qtl=qtl, base=base: e.activation(
                            attn[:][:, 0:kext], ps[:, 512 * base: 512 * base + kext],
                            AF.Exp, scale=1.0 / 32.0,
                            accum_out=sgm[:, 4 * h + qtl: 4 * h + qtl + 1]),
                          reads=bks[:nchunks] + ["consts"],
                          writes=[f"attn{i}" for i in range(nchunks)] + ["sgm"])
                    else:
                        pL = 2 * (s % 3)
                        pH = 2 * ((s + 1) % 3)
                        bksH = PSK[pH:pH + 2]
                        bksL = PSK[pL:pL + 2]
                        for chn in (2, 3):
                            N = min(512, kext - 512 * chn)
                            A("T", lambda e, h=h, qtl=qtl, chn=chn, N=N, pH=pH: e.matmul(
                                ps[:, 512 * (pH + chn - 2): 512 * (pH + chn - 2) + N],
                                qt_ap[0:64, 512 * h + 128 * qtl: 512 * h + 128 * qtl + 128],
                                kt_ap[0:64, T * h + 512 * chn: T * h + 512 * chn + N],
                                start=True, stop=True),
                              reads=QD + KD, writes=[PSK[pH + chn - 2]])
                        A("V", lambda e, kext=kext, pH=pH: e.tensor_tensor(
                            out=ps[:, 512 * pH + kext - 1152: 512 * pH + kext - 1024],
                            in0=ps[:, 512 * pH + kext - 1152: 512 * pH + kext - 1024],
                            in1=maskadd[:], op=OP.add),
                          reads=["maskadd"] + bksH, writes=[PSK[pH + 1]])
                        A("S", lambda e, kext=kext, h=h, qtl=qtl, pH=pH: e.activation(
                            attn[:][:, 0:kext - 1024],
                            ps[:, 512 * pH: 512 * pH + kext - 1024],
                            AF.Exp, scale=1.0 / 32.0,
                            accum_out=lnout[:, 4 * h + qtl: 4 * h + qtl + 1]),
                          reads=bksH + ["consts"],
                          writes=["attn0", "attn1", "lnout"])
                        for chn in (0, 1):
                            A("T", lambda e, h=h, qtl=qtl, chn=chn, pL=pL: e.matmul(
                                ps[:, 512 * (pL + chn): 512 * (pL + chn) + 512],
                                qt_ap[0:64, 512 * h + 128 * qtl: 512 * h + 128 * qtl + 128],
                                kt_ap[0:64, T * h + 512 * chn: T * h + 512 * chn + 512],
                                start=True, stop=True),
                              reads=QD + KD, writes=[PSK[pL + chn]])
                        A("S", lambda e, h=h, qtl=qtl, pL=pL: e.activation(
                            attn[:][:, 0:1024], ps[:, 512 * pL: 512 * pL + 1024],
                            AF.Exp, scale=1.0 / 32.0,
                            accum_out=sgm[:, 4 * h + qtl: 4 * h + qtl + 1]),
                          reads=bksL + ["consts"],
                          writes=["attn0", "attn1", "sgm"])
                    drain_ffn(5)
            drain_ffn(n_ffn)
            if ln_phases:
                ln_phases[0]()
                ln_phases[1]()
                ln_phases = ln_phases[2:]
                A("V", lambda e: e.tensor_copy(xres[:], outb[:]),
                  reads=["outb"], writes=["xres"])
            if J == NJ - 1:
                A("V", lambda e: e.tensor_tensor(
                    out=sgm[:], in0=sgm[:], in1=lnout[:], op=OP.add),
                  reads=["sgm", "lnout"], writes=["sgm"])
            A("S", lambda e: e.activation(lnout[:], sgm[:], AF.Ln),
              reads=["sgm", "consts"], writes=["lnout"])
            A("V", lambda e: e.tensor_scalar(out=b2bf[:], in0=lnout[:], scalar1=-32.0,
                                             scalar2=120.0, op0=OP.mult, op1=OP.add),
              reads=["lnout"], writes=["b2bf"])
            A("T", lambda e: e.matmul(ps[0:64, 0:128], b2bf[:], i128[:],
                                      start=True, stop=True),
              reads=["b2bf", "i128"], writes=["ps0"])
            A("V", lambda e: e.tensor_copy(augs[:], ps[0:64, 0:128]),
              reads=["ps0"], writes=["augs"])
            A("G", lambda e: e.dma_start(out=aug_h, in_=augs[:]),
              reads=["augs"], writes=["augh"], dma_sem="db1")
            A("Y", lambda e: e.dma_start(
                out=qt_ap[64:65, :],
                in_=aug_h.rearrange("c q -> (c q)").unsqueeze(0)),
              reads=["augh"], writes=["qaug"], dma_sem="db2")

            wo_loads = [load_w(g) for g in range(12, 16)]

            def mk_chain(hp2):
                def chain(hp2=hp2):
                    A("S", lambda e: e.activation(sln, srowt[:], AF.Ln),
                      reads=["srow", "consts"], writes=["rsigt"])
                    A("S", lambda e: e.activation(recb[:], sln, AF.Exp, scale=-1.0),
                      reads=["rsigt", "consts"], writes=["recb"])
                    for j in range(2):
                        cpair = hp2 - 1 + j
                        co = 512 * j
                        A("T", lambda e, co=co: e.matmul(
                            bank(5), indic[0:1, :], recb[0:1, co:co + 512],
                            start=True, stop=False),
                          reads=["recb", "indic"], writes=["ps5"])
                        A("T", lambda e, co=co: e.matmul(
                            bank(5), indic[32:33, :], recb[32:33, co:co + 512],
                            start=False, stop=True),
                          reads=["recb", "indic"], writes=["ps5"])
                        A("V", lambda e, cpair=cpair: e.tensor_tensor(
                            out=attn3[:, cpair, :], in0=attn3[:, cpair, :],
                            in1=bank(5), op=OP.mult),
                          reads=["ps5", f"attn{cpair}"], writes=[f"attn{cpair}"])
                return chain

            pending_chain = [None]
            for hp2 in range(CH):
                nkt = 4 * J + 4
                hs = (2 * hp2, 2 * hp2 + 1)
                sb_of = (0, 6)
                def emit_pv(kti):
                    qs = max(0, 128 * (kti - 4 * J))
                    for j in range(2):
                        h = hs[j]
                        pb = 2 * j + kti % 2
                        A("T", lambda e, h=h, kti=kti, qs=qs, pb=pb, j=j, nkt=nkt: e.matmul(
                            ps[0:65, 512 * (2 + j) + qs: 512 * (2 + j) + 512],
                            vt_ap[:, 65 * H * kti + 65 * h: 65 * H * kti + 65 * h + 65],
                            p2[pb][:, qs:512],
                            start=(kti == 0), stop=(kti == nkt - 1)),
                          reads=[f"p2_{pb}", "vtok", "vtones"], writes=[PSK[2 + j]])

                for kti in range(nkt):
                    qs = max(0, 128 * (kti - 4 * J))
                    for j in range(2):
                        h = hs[j]
                        sbk = sb_of[j] + kti % 2
                        skey = PSK[sbk]
                        A("T", lambda e, h=h, kti=kti, qs=qs, sbk=sbk: e.matmul(
                            ps[:, 512 * sbk + qs: 512 * sbk + 512],
                            kt_ap[0:65, T * h + 128 * kti: T * h + 128 * kti + 128],
                            qt_ap[0:65, 512 * h + qs: 512 * h + 512],
                            start=True, stop=True),
                          reads=KD + QD + ["qaug", "ktones"], writes=[skey])
                    for j in range(2):
                        h = hs[j]
                        sbk = sb_of[j] + kti % 2
                        pb = 2 * j + kti % 2
                        A("S", lambda e, kti=kti, qs=qs, sbk=sbk, pb=pb: e.activation(
                            p2[pb][:, qs:512], ps[:, 512 * sbk + qs: 512 * sbk + 512],
                            AF.Exp, scale=0.125),
                          reads=[PSK[sbk], "consts"], writes=[f"p2_{pb}"])
                    if kti >= 4 * J:
                        for j in range(2):
                            pb = 2 * j + kti % 2
                            A("V", lambda e, kti=kti, qs=qs, pb=pb: e.tensor_tensor(
                                out=p2[pb][:, qs:qs + 128],
                                in0=p2[pb][:, qs:qs + 128],
                                in1=tri01[:], op=OP.mult),
                              reads=["tri01", f"p2_{pb}"], writes=[f"p2_{pb}"])
                    if kti > 0:
                        emit_pv(kti - 1)
                    if kti == 2 and pending_chain[0] is not None:
                        pending_chain[0]()
                        pending_chain[0] = None
                emit_pv(nkt - 1)
                for j in range(2):
                    h = hs[j]
                    pvb = 2 + j
                    pvk = PSK[pvb]
                    A("V", lambda e, h=h, pvb=pvb: e.tensor_copy(
                        attn[:][64 * (h % 2): 64 * (h % 2) + 64,
                                512 * (h // 2): 512 * (h // 2) + 512],
                        ps[0:64, 512 * pvb: 512 * pvb + 512]),
                      reads=[pvk], writes=[f"attn{hp2}"])
                    A("V", lambda e, h=h, pvb=pvb: e.tensor_copy(
                        srowt[32 * (h % 2):32 * (h % 2) + 1][:, 512 * ((h // 2) % 2):512 * ((h // 2) % 2) + 512],
                        ps[64:65, 512 * pvb: 512 * pvb + 512]),
                      reads=[pvk], writes=["srow"])
                if hp2 % 2 == 1:
                    pending_chain[0] = mk_chain(hp2)
                if hp2 < len(ln_phases):
                    ln_phases[hp2]()

            def mk_wo(wo_loads=wo_loads, seq=seq, J=J, chain=pending_chain[0]):
                chain_h = [chain]

                def wo_emit():
                    for g in range(12, 16):
                        wst, slot, wkey = wo_loads[g - 12]
                        for fc2 in range(2):
                            fco = (g - 12) * 2 + fc2
                            bk = fco % 2
                            bkey = PSK[bk]
                            for dc in range(CH):
                                if dc == 6 and chain_h[0] is not None:
                                    chain_h[0]()
                                    chain_h[0] = None
                                A("T", lambda e, slot=slot, fc2=fc2, dc=dc, bk=bk: e.matmul(
                                    bank(bk), wb3[slot][:, dc, 128 * fc2:128 * fc2 + 128],
                                    attn3[:, dc, :],
                                    start=(dc == 0), stop=(dc == CH - 1)),
                                  reads=[wkey, f"attn{dc}"],
                                  writes=[bkey] if dc in (0, CH - 1) else [], deps=[wst])
                            A("V", lambda e, fco=fco, bk=bk: e.tensor_tensor(
                                out=outb3[:, fco, :], in0=bank(bk), in1=xrk3[:, fco, :],
                                op=OP.add),
                              reads=[bkey, "xrk"], writes=["outb"])
                    A("G", lambda e, seq=seq, J=J: e.dma_start(
                        out=y_h[seq, J], in_=outb[:]),
                      reads=["outb"], dma_sem="dy")
                return wo_emit

            pending_wo[0] = mk_wo()
            pending_chain[0] = None
            if is_last:
                pending_wo[0]()
                pending_wo[0] = None
            first_pass[0] = False

    last_y = [s for s in P.steps if s.sem == "dy"][-1]
    A("Y", lambda e: e.nop(), deps=[last_y], dma_sem=None)
    P.steps[-1].sem = "fin"
    P.steps[-1].val = 1
    A("G", lambda e: e.nop(), deps=[last_y])
    P.steps[-1].sem = "fin"
    P.steps[-1].val = 2

    with es:
        with nc.Block() as block:
            P.emit(block, semh)
    return nc


def _deint_perm():
    perm = np.arange(3 * C)
    for blk in range(2):
        for h in range(H):
            base = blk * C + h * DH
            idx = np.concatenate([np.arange(0, DH, 2), np.arange(1, DH, 2)])
            perm[base: base + DH] = base + idx
    return perm


def _host_inputs(inputs):
    x = np.ascontiguousarray(np.asarray(inputs["L0_attn_x"], dtype=np.float32))
    wkqv = np.asarray(inputs["L0_attn_W_kqv"], dtype=np.float32)
    wo = np.asarray(inputs["L0_attn_W_o"], dtype=np.float32)
    wr = np.asarray(inputs["L0_ffn_W_r"], dtype=np.float32)
    wk = np.asarray(inputs["L0_ffn_W_k"], dtype=np.float32)
    wv = np.asarray(inputs["L0_ffn_W_v"], dtype=np.float32)
    mu_r = np.asarray(inputs["L0_ffn_mu_r"], dtype=np.float32)
    mu_k = np.asarray(inputs["L0_ffn_mu_k"], dtype=np.float32)

    wkqvt = np.ascontiguousarray(np.ascontiguousarray(wkqv.T)[:, _deint_perm()])
    wot = np.ascontiguousarray(wo.T)
    wrt = np.ascontiguousarray(wr.T)
    wkt = np.ascontiguousarray(wk.T)
    wvt = np.ascontiguousarray(wv.T)
    mu2 = np.ascontiguousarray(np.concatenate(
        [mu_r.reshape(CH, 128).T, mu_k.reshape(CH, 128).T], axis=1))

    pos = np.arange(T, dtype=np.float64)
    inv = THETA ** (-(np.arange(32, dtype=np.float64) / 32.0))
    ang = pos[None, :] * inv[:, None]
    cos32 = np.cos(ang)
    sin32 = np.sin(ang)
    cmap = np.concatenate([cos32, cos32, cos32, cos32], axis=0)
    smap = np.concatenate([sin32, -sin32, sin32, -sin32], axis=0)
    maps = np.ascontiguousarray(
        np.concatenate([cmap, smap], axis=1).astype(np.float32))

    in_maps = []
    for c in range(NCORES):
        xb = x[BSEQ * c: BSEQ * (c + 1)]
        xc = np.ascontiguousarray(
            xb.reshape(BSEQ, NJ, 512, CH, 128).transpose(0, 1, 4, 3, 2)
        ).reshape(BSEQ, NJ, 128, CH * 512)
        in_maps.append({
            "x": xc, "wkqvt": wkqvt, "wrt": wrt, "wkt": wkt, "wvt": wvt,
            "wot": wot, "mu2": mu2, "ropemaps": maps,
        })
    return in_maps


def kernel(**inputs):
    in_maps = _host_inputs(inputs)
    nc = build_program()
    res = run_bass_kernel_spmd(nc, in_maps, core_ids=list(range(NCORES)))
    outs = []
    for c in range(NCORES):
        y = np.asarray(res.results[c]["y"]).reshape(BSEQ, NJ, 128, CH, 512)
        outs.append(y.transpose(0, 1, 4, 3, 2).reshape(BSEQ, T, C))
    return np.ascontiguousarray(np.concatenate(outs, axis=0), dtype=np.float32)


if __name__ == "__main__":
    nc = build_program()
    print("built ok, instructions:", sum(1 for _ in nc.all_instructions()))


# version 36
# speedup vs baseline: 1.0492x; 1.0032x over previous
import numpy as np
import concourse.bass as bass
import concourse.mybir as mybir
from concourse.bass_utils import run_bass_kernel_spmd
from contextlib import ExitStack

F32 = mybir.dt.float32
BF16 = mybir.dt.bfloat16
AF = mybir.ActivationFunctionType
OP = mybir.AluOpType

NCORES = 8
BSEQ = 2
T = 2048
C = 1024
H = 16
DH = 64
CH = C // 128
NJ = T // 512
KTT = T // 128
THETA = 10000.0


class Step:
    __slots__ = ("eng", "fn", "sem", "val", "deps", "snap", "inc", "waits",
                 "inc_on", "force_prev")

    def __init__(self, eng, fn, sem, val, deps, inc):
        self.eng, self.fn, self.sem, self.val = eng, fn, sem, val
        self.deps, self.inc = deps, inc
        self.snap = None
        self.waits = ()
        self.force_prev = None


class Prog:
    ENGS = ("Y", "G", "T", "V", "S")
    NOCHAIN = ("dka", "dkb", "dqa", "dqb")

    def __init__(self):
        self.steps = []
        self.sem_cnt = {}
        self.last_w = {}
        self.last_r = {}
        self.last_dma = {}

    def add(self, eng, fn, reads=(), writes=(), deps=(), dma_sem=None):
        is_dma = dma_sem is not None
        dl = [d for d in deps if d is not None]
        for k in reads:
            w = self.last_w.get(k)
            if w is not None:
                dl.append(w)
        for k in writes:
            w = self.last_w.get(k)
            if w is not None:
                dl.append(w)
            dl.extend(self.last_r.get(k, ()))
        if dma_sem is not None:
            sem, inc = dma_sem, 16
        else:
            sem, inc = "e" + eng, 1
        s = Step(eng, fn, sem, 0, dl, inc)
        if dma_sem is not None and dma_sem not in self.NOCHAIN:
            prev = self.last_dma.get(dma_sem)
            if prev is not None:
                dl.append(prev)
                s.force_prev = prev
            self.last_dma[dma_sem] = s
        self.steps.append(s)
        for k in reads:
            self.last_r.setdefault(k, []).append(s)
        for k in writes:
            self.last_w[k] = s
            self.last_r[k] = []
        return s

    def plan(self):
        for s in self.steps:
            s.inc_on = (s.inc == 16)
        for s in self.steps:
            for d in s.deps:
                d.inc_on = True
        self.sem_cnt = {}
        for s in self.steps:
            if s.inc_on:
                v = self.sem_cnt.get(s.sem, 0) + s.inc
                self.sem_cnt[s.sem] = v
                s.val = v
        know = {e: {} for e in self.ENGS}
        for s in self.steps:
            k = know[s.eng]
            wd = {}
            for d in s.deps:
                if k.get(d.sem, 0) < d.val:
                    wd[d.sem] = max(wd.get(d.sem, 0), d.val)
                    k[d.sem] = d.val
                    if d.snap:
                        for sm, v in d.snap.items():
                            if k.get(sm, 0) < v:
                                k[sm] = v
            if s.force_prev is not None and s.force_prev.val > 0:
                fp = s.force_prev
                if wd.get(fp.sem, 0) < fp.val:
                    wd[fp.sem] = fp.val
                    if k.get(fp.sem, 0) < fp.val:
                        k[fp.sem] = fp.val
            s.waits = sorted(wd.items())
            snap = dict(k)
            snap[s.sem] = max(snap.get(s.sem, 0), s.val)
            s.snap = snap

    def emit(self, block, handles):
        self.plan()
        by_eng = {e: [s for s in self.steps if s.eng == e] for e in self.ENGS}

        def runner(ename):
            def run(eng):
                for s in by_eng[ename]:
                    for sm, v in s.waits:
                        eng.wait_ge(handles[sm], v)
                    inst = s.fn(eng)
                    if s.inc_on:
                        inst.then_inc(handles[s.sem], s.inc)
            return run

        block.sync(runner("Y"))
        block.gpsimd(runner("G"))
        block.tensor(runner("T"))
        block.vector(runner("V"))
        block.scalar(runner("S"))


def build_program():
    nc = bass.Bass("TRN2", target_bir_lowering=False)
    P = Prog()
    A = P.add

    x_h = nc.dram_tensor("x", [BSEQ, NJ, 128, CH * 512], F32, kind="ExternalInput").ap()
    wkqv_h = nc.dram_tensor("wkqvt", [C, 3 * C], F32, kind="ExternalInput").ap()
    wo_h = nc.dram_tensor("wot", [C, C], F32, kind="ExternalInput").ap()
    wr_h = nc.dram_tensor("wrt", [C, C], F32, kind="ExternalInput").ap()
    wk_h = nc.dram_tensor("wkt", [C, C], F32, kind="ExternalInput").ap()
    wv_h = nc.dram_tensor("wvt", [C, C], F32, kind="ExternalInput").ap()
    mu_h = nc.dram_tensor("mu2", [128, 2 * CH], F32, kind="ExternalInput").ap()
    maps_h = nc.dram_tensor("ropemaps", [128, 2 * T], F32, kind="ExternalInput").ap()
    y_h = nc.dram_tensor("y", [BSEQ, NJ, 128, CH * 512], F32, kind="ExternalOutput").ap()
    NGRP = 28
    wtmp_h = nc.dram_tensor("wtmp", [NGRP, 128, 2048], BF16, kind="Internal").ap()
    aug_h = nc.dram_tensor("augh", [64, 128], BF16, kind="Internal").ap()

    es = ExitStack()
    sbuf = lambda name, shape, dt: es.enter_context(nc.sbuf_tensor(name, shape, dt))

    NSLOT = 5
    wbuf = [sbuf(f"wbuf{i}", [128, 2048], BF16) for i in range(NSLOT)]
    kt = sbuf("ktil", [65, H * T], BF16)
    vtok = sbuf("vtok", [128, KTT * H * 65], BF16)
    qt = sbuf("qtil", [65, H * 512], BF16)
    xres = sbuf("xres", [128, CH * 512], BF16)
    outb = sbuf("outb", [128, CH * 512], BF16)
    xn = sbuf("xn", [128, CH * 513], BF16)
    call = sbuf("call", [128, CH * 512], BF16)
    attn = sbuf("attn", [128, CH * 512], BF16)
    p2 = [sbuf(f"p2_{i}", [128, 512], BF16) for i in range(4)]
    rsig = sbuf("rsigt", [128, CH * 512], BF16)
    xrk = sbuf("xrk", [128, CH * 512], BF16)
    recb = sbuf("recb", [33, 1024], BF16)
    srowt = sbuf("srowt", [33, 1024], F32)
    sgm = sbuf("sgm", [128, 64], F32)
    lnout = sbuf("lnout", [128, 64], F32)
    b2bf = sbuf("b2bf", [128, 64], BF16)
    mu2 = sbuf("mu2s", [128, 2 * CH], F32)
    r_s = sbuf("r_s", [1, 512], F32)
    s2_s = sbuf("s2_s", [1, 512], F32)
    maskadd = sbuf("maskadd", [128, 128], F32)
    tri01 = sbuf("tri01", [128, 128], BF16)
    i128 = sbuf("i128", [128, 128], BF16)
    indic = sbuf("indic", [33, 128], BF16)
    ones128 = sbuf("ones128", [128, 1], BF16)
    onesf1 = sbuf("onesf1", [1, 128], F32)
    consts = sbuf("consts", [128, 8], F32)
    xtail = sbuf("xtail", [128, CH], BF16)
    augs = sbuf("augs", [64, 128], BF16)

    ps_t = es.enter_context(nc.psum_tensor("ps", [128, 4096], F32))
    ps = ps_t.ap() if hasattr(ps_t, "ap") else ps_t[:]

    def bank(b):
        return ps[:, 512 * b: 512 * b + 512]


    sem_names = ["eY", "eG", "eT", "eV", "eS", "fin",
                 "dwl0", "dwl1", "dwl2", "dwl3", "dwl4", "dws0", "dws1", "dws2", "dws3", "dws4", "dst0", "dst1", "dst2", "dst3", "dst4", "dmu",
                 "dx", "dmap", "dka", "dkb", "dqa", "dqb", "dy", "dy2", "db1", "db2"]
    semh = {n: es.enter_context(nc.semaphore(n)) for n in sem_names}

    CONST_VALS = [0.0, 1.0, 1.0 / 32.0, 0.125, 1e-5, -1.0, 0.5, -0.5]
    for i, v in enumerate(CONST_VALS):
        nc.const_aps.register(F32, v, consts[:, i:i + 1])

    wb3 = [wbuf[i][:].rearrange("p (c f) -> p c f", c=CH) for i in range(NSLOT)]
    vt_ap = vtok[:]
    vt3 = vt_ap.rearrange("p (k h d) -> p k h d", k=KTT, h=H)
    kt_ap = kt[:]
    rsf32 = rsig[:].bitcast(F32)
    mapblk = rsf32[:, 0:1024]
    sln = rsf32[0:33, 1024:2048]
    qt_ap = qt[:]
    call3 = call[:].rearrange("p (c t) -> p c t", c=CH)
    xn_ap = xn[:]
    xn3 = xn_ap.rearrange("p (c t) -> p c t", c=CH)
    attn3 = attn[:].rearrange("p (c t) -> p c t", c=CH)
    pack3 = xrk[:].rearrange("p (c t) -> p c t", c=CH)
    rsig3 = rsig[:].rearrange("p (c t) -> p c t", c=CH)
    xrk3 = xrk[:].rearrange("p (c t) -> p c t", c=CH)
    xres3 = xres[:].rearrange("p (c t) -> p c t", c=CH)
    outb3 = outb[:].rearrange("p (c t) -> p c t", c=CH)

    for i, v in enumerate(CONST_VALS):
        A("V", lambda e, i=i, v=v: e.memset(consts[:, i:i + 1], v), writes=["consts"])
    A("V", lambda e: e.memset(ones128[:], 1.0), writes=["ones128"])
    A("V", lambda e: e.memset(onesf1[:], 1.0), writes=["onesf1"])
    A("V", lambda e: e.memset(kt[64:65, :], 1.0), writes=["ktones"])
    A("V", lambda e: e.memset(indic[0:1, 0:64], 1.0), writes=["indic"])
    A("V", lambda e: e.memset(indic[0:1, 64:128], 0.0), writes=["indic"])
    A("V", lambda e: e.memset(indic[32:33, 0:64], 0.0), writes=["indic"])
    A("V", lambda e: e.memset(indic[32:33, 64:128], 1.0), writes=["indic"])
    A("V", lambda e: e.memset(xtail[:], 0.0), writes=["xtail"])
    A("V", lambda e: e.memset(srowt[:], 1.0), writes=["srow"])
    A("V", lambda e: e.memset(vt3[:, :, :, 64:65], 1.0), writes=["vtones"])
    A("V", lambda e: e.memset(maskadd[:], 0.0), writes=["maskadd"])
    A("G", lambda e: e.affine_select(out=maskadd[:], in_=maskadd[:],
                                     pattern=[[-1, 128]], compare_op=OP.is_ge,
                                     fill=-1.0e4, base=0, channel_multiplier=1),
      reads=["maskadd"], writes=["maskadd"])
    A("V", lambda e: e.memset(tri01[:], 1.0), writes=["tri01"])
    A("G", lambda e: e.affine_select(out=tri01[:], in_=tri01[:],
                                     pattern=[[1, 128]], compare_op=OP.is_ge,
                                     fill=0.0, base=0, channel_multiplier=-1),
      reads=["tri01"], writes=["tri01"])
    A("V", lambda e: e.memset(i128[:], 1.0), writes=["i128"])
    A("G", lambda e: e.affine_select(out=i128[:], in_=i128[:],
                                     pattern=[[1, 128]], compare_op=OP.is_ge,
                                     fill=0.0, base=0, channel_multiplier=-1),
      reads=["i128"], writes=["i128"])
    A("G", lambda e: e.affine_select(out=i128[:], in_=i128[:],
                                     pattern=[[-1, 128]], compare_op=OP.is_ge,
                                     fill=0.0, base=0, channel_multiplier=1),
      reads=["i128"], writes=["i128"])
    A("G", lambda e: e.dma_start(out=mu2[:], in_=mu_h), writes=["mu2"], dma_sem="dmu")

    wsrcs = ([(wkqv_h, g) for g in range(12)] + [(wo_h, g) for g in range(4)] +
             [(wr_h, g) for g in range(4)] + [(wk_h, g) for g in range(4)] +
             [(wv_h, g) for g in range(4)])

    wload_i = [0]
    first_pass = [True]

    def load_w(gi):
        slot = wload_i[0] % NSLOT
        wload_i[0] += 1
        key = f"wbuf{slot}"
        if first_pass[0]:
            h, g = wsrcs[gi]
            src = h.rearrange("(c p) f -> p c f", p=128)[:, :, 256 * g: 256 * g + 256]
            st = A("G", lambda e, src=src, slot=slot: e.dma_start(out=wb3[slot], in_=src),
                   writes=[key], dma_sem=f"dwl{slot}")
            A("Y", lambda e, gi=gi, slot=slot: e.dma_start(out=wtmp_h[gi], in_=wbuf[slot][:]),
              reads=[key], writes=[f"wtmp{gi}"], dma_sem=f"dws{slot}")
            return st, slot, key
        st = A("Y", lambda e, gi=gi, slot=slot: e.dma_start(out=wbuf[slot][:], in_=wtmp_h[gi]),
               reads=[f"wtmp{gi}"], writes=[key], dma_sem=f"dst{slot}")
        return st, slot, key

    PSK = [f"ps{i}" for i in range(8)]

    def emit_ln_phases(seq, J):
        first = (seq == 0 and J == 0)
        b4row = ps[0:1, 2048:2560]

        def ph_dma(seq=seq, J=J, first=first):
            dstk = "xres" if first else "outb"
            dst = xres[:] if first else outb[:]
            A("G", lambda e, seq=seq, J=J, dst=dst: e.dma_start(
                out=dst, in_=x_h[seq, J]),
              writes=[dstk], dma_sem="dx")

        def ph_maps(J=J):
            A("Y", lambda e, J=J: e.dma_start(
                out=mapblk.rearrange("p (g t) -> p g t", g=2),
                in_=maps_h.rearrange("p (g t) -> p g t", g=2)[:, :, 512 * J: 512 * J + 512]),
              writes=["mapblk", "rsigt"], dma_sem="dmap")

        def ph0():
            for c in range(CH):
                A("T", lambda e, c=c: e.matmul(b4row, ones128[:], xres3[:, c, :],
                                               start=(c == 0), stop=(c == CH - 1)),
                  reads=["xres", "ones128"],
                  writes=["ps4"] if c in (0, CH - 1) else [])

        def ph1():
            A("V", lambda e: e.tensor_scalar(out=s2_s[:], in0=b4row,
                                             scalar1=1.0 / C, scalar2=None, op0=OP.mult),
              reads=["ps4"], writes=["s2_s"])
            A("V", lambda e: e.tensor_tensor(out=r_s[:], in0=s2_s[:], in1=s2_s[:], op=OP.mult),
              reads=["s2_s"], writes=["r_s"])
            A("V", lambda e: e.tensor_tensor(out=call3, in0=xres3, in1=xres3, op=OP.mult),
              reads=["xres"], writes=["call"])
            for c in range(CH):
                A("T", lambda e, c=c: e.matmul(b4row, ones128[:], call3[:, c, :],
                                               start=(c == 0), stop=(c == CH - 1)),
                  reads=["call", "ones128"],
                  writes=["ps4"] if c in (0, CH - 1) else [])

        def ph2(first=first):
            A("V", lambda e: e.scalar_tensor_tensor(out=r_s[:], in0=b4row,
                                                    scalar=1.0 / C, in1=r_s[:],
                                                    op0=OP.mult, op1=OP.subtract),
              reads=["ps4", "r_s"], writes=["r_s"])
            A("T", lambda e: e.matmul(bank(4), onesf1[:], s2_s[:], start=True, stop=True),
              reads=["s2_s", "onesf1", "ps4"], writes=["ps4"])
            A("S", lambda e: e.activation(s2_s[:], r_s[:], AF.Ln, bias=1e-5, scale=1.0),
              reads=["r_s", "consts"], writes=["s2_s"])
            A("S", lambda e: e.activation(r_s[:], s2_s[:], AF.Exp, scale=-0.5),
              reads=["s2_s", "consts"], writes=["r_s"])
            if not first:
                A("V", lambda e: e.tensor_copy(xtail[:], xn_ap[:, 512::513]),
                  reads=["xn"], writes=["xtail"])

        def ph3():
            A("V", lambda e: e.tensor_tensor(
                out=call3, in0=xres3,
                in1=bank(4).unsqueeze(1).broadcast_to([128, CH, 512]), op=OP.subtract),
              reads=["xres", "ps4", "call"], writes=["call"])

        def ph4(J=J):
            A("T", lambda e: e.matmul(bank(4), onesf1[:], r_s[:], start=True, stop=True),
              reads=["r_s", "onesf1"], writes=["ps4"])
            A("V", lambda e: e.tensor_tensor(
                out=xn3[:, :, 1:513], in0=call3,
                in1=bank(4).unsqueeze(1).broadcast_to([128, CH, 512]), op=OP.mult),
              reads=["call", "ps4"], writes=["xn"])
            if J == 0:
                A("V", lambda e: e.memset(xn_ap[:, 0::513], 0.0), reads=["xn"], writes=["xn"])
            else:
                A("V", lambda e: e.tensor_copy(xn_ap[:, 0::513], xtail[:]),
                  reads=["xtail", "xn"], writes=["xn"])

        return [ph_dma, ph_maps, ph0, ph1, ph2, ph3, ph4]

    def emit_ln(seq, J):
        for ph in emit_ln_phases(seq, J):
            ph()

    emit_ln(0, 0)
    pending_wo = [None]
    for seq in range(BSEQ):
        for J in range(NJ):
            is_last = (seq == BSEQ - 1 and J == NJ - 1)
            if pending_wo[0] is not None:
                pending_wo[0]()
                pending_wo[0] = None
            ln_phases = []
            if not is_last:
                nseq, nJ = (seq, J + 1) if J + 1 < NJ else (seq + 1, 0)
                ln_phases = emit_ln_phases(nseq, nJ)
            for g in range(12):
                wst, slot, wkey = load_w(g)
                if g < 8:
                    part = "K" if g < 4 else "Q"
                    for fc2 in range(2):
                        gch = (g % 4) * 2 + fc2
                        bk = 2 + gch % 2
                        bkey = PSK[bk]
                        for ci in range(CH):
                            A("T", lambda e, slot=slot, fc2=fc2, ci=ci, bk=bk: e.matmul(
                                bank(bk), wb3[slot][:, ci, 128 * fc2:128 * fc2 + 128],
                                xn3[:, ci, 1:513],
                                start=(ci == 0), stop=(ci == CH - 1)),
                              reads=[wkey, "xn"],
                              writes=[bkey] if ci in (0, CH - 1) else [], deps=[wst])
                        if part == "K":
                            key = f"kK{gch}"
                            tgt = kt_ap
                            sA = T * (2 * gch) + 512 * J
                            sB = T * (2 * gch + 1) + 512 * J
                            extra = []
                        else:
                            key = f"kQ{gch}"
                            tgt = qt_ap
                            sA = 512 * (2 * gch)
                            sB = 512 * (2 * gch + 1)
                            extra = [f"kQa{i}" for i in range(4)]
                        A("V", lambda e, gch=gch, bk=bk: e.tensor_tensor(
                            out=call3[:, gch, :], in0=bank(bk), in1=mapblk[:, 512:1024],
                            op=OP.mult),
                          reads=[bkey, "mapblk", "rsigt"], writes=["call"])
                        A("V", lambda e, bk=bk, tgt=tgt, sA=sA: e.tensor_tensor(
                            out=tgt[0:64, sA:sA + 512], in0=ps[0:64, 512 * bk:512 * bk + 512],
                            in1=mapblk[0:64, 0:512], op=OP.mult),
                          reads=[bkey, "mapblk", "rsigt"] + extra, writes=[key])
                        A("V", lambda e, bk=bk, tgt=tgt, sB=sB: e.tensor_tensor(
                            out=tgt[0:64, sB:sB + 512], in0=ps[64:128, 512 * bk:512 * bk + 512],
                            in1=mapblk[0:64, 0:512], op=OP.mult),
                          reads=[bkey, "mapblk", "rsigt"] + extra, writes=[key])
                    if g % 4 == 3:
                        if part == "K":
                            d32 = kt_ap[0:32, :].rearrange("p (h t) -> p h t", h=H)
                            d32b = kt_ap[32:64, :].rearrange("p (h t) -> p h t", h=H)
                            dsts32 = [d32[:, 0::2, 512 * J:512 * J + 512],
                                      d32b[:, 0::2, 512 * J:512 * J + 512],
                                      d32[:, 1::2, 512 * J:512 * J + 512],
                                      d32b[:, 1::2, 512 * J:512 * J + 512]]
                            sb2, kd = "dkb", "kK"
                        else:
                            d32 = qt_ap[0:32, :].rearrange("p (h t) -> p h t", h=H)
                            d32b = qt_ap[32:64, :].rearrange("p (h t) -> p h t", h=H)
                            dsts32 = [d32[:, 0::2, :], d32b[:, 0::2, :],
                                      d32[:, 1::2, :], d32b[:, 1::2, :]]
                            sb2, kd = "dqb", "kQ"
                        srcs = [call3[32:64, :, :], call3[0:32, :, :],
                                call3[96:128, :, :], call3[64:96, :, :]]
                        allk = [f"{kd}{c}" for c in range(CH)]
                        for i in range(4):
                            A("G", lambda e, i=i, dsts32=dsts32, srcs=srcs: e.dma_start(
                                out=dsts32[i], in_=srcs[i], accum_op=OP.add),
                              reads=["call"] + allk, writes=[f"{kd}a{i}"], dma_sem=sb2)
                else:
                    for tt in range(4):
                        bk = 2 + tt % 2
                        bkey = PSK[bk]
                        for ci in range(CH):
                            A("T", lambda e, slot=slot, tt=tt, ci=ci, bk=bk: e.matmul(
                                ps[:, 512 * bk: 512 * bk + 256],
                                xn3[:, ci, 1 + 128 * tt:1 + 128 * tt + 128],
                                wb3[slot][:, ci, :],
                                start=(ci == 0), stop=(ci == CH - 1)),
                              reads=[wkey, "xn"],
                              writes=[bkey] if ci in (0, CH - 1) else [], deps=[wst])
                        ktile = 4 * J + tt
                        h0 = (g - 8) * 4
                        A("V", lambda e, ktile=ktile, h0=h0, bk=bk: e.tensor_copy(
                            vt3[:, ktile:ktile + 1, h0:h0 + 4, 0:64],
                            ps[:, 512 * bk: 512 * bk + 256].rearrange(
                                "p (o h d) -> p o h d", o=1, h=4)),
                          reads=[bkey], writes=["vtok"])

            KD = [f"kK{c}" for c in range(CH)] + [f"kKa{i}" for i in range(4)]
            QD = [f"kQ{c}" for c in range(CH)] + [f"kQa{i}" for i in range(4)]

            ffn_work = []
            ffn_loadidx = []
            W = ffn_work.append
            W(lambda: A("V", lambda e: e.tensor_tensor(
                out=call3, in0=xn3[:, :, 1:513], in1=xn3[:, :, 0:512], op=OP.subtract),
                reads=["xn"], writes=["call"]))
            for c in range(CH):
                W(lambda c=c: A("V", lambda e, c=c: e.scalar_tensor_tensor(
                    out=xrk3[:, c, :], in0=call3[:, c, :], scalar=mu2[:, c:c + 1],
                    in1=xn3[:, c, 0:512], op0=OP.mult, op1=OP.add),
                    reads=["call", "xn", "mu2"], writes=["xrk"]))

            def ffn_proj(g0, rhs3, rhs_key, epi):
                for g in range(g0, g0 + 4):
                    holder = {}
                    ffn_loadidx.append(len(ffn_work))
                    W(lambda holder=holder, g=g: holder.update(zip(("st", "slot", "key"), load_w(g))))
                    for fc2 in range(2):
                        fco = (g - g0) * 2 + fc2
                        bk = 6 + fco % 2
                        bkey = PSK[bk]
                        for ci in range(CH):
                            W(lambda holder=holder, fc2=fc2, ci=ci, bk=bk, bkey=bkey,
                              rhs3=rhs3, rhs_key=rhs_key: A(
                                "T", lambda e, holder=holder, fc2=fc2, ci=ci, bk=bk, rhs3=rhs3: e.matmul(
                                    bank(bk), wb3[holder["slot"]][:, ci, 128 * fc2:128 * fc2 + 128],
                                    rhs3[:, ci, :],
                                    start=(ci == 0), stop=(ci == CH - 1)),
                                reads=[holder["key"], rhs_key],
                                writes=[bkey] if ci in (0, CH - 1) else [],
                                deps=[holder["st"]]))
                        W(lambda fco=fco, bk=bk, bkey=bkey, epi=epi: epi(fco, bk, bkey))

            def sig_epi(fco, bk, bkey):
                A("S", lambda e, bk=bk: e.activation(
                    bank(bk), bank(bk), AF.Tanh, scale=0.5),
                  reads=[bkey, "consts"], writes=[bkey])
                return A("V", lambda e, fco=fco, bk=bk: e.tensor_scalar(
                    out=rsig3[:, fco, :], in0=bank(bk),
                    scalar1=0.5, scalar2=0.5, op0=OP.mult, op1=OP.add),
                  reads=[bkey], writes=["rsigt"])

            def relu_epi(fco, bk, bkey):
                A("V", lambda e, fco=fco, bk=bk: e.tensor_copy(
                    call3[:, fco, :], bank(bk)),
                  reads=[bkey], writes=["call"])
                A("S", lambda e, fco=fco: e.activation(
                    call3[:, fco, :], call3[:, fco, :], AF.Relu),
                  reads=["call", "consts"], writes=["call"])
                return A("V", lambda e, fco=fco: e.tensor_tensor(
                    out=call3[:, fco, :], in0=call3[:, fco, :],
                    in1=call3[:, fco, :], op=OP.mult),
                  reads=["call"], writes=["call"])

            def wv_epi(fco, bk, bkey):
                return A("V", lambda e, fco=fco, bk=bk: e.tensor_tensor(
                    out=xrk3[:, fco, :], in0=bank(bk), in1=rsig3[:, fco, :],
                    op=OP.mult),
                  reads=[bkey, "rsigt"], writes=["xrk"])

            ffn_proj(16, xrk3, "xrk", sig_epi)
            for c in range(CH):
                W(lambda c=c: A("V", lambda e, c=c: e.scalar_tensor_tensor(
                    out=xrk3[:, c, :], in0=call3[:, c, :],
                    scalar=mu2[:, CH + c:CH + c + 1],
                    in1=xn3[:, c, 0:512], op0=OP.mult, op1=OP.add),
                    reads=["call", "xn", "mu2"], writes=["xrk"]))
            ffn_proj(20, xrk3, "xrk", relu_epi)
            ffn_proj(24, call3, "call", wv_epi)
            W(lambda: A("V", lambda e: e.tensor_tensor(
                out=xrk3, in0=xrk3, in1=xres3, op=OP.add),
                reads=["xres", "xrk"], writes=["xrk"]))
            LOOKAHEAD = 30
            marked = set(ffn_loadidx)
            loads = [ffn_work[i] for i in sorted(marked)]
            rest = [f for i, f in enumerate(ffn_work) if i not in marked]
            targets = []
            cnt = 0
            for i in range(len(ffn_work)):
                if i in marked:
                    targets.append(max(0, cnt - LOOKAHEAD))
                else:
                    cnt += 1
            woven = []
            ti = 0
            for pos, f in enumerate(rest):
                while ti < len(targets) and targets[ti] <= pos:
                    woven.append(loads[ti])
                    ti += 1
                woven.append(f)
            woven.extend(loads[ti:])
            ffn_work = woven

            n_ffn = len(ffn_work)
            ffn_i = [0]

            def drain_ffn(k):
                while ffn_i[0] < n_ffn and k > 0:
                    ffn_work[ffn_i[0]]()
                    ffn_i[0] += 1
                    k -= 1

            for h in range(H):
                for qtl in range(4):
                    s = 4 * h + qtl
                    kext = 512 * J + 128 * (qtl + 1)
                    nchunks = (kext + 511) // 512
                    if nchunks < 4:
                        if nchunks == 1:
                            base = s % 4
                        elif nchunks == 2:
                            base = 2 * (s % 3)
                        else:
                            base = 3 * (s % 2)
                        bks = PSK[base:base + nchunks]
                        for chn in range(nchunks):
                            N = min(512, kext - 512 * chn)
                            A("T", lambda e, h=h, qtl=qtl, chn=chn, N=N, base=base: e.matmul(
                                ps[:, 512 * (base + chn): 512 * (base + chn) + N],
                                qt_ap[0:64, 512 * h + 128 * qtl: 512 * h + 128 * qtl + 128],
                                kt_ap[0:64, T * h + 512 * chn: T * h + 512 * chn + N],
                                start=True, stop=True),
                              reads=QD + KD, writes=[bks[chn]])
                        A("V", lambda e, kext=kext, base=base: e.tensor_tensor(
                            out=ps[:, 512 * base + kext - 128: 512 * base + kext],
                            in0=ps[:, 512 * base + kext - 128: 512 * base + kext],
                            in1=maskadd[:], op=OP.add),
                          reads=["maskadd"] + bks[:nchunks], writes=[bks[nchunks - 1]])
                        A("S", lambda e, kext=kext, h=h, qtl=qtl, base=base: e.activation(
                            attn[:][:, 0:kext], ps[:, 512 * base: 512 * base + kext],
                            AF.Exp, scale=1.0 / 32.0,
                            accum_out=sgm[:, 4 * h + qtl: 4 * h + qtl + 1]),
                          reads=bks[:nchunks] + ["consts"],
                          writes=[f"attn{i}" for i in range(nchunks)] + ["sgm"])
                    else:
                        pL = 2 * (s % 3)
                        pH = 2 * ((s + 1) % 3)
                        bksH = PSK[pH:pH + 2]
                        bksL = PSK[pL:pL + 2]
                        for chn in (2, 3):
                            N = min(512, kext - 512 * chn)
                            A("T", lambda e, h=h, qtl=qtl, chn=chn, N=N, pH=pH: e.matmul(
                                ps[:, 512 * (pH + chn - 2): 512 * (pH + chn - 2) + N],
                                qt_ap[0:64, 512 * h + 128 * qtl: 512 * h + 128 * qtl + 128],
                                kt_ap[0:64, T * h + 512 * chn: T * h + 512 * chn + N],
                                start=True, stop=True),
                              reads=QD + KD, writes=[PSK[pH + chn - 2]])
                        A("V", lambda e, kext=kext, pH=pH: e.tensor_tensor(
                            out=ps[:, 512 * pH + kext - 1152: 512 * pH + kext - 1024],
                            in0=ps[:, 512 * pH + kext - 1152: 512 * pH + kext - 1024],
                            in1=maskadd[:], op=OP.add),
                          reads=["maskadd"] + bksH, writes=[PSK[pH + 1]])
                        A("S", lambda e, kext=kext, h=h, qtl=qtl, pH=pH: e.activation(
                            attn[:][:, 0:kext - 1024],
                            ps[:, 512 * pH: 512 * pH + kext - 1024],
                            AF.Exp, scale=1.0 / 32.0,
                            accum_out=lnout[:, 4 * h + qtl: 4 * h + qtl + 1]),
                          reads=bksH + ["consts"],
                          writes=["attn0", "attn1", "lnout"])
                        for chn in (0, 1):
                            A("T", lambda e, h=h, qtl=qtl, chn=chn, pL=pL: e.matmul(
                                ps[:, 512 * (pL + chn): 512 * (pL + chn) + 512],
                                qt_ap[0:64, 512 * h + 128 * qtl: 512 * h + 128 * qtl + 128],
                                kt_ap[0:64, T * h + 512 * chn: T * h + 512 * chn + 512],
                                start=True, stop=True),
                              reads=QD + KD, writes=[PSK[pL + chn]])
                        A("S", lambda e, h=h, qtl=qtl, pL=pL: e.activation(
                            attn[:][:, 0:1024], ps[:, 512 * pL: 512 * pL + 1024],
                            AF.Exp, scale=1.0 / 32.0,
                            accum_out=sgm[:, 4 * h + qtl: 4 * h + qtl + 1]),
                          reads=bksL + ["consts"],
                          writes=["attn0", "attn1", "sgm"])
                    drain_ffn(5)
            drain_ffn(n_ffn)
            if ln_phases:
                ln_phases[0]()
                ln_phases[1]()
                ln_phases = ln_phases[2:]
                A("V", lambda e: e.tensor_copy(xres[:], outb[:]),
                  reads=["outb"], writes=["xres"])
            if J == NJ - 1:
                A("V", lambda e: e.tensor_tensor(
                    out=sgm[:], in0=sgm[:], in1=lnout[:], op=OP.add),
                  reads=["sgm", "lnout"], writes=["sgm"])
            A("S", lambda e: e.activation(lnout[:], sgm[:], AF.Ln),
              reads=["sgm", "consts"], writes=["lnout"])
            A("V", lambda e: e.tensor_scalar(out=b2bf[:], in0=lnout[:], scalar1=-32.0,
                                             scalar2=120.0, op0=OP.mult, op1=OP.add),
              reads=["lnout"], writes=["b2bf"])
            A("T", lambda e: e.matmul(ps[0:64, 0:128], b2bf[:], i128[:],
                                      start=True, stop=True),
              reads=["b2bf", "i128"], writes=["ps0"])
            A("V", lambda e: e.tensor_copy(augs[:], ps[0:64, 0:128]),
              reads=["ps0"], writes=["augs"])
            A("G", lambda e: e.dma_start(out=aug_h, in_=augs[:]),
              reads=["augs"], writes=["augh"], dma_sem="db1")
            A("Y", lambda e: e.dma_start(
                out=qt_ap[64:65, :],
                in_=aug_h.rearrange("c q -> (c q)").unsqueeze(0)),
              reads=["augh"], writes=["qaug"], dma_sem="db2")

            wo_loads = [load_w(g) for g in range(12, 16)]

            def mk_chain(hp2):
                def chain(hp2=hp2):
                    A("S", lambda e: e.activation(sln, srowt[:], AF.Ln),
                      reads=["srow", "consts"], writes=["rsigt"])
                    A("S", lambda e: e.activation(recb[:], sln, AF.Exp, scale=-1.0),
                      reads=["rsigt", "consts"], writes=["recb"])
                    for j in range(2):
                        cpair = hp2 - 1 + j
                        co = 512 * j
                        A("T", lambda e, co=co: e.matmul(
                            bank(5), indic[0:1, :], recb[0:1, co:co + 512],
                            start=True, stop=False),
                          reads=["recb", "indic"], writes=["ps5"])
                        A("T", lambda e, co=co: e.matmul(
                            bank(5), indic[32:33, :], recb[32:33, co:co + 512],
                            start=False, stop=True),
                          reads=["recb", "indic"], writes=["ps5"])
                        A("V", lambda e, cpair=cpair: e.tensor_tensor(
                            out=attn3[:, cpair, :], in0=attn3[:, cpair, :],
                            in1=bank(5), op=OP.mult),
                          reads=["ps5", f"attn{cpair}"], writes=[f"attn{cpair}"])
                return chain

            pending_chain = [None]
            for hp2 in range(CH):
                nkt = 4 * J + 4
                hs = (2 * hp2, 2 * hp2 + 1)
                sb_of = (0, 6)
                def emit_pv(kti):
                    qs = max(0, 128 * (kti - 4 * J))
                    for j in range(2):
                        h = hs[j]
                        pb = 2 * j + kti % 2
                        A("T", lambda e, h=h, kti=kti, qs=qs, pb=pb, j=j, nkt=nkt: e.matmul(
                            ps[0:65, 512 * (2 + j) + qs: 512 * (2 + j) + 512],
                            vt_ap[:, 65 * H * kti + 65 * h: 65 * H * kti + 65 * h + 65],
                            p2[pb][:, qs:512],
                            start=(kti == 0), stop=(kti == nkt - 1)),
                          reads=[f"p2_{pb}", "vtok", "vtones"], writes=[PSK[2 + j]])

                for kti in range(nkt):
                    qs = max(0, 128 * (kti - 4 * J))
                    for j in range(2):
                        h = hs[j]
                        sbk = sb_of[j] + kti % 2
                        skey = PSK[sbk]
                        A("T", lambda e, h=h, kti=kti, qs=qs, sbk=sbk: e.matmul(
                            ps[:, 512 * sbk + qs: 512 * sbk + 512],
                            kt_ap[0:65, T * h + 128 * kti: T * h + 128 * kti + 128],
                            qt_ap[0:65, 512 * h + qs: 512 * h + 512],
                            start=True, stop=True),
                          reads=KD + QD + ["qaug", "ktones"], writes=[skey])
                    for j in range(2):
                        h = hs[j]
                        sbk = sb_of[j] + kti % 2
                        pb = 2 * j + kti % 2
                        A("S", lambda e, kti=kti, qs=qs, sbk=sbk, pb=pb: e.activation(
                            p2[pb][:, qs:512], ps[:, 512 * sbk + qs: 512 * sbk + 512],
                            AF.Exp, scale=0.125),
                          reads=[PSK[sbk], "consts"], writes=[f"p2_{pb}"])
                    if kti >= 4 * J:
                        for j in range(2):
                            pb = 2 * j + kti % 2
                            A("V", lambda e, kti=kti, qs=qs, pb=pb: e.tensor_tensor(
                                out=p2[pb][:, qs:qs + 128],
                                in0=p2[pb][:, qs:qs + 128],
                                in1=tri01[:], op=OP.mult),
                              reads=["tri01", f"p2_{pb}"], writes=[f"p2_{pb}"])
                    if kti > 0:
                        emit_pv(kti - 1)
                    if kti == 2 and pending_chain[0] is not None:
                        pending_chain[0]()
                        pending_chain[0] = None
                emit_pv(nkt - 1)
                for j in range(2):
                    h = hs[j]
                    pvb = 2 + j
                    pvk = PSK[pvb]
                    A("V", lambda e, h=h, pvb=pvb: e.tensor_copy(
                        attn[:][64 * (h % 2): 64 * (h % 2) + 64,
                                512 * (h // 2): 512 * (h // 2) + 512],
                        ps[0:64, 512 * pvb: 512 * pvb + 512]),
                      reads=[pvk], writes=[f"attn{hp2}"])
                    A("V", lambda e, h=h, pvb=pvb: e.tensor_copy(
                        srowt[32 * (h % 2):32 * (h % 2) + 1][:, 512 * ((h // 2) % 2):512 * ((h // 2) % 2) + 512],
                        ps[64:65, 512 * pvb: 512 * pvb + 512]),
                      reads=[pvk], writes=["srow"])
                if hp2 % 2 == 1:
                    pending_chain[0] = mk_chain(hp2)
                if hp2 < len(ln_phases):
                    ln_phases[hp2]()

            def mk_wo(wo_loads=wo_loads, seq=seq, J=J, chain=pending_chain[0]):
                chain_h = [chain]

                def wo_emit():
                    for g in range(12, 16):
                        wst, slot, wkey = wo_loads[g - 12]
                        for fc2 in range(2):
                            fco = (g - 12) * 2 + fc2
                            bk = fco % 2
                            bkey = PSK[bk]
                            for dc in range(CH):
                                if dc == 6 and chain_h[0] is not None:
                                    chain_h[0]()
                                    chain_h[0] = None
                                A("T", lambda e, slot=slot, fc2=fc2, dc=dc, bk=bk: e.matmul(
                                    bank(bk), wb3[slot][:, dc, 128 * fc2:128 * fc2 + 128],
                                    attn3[:, dc, :],
                                    start=(dc == 0), stop=(dc == CH - 1)),
                                  reads=[wkey, f"attn{dc}"],
                                  writes=[bkey] if dc in (0, CH - 1) else [], deps=[wst])
                            A("V", lambda e, fco=fco, bk=bk: e.tensor_tensor(
                                out=outb3[:, fco, :], in0=bank(bk), in1=xrk3[:, fco, :],
                                op=OP.add),
                              reads=[bkey, "xrk"], writes=["outb"])
                    A("G", lambda e, seq=seq, J=J: e.dma_start(
                        out=y_h[seq, J], in_=outb[:]),
                      reads=["outb"], dma_sem="dy")
                return wo_emit

            pending_wo[0] = mk_wo()
            pending_chain[0] = None
            if is_last:
                pending_wo[0]()
                pending_wo[0] = None
            first_pass[0] = False

    last_y = [s for s in P.steps if s.sem == "dy"][-1]
    A("Y", lambda e: e.nop(), deps=[last_y], dma_sem=None)
    P.steps[-1].sem = "fin"
    P.steps[-1].val = 1
    A("G", lambda e: e.nop(), deps=[last_y])
    P.steps[-1].sem = "fin"
    P.steps[-1].val = 2

    with es:
        with nc.Block() as block:
            P.emit(block, semh)
    return nc


def _deint_perm():
    perm = np.arange(3 * C)
    for blk in range(2):
        for h in range(H):
            base = blk * C + h * DH
            idx = np.concatenate([np.arange(0, DH, 2), np.arange(1, DH, 2)])
            perm[base: base + DH] = base + idx
    return perm


def _host_inputs(inputs):
    x = np.ascontiguousarray(np.asarray(inputs["L0_attn_x"], dtype=np.float32))
    wkqv = np.asarray(inputs["L0_attn_W_kqv"], dtype=np.float32)
    wo = np.asarray(inputs["L0_attn_W_o"], dtype=np.float32)
    wr = np.asarray(inputs["L0_ffn_W_r"], dtype=np.float32)
    wk = np.asarray(inputs["L0_ffn_W_k"], dtype=np.float32)
    wv = np.asarray(inputs["L0_ffn_W_v"], dtype=np.float32)
    mu_r = np.asarray(inputs["L0_ffn_mu_r"], dtype=np.float32)
    mu_k = np.asarray(inputs["L0_ffn_mu_k"], dtype=np.float32)

    wkqvt = np.ascontiguousarray(np.ascontiguousarray(wkqv.T)[:, _deint_perm()])
    wot = np.ascontiguousarray(wo.T)
    wrt = np.ascontiguousarray(wr.T)
    wkt = np.ascontiguousarray(wk.T)
    wvt = np.ascontiguousarray(wv.T)
    mu2 = np.ascontiguousarray(np.concatenate(
        [mu_r.reshape(CH, 128).T, mu_k.reshape(CH, 128).T], axis=1))

    pos = np.arange(T, dtype=np.float64)
    inv = THETA ** (-(np.arange(32, dtype=np.float64) / 32.0))
    ang = pos[None, :] * inv[:, None]
    cos32 = np.cos(ang)
    sin32 = np.sin(ang)
    cmap = np.concatenate([cos32, cos32, cos32, cos32], axis=0)
    smap = np.concatenate([sin32, -sin32, sin32, -sin32], axis=0)
    maps = np.ascontiguousarray(
        np.concatenate([cmap, smap], axis=1).astype(np.float32))

    in_maps = []
    for c in range(NCORES):
        xb = x[BSEQ * c: BSEQ * (c + 1)]
        xc = np.ascontiguousarray(
            xb.reshape(BSEQ, NJ, 512, CH, 128).transpose(0, 1, 4, 3, 2)
        ).reshape(BSEQ, NJ, 128, CH * 512)
        in_maps.append({
            "x": xc, "wkqvt": wkqvt, "wrt": wrt, "wkt": wkt, "wvt": wvt,
            "wot": wot, "mu2": mu2, "ropemaps": maps,
        })
    return in_maps


def kernel(**inputs):
    in_maps = _host_inputs(inputs)
    nc = build_program()
    res = run_bass_kernel_spmd(nc, in_maps, core_ids=list(range(NCORES)))
    outs = []
    for c in range(NCORES):
        y = np.asarray(res.results[c]["y"]).reshape(BSEQ, NJ, 128, CH, 512)
        outs.append(y.transpose(0, 1, 4, 3, 2).reshape(BSEQ, T, C))
    return np.ascontiguousarray(np.concatenate(outs, axis=0), dtype=np.float32)


if __name__ == "__main__":
    nc = build_program()
    print("built ok, instructions:", sum(1 for _ in nc.all_instructions()))


# version 37
# speedup vs baseline: 1.0779x; 1.0273x over previous
import numpy as np
import concourse.bass as bass
import concourse.mybir as mybir
from concourse.bass_utils import run_bass_kernel_spmd
from contextlib import ExitStack

F32 = mybir.dt.float32
BF16 = mybir.dt.bfloat16
AF = mybir.ActivationFunctionType
OP = mybir.AluOpType

NCORES = 8
BSEQ = 2
T = 2048
C = 1024
H = 16
DH = 64
CH = C // 128
NJ = T // 512
KTT = T // 128
THETA = 10000.0


class Step:
    __slots__ = ("eng", "fn", "sem", "val", "deps", "snap", "inc", "waits",
                 "inc_on", "force_prev")

    def __init__(self, eng, fn, sem, val, deps, inc):
        self.eng, self.fn, self.sem, self.val = eng, fn, sem, val
        self.deps, self.inc = deps, inc
        self.snap = None
        self.waits = ()
        self.force_prev = None


class Prog:
    ENGS = ("Y", "G", "T", "V", "S")
    NOCHAIN = ("dka", "dkb", "dqa", "dqb")

    def __init__(self):
        self.steps = []
        self.sem_cnt = {}
        self.last_w = {}
        self.last_r = {}
        self.last_dma = {}

    def add(self, eng, fn, reads=(), writes=(), deps=(), dma_sem=None):
        is_dma = dma_sem is not None
        dl = [d for d in deps if d is not None]
        for k in reads:
            w = self.last_w.get(k)
            if w is not None:
                dl.append(w)
        for k in writes:
            w = self.last_w.get(k)
            if w is not None:
                dl.append(w)
            dl.extend(self.last_r.get(k, ()))
        if dma_sem is not None:
            sem, inc = dma_sem, 16
        else:
            sem, inc = "e" + eng, 1
        s = Step(eng, fn, sem, 0, dl, inc)
        if dma_sem is not None and dma_sem not in self.NOCHAIN:
            prev = self.last_dma.get(dma_sem)
            if prev is not None:
                dl.append(prev)
                s.force_prev = prev
            self.last_dma[dma_sem] = s
        self.steps.append(s)
        for k in reads:
            self.last_r.setdefault(k, []).append(s)
        for k in writes:
            self.last_w[k] = s
            self.last_r[k] = []
        return s

    def plan(self):
        for s in self.steps:
            s.inc_on = (s.inc == 16)
        for s in self.steps:
            for d in s.deps:
                d.inc_on = True
        self.sem_cnt = {}
        for s in self.steps:
            if s.inc_on:
                v = self.sem_cnt.get(s.sem, 0) + s.inc
                self.sem_cnt[s.sem] = v
                s.val = v
        know = {e: {} for e in self.ENGS}
        for s in self.steps:
            k = know[s.eng]
            wd = {}
            for d in s.deps:
                if k.get(d.sem, 0) < d.val:
                    wd[d.sem] = max(wd.get(d.sem, 0), d.val)
                    k[d.sem] = d.val
                    if d.snap:
                        for sm, v in d.snap.items():
                            if k.get(sm, 0) < v:
                                k[sm] = v
            if s.force_prev is not None and s.force_prev.val > 0:
                fp = s.force_prev
                if wd.get(fp.sem, 0) < fp.val:
                    wd[fp.sem] = fp.val
                    if k.get(fp.sem, 0) < fp.val:
                        k[fp.sem] = fp.val
            s.waits = sorted(wd.items())
            snap = dict(k)
            snap[s.sem] = max(snap.get(s.sem, 0), s.val)
            s.snap = snap

    def emit(self, block, handles):
        self.plan()
        by_eng = {e: [s for s in self.steps if s.eng == e] for e in self.ENGS}

        def runner(ename):
            def run(eng):
                for s in by_eng[ename]:
                    for sm, v in s.waits:
                        eng.wait_ge(handles[sm], v)
                    inst = s.fn(eng)
                    if s.inc_on:
                        inst.then_inc(handles[s.sem], s.inc)
            return run

        block.sync(runner("Y"))
        block.gpsimd(runner("G"))
        block.tensor(runner("T"))
        block.vector(runner("V"))
        block.scalar(runner("S"))


def build_program():
    nc = bass.Bass("TRN2", target_bir_lowering=False)
    P = Prog()
    A = P.add

    x_h = nc.dram_tensor("x", [BSEQ, NJ, 128, CH * 512], F32, kind="ExternalInput").ap()
    wkqv_h = nc.dram_tensor("wkqvt", [C, 3 * C], F32, kind="ExternalInput").ap()
    wo_h = nc.dram_tensor("wot", [C, C], F32, kind="ExternalInput").ap()
    wr_h = nc.dram_tensor("wrt", [C, C], F32, kind="ExternalInput").ap()
    wk_h = nc.dram_tensor("wkt", [C, C], F32, kind="ExternalInput").ap()
    wv_h = nc.dram_tensor("wvt", [C, C], F32, kind="ExternalInput").ap()
    mu_h = nc.dram_tensor("mu2", [128, 2 * CH], F32, kind="ExternalInput").ap()
    maps_h = nc.dram_tensor("ropemaps", [128, 2 * T], F32, kind="ExternalInput").ap()
    y_h = nc.dram_tensor("y", [BSEQ, NJ, 128, CH * 512], F32, kind="ExternalOutput").ap()
    NGRP = 28
    wtmp_h = nc.dram_tensor("wtmp", [NGRP, 128, 2048], BF16, kind="Internal").ap()
    aug_h = nc.dram_tensor("augh", [64, 128], BF16, kind="Internal").ap()

    es = ExitStack()
    sbuf = lambda name, shape, dt: es.enter_context(nc.sbuf_tensor(name, shape, dt))

    NSLOT = 5
    wbuf = [sbuf(f"wbuf{i}", [128, 2048], BF16) for i in range(NSLOT)]
    kt = sbuf("ktil", [65, H * T], BF16)
    vtok = sbuf("vtok", [128, KTT * H * 65], BF16)
    qt = sbuf("qtil", [65, H * 512], BF16)
    xres = sbuf("xres", [128, CH * 512], BF16)
    outb = sbuf("outb", [128, CH * 512], BF16)
    xn = sbuf("xn", [128, CH * 513], BF16)
    call = sbuf("call", [128, CH * 512], BF16)
    attn = sbuf("attn", [128, CH * 512], BF16)
    p2 = [sbuf(f"p2_{i}", [128, 512], BF16) for i in range(4)]
    rsig = sbuf("rsigt", [128, CH * 512], BF16)
    xrk = sbuf("xrk", [128, CH * 512], BF16)
    recb = sbuf("recb", [33, 1024], BF16)
    srowt = sbuf("srowt", [33, 1024], F32)
    sgm = sbuf("sgm", [128, 64], F32)
    lnout = sbuf("lnout", [128, 64], F32)
    b2bf = sbuf("b2bf", [128, 64], BF16)
    mu2 = sbuf("mu2s", [128, 2 * CH], F32)
    r_s = sbuf("r_s", [1, 512], F32)
    s2_s = sbuf("s2_s", [1, 512], F32)
    maskadd = sbuf("maskadd", [128, 128], F32)
    tri01 = sbuf("tri01", [128, 128], BF16)
    i128 = sbuf("i128", [128, 128], BF16)
    indic = sbuf("indic", [33, 128], BF16)
    ones128 = sbuf("ones128", [128, 1], BF16)
    onesf1 = sbuf("onesf1", [1, 128], F32)
    consts = sbuf("consts", [128, 8], F32)
    xtail = sbuf("xtail", [128, CH], BF16)
    augs = sbuf("augs", [64, 128], BF16)

    ps_t = es.enter_context(nc.psum_tensor("ps", [128, 4096], F32))
    ps = ps_t.ap() if hasattr(ps_t, "ap") else ps_t[:]

    def bank(b):
        return ps[:, 512 * b: 512 * b + 512]


    sem_names = ["eY", "eG", "eT", "eV", "eS", "fin",
                 "dwl0", "dwl1", "dwl2", "dwl3", "dwl4", "dws0", "dws1", "dws2", "dws3", "dws4", "dst0", "dst1", "dst2", "dst3", "dst4", "dmu",
                 "dx", "dmap", "dka", "dkb", "dqa", "dqb", "dy", "dy2", "db1", "db2"]
    semh = {n: es.enter_context(nc.semaphore(n)) for n in sem_names}

    CONST_VALS = [0.0, 1.0, 1.0 / 32.0, 0.125, 1e-5, -1.0, 0.5, -0.5]
    for i, v in enumerate(CONST_VALS):
        nc.const_aps.register(F32, v, consts[:, i:i + 1])

    wb3 = [wbuf[i][:].rearrange("p (c f) -> p c f", c=CH) for i in range(NSLOT)]
    vt_ap = vtok[:]
    vt3 = vt_ap.rearrange("p (k h d) -> p k h d", k=KTT, h=H)
    kt_ap = kt[:]
    rsf32 = rsig[:].bitcast(F32)
    mapblk = rsf32[:, 0:1024]
    sln = rsf32[0:33, 1024:2048]
    qt_ap = qt[:]
    call3 = call[:].rearrange("p (c t) -> p c t", c=CH)
    xn_ap = xn[:]
    xn3 = xn_ap.rearrange("p (c t) -> p c t", c=CH)
    attn3 = attn[:].rearrange("p (c t) -> p c t", c=CH)
    pack3 = xrk[:].rearrange("p (c t) -> p c t", c=CH)
    rsig3 = rsig[:].rearrange("p (c t) -> p c t", c=CH)
    xrk3 = xrk[:].rearrange("p (c t) -> p c t", c=CH)
    xres3 = xres[:].rearrange("p (c t) -> p c t", c=CH)
    outb3 = outb[:].rearrange("p (c t) -> p c t", c=CH)

    for i, v in enumerate(CONST_VALS):
        A("V", lambda e, i=i, v=v: e.memset(consts[:, i:i + 1], v), writes=["consts"])
    A("V", lambda e: e.memset(ones128[:], 1.0), writes=["ones128"])
    A("V", lambda e: e.memset(onesf1[:], 1.0), writes=["onesf1"])
    A("V", lambda e: e.memset(kt[64:65, :], 1.0), writes=["ktones"])
    A("V", lambda e: e.memset(indic[0:1, 0:64], 1.0), writes=["indic"])
    A("V", lambda e: e.memset(indic[0:1, 64:128], 0.0), writes=["indic"])
    A("V", lambda e: e.memset(indic[32:33, 0:64], 0.0), writes=["indic"])
    A("V", lambda e: e.memset(indic[32:33, 64:128], 1.0), writes=["indic"])
    A("V", lambda e: e.memset(xtail[:], 0.0), writes=["xtail"])
    A("V", lambda e: e.memset(srowt[:], 1.0), writes=["srow"])
    A("V", lambda e: e.memset(vt3[:, :, :, 64:65], 1.0), writes=["vtones"])
    A("V", lambda e: e.memset(maskadd[:], 0.0), writes=["maskadd"])
    A("G", lambda e: e.affine_select(out=maskadd[:], in_=maskadd[:],
                                     pattern=[[-1, 128]], compare_op=OP.is_ge,
                                     fill=-1.0e4, base=0, channel_multiplier=1),
      reads=["maskadd"], writes=["maskadd"])
    A("V", lambda e: e.memset(tri01[:], 1.0), writes=["tri01"])
    A("G", lambda e: e.affine_select(out=tri01[:], in_=tri01[:],
                                     pattern=[[1, 128]], compare_op=OP.is_ge,
                                     fill=0.0, base=0, channel_multiplier=-1),
      reads=["tri01"], writes=["tri01"])
    A("V", lambda e: e.memset(i128[:], 1.0), writes=["i128"])
    A("G", lambda e: e.affine_select(out=i128[:], in_=i128[:],
                                     pattern=[[1, 128]], compare_op=OP.is_ge,
                                     fill=0.0, base=0, channel_multiplier=-1),
      reads=["i128"], writes=["i128"])
    A("G", lambda e: e.affine_select(out=i128[:], in_=i128[:],
                                     pattern=[[-1, 128]], compare_op=OP.is_ge,
                                     fill=0.0, base=0, channel_multiplier=1),
      reads=["i128"], writes=["i128"])
    A("G", lambda e: e.dma_start(out=mu2[:], in_=mu_h), writes=["mu2"], dma_sem="dmu")

    wsrcs = ([(wkqv_h, g) for g in range(12)] + [(wo_h, g) for g in range(4)] +
             [(wr_h, g) for g in range(4)] + [(wk_h, g) for g in range(4)] +
             [(wv_h, g) for g in range(4)])

    wload_i = [0]
    first_pass = [True]

    def load_w(gi):
        slot = wload_i[0] % NSLOT
        wload_i[0] += 1
        key = f"wbuf{slot}"
        if first_pass[0]:
            h, g = wsrcs[gi]
            src = h.rearrange("(c p) f -> p c f", p=128)[:, :, 256 * g: 256 * g + 256]
            st = A("G", lambda e, src=src, slot=slot: e.dma_start(out=wb3[slot], in_=src),
                   writes=[key], dma_sem=f"dwl{slot}")
            A("Y", lambda e, gi=gi, slot=slot: e.dma_start(out=wtmp_h[gi], in_=wbuf[slot][:]),
              reads=[key], writes=[f"wtmp{gi}"], dma_sem=f"dws{slot}")
            return st, slot, key
        st = A("Y", lambda e, gi=gi, slot=slot: e.dma_start(out=wbuf[slot][:], in_=wtmp_h[gi]),
               reads=[f"wtmp{gi}"], writes=[key], dma_sem=f"dst{slot}")
        return st, slot, key

    PSK = [f"ps{i}" for i in range(8)]

    def emit_ln_phases(seq, J):
        first = (seq == 0 and J == 0)
        b4row = ps[0:1, 2048:2560]

        def ph_dma(seq=seq, J=J, first=first):
            dstk = "xres" if first else "outb"
            dst = xres[:] if first else outb[:]
            A("G", lambda e, seq=seq, J=J, dst=dst: e.dma_start(
                out=dst, in_=x_h[seq, J]),
              writes=[dstk], dma_sem="dx")

        def ph_maps(J=J):
            A("Y", lambda e, J=J: e.dma_start(
                out=mapblk.rearrange("p (g t) -> p g t", g=2),
                in_=maps_h.rearrange("p (g t) -> p g t", g=2)[:, :, 512 * J: 512 * J + 512]),
              writes=["mapblk", "rsigt"], dma_sem="dmap")

        def ph0():
            for c in range(CH):
                A("T", lambda e, c=c: e.matmul(b4row, ones128[:], xres3[:, c, :],
                                               start=(c == 0), stop=(c == CH - 1)),
                  reads=["xres", "ones128"],
                  writes=["ps4"] if c in (0, CH - 1) else [])

        def ph1():
            A("V", lambda e: e.tensor_scalar(out=s2_s[:], in0=b4row,
                                             scalar1=1.0 / C, scalar2=None, op0=OP.mult),
              reads=["ps4"], writes=["s2_s"])
            A("V", lambda e: e.tensor_tensor(out=r_s[:], in0=s2_s[:], in1=s2_s[:], op=OP.mult),
              reads=["s2_s"], writes=["r_s"])
            A("V", lambda e: e.tensor_tensor(out=call3, in0=xres3, in1=xres3, op=OP.mult),
              reads=["xres"], writes=["call"])
            for c in range(CH):
                A("T", lambda e, c=c: e.matmul(b4row, ones128[:], call3[:, c, :],
                                               start=(c == 0), stop=(c == CH - 1)),
                  reads=["call", "ones128"],
                  writes=["ps4"] if c in (0, CH - 1) else [])

        def ph2(first=first):
            A("V", lambda e: e.scalar_tensor_tensor(out=r_s[:], in0=b4row,
                                                    scalar=1.0 / C, in1=r_s[:],
                                                    op0=OP.mult, op1=OP.subtract),
              reads=["ps4", "r_s"], writes=["r_s"])
            A("T", lambda e: e.matmul(bank(4), onesf1[:], s2_s[:], start=True, stop=True),
              reads=["s2_s", "onesf1", "ps4"], writes=["ps4"])
            A("S", lambda e: e.activation(s2_s[:], r_s[:], AF.Ln, bias=1e-5, scale=1.0),
              reads=["r_s", "consts"], writes=["s2_s"])
            A("S", lambda e: e.activation(r_s[:], s2_s[:], AF.Exp, scale=-0.5),
              reads=["s2_s", "consts"], writes=["r_s"])
            if not first:
                A("V", lambda e: e.tensor_copy(xtail[:], xn_ap[:, 512::513]),
                  reads=["xn"], writes=["xtail"])

        def ph3():
            A("V", lambda e: e.tensor_tensor(
                out=call3, in0=xres3,
                in1=bank(4).unsqueeze(1).broadcast_to([128, CH, 512]), op=OP.subtract),
              reads=["xres", "ps4", "call"], writes=["call"])

        def ph4(J=J):
            A("T", lambda e: e.matmul(bank(4), onesf1[:], r_s[:], start=True, stop=True),
              reads=["r_s", "onesf1"], writes=["ps4"])
            A("V", lambda e: e.tensor_tensor(
                out=xn3[:, :, 1:513], in0=call3,
                in1=bank(4).unsqueeze(1).broadcast_to([128, CH, 512]), op=OP.mult),
              reads=["call", "ps4"], writes=["xn"])
            if J == 0:
                A("V", lambda e: e.memset(xn_ap[:, 0::513], 0.0), reads=["xn"], writes=["xn"])
            else:
                A("V", lambda e: e.tensor_copy(xn_ap[:, 0::513], xtail[:]),
                  reads=["xtail", "xn"], writes=["xn"])

        return [ph_dma, ph_maps, ph0, ph1, ph2, ph3, ph4]

    def emit_ln(seq, J):
        for ph in emit_ln_phases(seq, J):
            ph()

    emit_ln(0, 0)
    pending_wo = [None]
    for seq in range(BSEQ):
        for J in range(NJ):
            is_last = (seq == BSEQ - 1 and J == NJ - 1)
            if pending_wo[0] is not None:
                pending_wo[0]()
                pending_wo[0] = None
            ln_phases = []
            if not is_last:
                nseq, nJ = (seq, J + 1) if J + 1 < NJ else (seq + 1, 0)
                ln_phases = emit_ln_phases(nseq, nJ)
            for g in range(12):
                wst, slot, wkey = load_w(g)
                if g < 8:
                    part = "K" if g < 4 else "Q"
                    for fc2 in range(2):
                        gch = (g % 4) * 2 + fc2
                        bk = 2 + gch % 2
                        bkey = PSK[bk]
                        for ci in range(CH):
                            A("T", lambda e, slot=slot, fc2=fc2, ci=ci, bk=bk: e.matmul(
                                bank(bk), wb3[slot][:, ci, 128 * fc2:128 * fc2 + 128],
                                xn3[:, ci, 1:513],
                                start=(ci == 0), stop=(ci == CH - 1)),
                              reads=[wkey, "xn"],
                              writes=[bkey] if ci in (0, CH - 1) else [], deps=[wst])
                        if part == "K":
                            key = f"kK{gch}"
                            tgt = kt_ap
                            sA = T * (2 * gch) + 512 * J
                            sB = T * (2 * gch + 1) + 512 * J
                            extra = []
                        else:
                            key = f"kQ{gch}"
                            tgt = qt_ap
                            sA = 512 * (2 * gch)
                            sB = 512 * (2 * gch + 1)
                            extra = [f"kQa{i}" for i in range(4)]
                        A("V", lambda e, gch=gch, bk=bk: e.tensor_tensor(
                            out=call3[:, gch, :], in0=bank(bk), in1=mapblk[:, 512:1024],
                            op=OP.mult),
                          reads=[bkey, "mapblk", "rsigt"], writes=["call"])
                        A("V", lambda e, bk=bk, tgt=tgt, sA=sA: e.tensor_tensor(
                            out=tgt[0:64, sA:sA + 512], in0=ps[0:64, 512 * bk:512 * bk + 512],
                            in1=mapblk[0:64, 0:512], op=OP.mult),
                          reads=[bkey, "mapblk", "rsigt"] + extra, writes=[key])
                        A("V", lambda e, bk=bk, tgt=tgt, sB=sB: e.tensor_tensor(
                            out=tgt[0:64, sB:sB + 512], in0=ps[64:128, 512 * bk:512 * bk + 512],
                            in1=mapblk[0:64, 0:512], op=OP.mult),
                          reads=[bkey, "mapblk", "rsigt"] + extra, writes=[key])
                    if g % 4 == 3:
                        if part == "K":
                            d32 = kt_ap[0:32, :].rearrange("p (h t) -> p h t", h=H)
                            d32b = kt_ap[32:64, :].rearrange("p (h t) -> p h t", h=H)
                            dsts32 = [d32[:, 0::2, 512 * J:512 * J + 512],
                                      d32b[:, 0::2, 512 * J:512 * J + 512],
                                      d32[:, 1::2, 512 * J:512 * J + 512],
                                      d32b[:, 1::2, 512 * J:512 * J + 512]]
                            sb2, kd = "dkb", "kK"
                        else:
                            d32 = qt_ap[0:32, :].rearrange("p (h t) -> p h t", h=H)
                            d32b = qt_ap[32:64, :].rearrange("p (h t) -> p h t", h=H)
                            dsts32 = [d32[:, 0::2, :], d32b[:, 0::2, :],
                                      d32[:, 1::2, :], d32b[:, 1::2, :]]
                            sb2, kd = "dqb", "kQ"
                        srcs = [call3[32:64, :, :], call3[0:32, :, :],
                                call3[96:128, :, :], call3[64:96, :, :]]
                        allk = [f"{kd}{c}" for c in range(CH)]
                        for i in range(4):
                            A("G", lambda e, i=i, dsts32=dsts32, srcs=srcs: e.dma_start(
                                out=dsts32[i], in_=srcs[i], accum_op=OP.add),
                              reads=["call"] + allk, writes=[f"{kd}a{i}"], dma_sem=sb2)
                else:
                    for tt in range(4):
                        bk = 2 + tt % 2
                        bkey = PSK[bk]
                        for ci in range(CH):
                            A("T", lambda e, slot=slot, tt=tt, ci=ci, bk=bk: e.matmul(
                                ps[:, 512 * bk: 512 * bk + 256],
                                xn3[:, ci, 1 + 128 * tt:1 + 128 * tt + 128],
                                wb3[slot][:, ci, :],
                                start=(ci == 0), stop=(ci == CH - 1)),
                              reads=[wkey, "xn"],
                              writes=[bkey] if ci in (0, CH - 1) else [], deps=[wst])
                        ktile = 4 * J + tt
                        h0 = (g - 8) * 4
                        A("V", lambda e, ktile=ktile, h0=h0, bk=bk: e.tensor_copy(
                            vt3[:, ktile:ktile + 1, h0:h0 + 4, 0:64],
                            ps[:, 512 * bk: 512 * bk + 256].rearrange(
                                "p (o h d) -> p o h d", o=1, h=4)),
                          reads=[bkey], writes=["vtok"])

            KD = [f"kK{c}" for c in range(CH)] + [f"kKa{i}" for i in range(4)]
            QD = [f"kQ{c}" for c in range(CH)] + [f"kQa{i}" for i in range(4)]

            ffn_work = []
            ffn_loadidx = []
            W = ffn_work.append
            W(lambda: A("V", lambda e: e.tensor_tensor(
                out=call3, in0=xn3[:, :, 1:513], in1=xn3[:, :, 0:512], op=OP.subtract),
                reads=["xn"], writes=["call"]))
            for c in range(CH):
                W(lambda c=c: A("V", lambda e, c=c: e.scalar_tensor_tensor(
                    out=xrk3[:, c, :], in0=call3[:, c, :], scalar=mu2[:, c:c + 1],
                    in1=xn3[:, c, 0:512], op0=OP.mult, op1=OP.add),
                    reads=["call", "xn", "mu2"], writes=["xrk"]))

            def ffn_proj(g0, rhs3, rhs_key, epi):
                for g in range(g0, g0 + 4):
                    holder = {}
                    ffn_loadidx.append(len(ffn_work))
                    W(lambda holder=holder, g=g: holder.update(zip(("st", "slot", "key"), load_w(g))))
                    for fc2 in range(2):
                        fco = (g - g0) * 2 + fc2
                        bk = 6 + fco % 2
                        bkey = PSK[bk]
                        for ci in range(CH):
                            W(lambda holder=holder, fc2=fc2, ci=ci, bk=bk, bkey=bkey,
                              rhs3=rhs3, rhs_key=rhs_key: A(
                                "T", lambda e, holder=holder, fc2=fc2, ci=ci, bk=bk, rhs3=rhs3: e.matmul(
                                    bank(bk), wb3[holder["slot"]][:, ci, 128 * fc2:128 * fc2 + 128],
                                    rhs3[:, ci, :],
                                    start=(ci == 0), stop=(ci == CH - 1)),
                                reads=[holder["key"], rhs_key],
                                writes=[bkey] if ci in (0, CH - 1) else [],
                                deps=[holder["st"]]))
                        W(lambda fco=fco, bk=bk, bkey=bkey, epi=epi: epi(fco, bk, bkey))

            def sig_epi(fco, bk, bkey):
                A("S", lambda e, bk=bk: e.activation(
                    bank(bk), bank(bk), AF.Tanh, scale=0.5),
                  reads=[bkey, "consts"], writes=[bkey])
                return A("V", lambda e, fco=fco, bk=bk: e.tensor_scalar(
                    out=rsig3[:, fco, :], in0=bank(bk),
                    scalar1=0.5, scalar2=0.5, op0=OP.mult, op1=OP.add),
                  reads=[bkey], writes=["rsigt"])

            def relu_epi(fco, bk, bkey):
                A("V", lambda e, fco=fco, bk=bk: e.tensor_scalar(
                    out=call3[:, fco, :], in0=bank(bk),
                    scalar1=0.0, scalar2=None, op0=OP.max),
                  reads=[bkey], writes=["call"])
                return A("V", lambda e, fco=fco: e.tensor_tensor(
                    out=call3[:, fco, :], in0=call3[:, fco, :],
                    in1=call3[:, fco, :], op=OP.mult),
                  reads=["call"], writes=["call"])

            def wv_epi(fco, bk, bkey):
                return A("V", lambda e, fco=fco, bk=bk: e.tensor_tensor(
                    out=xrk3[:, fco, :], in0=bank(bk), in1=rsig3[:, fco, :],
                    op=OP.mult),
                  reads=[bkey, "rsigt"], writes=["xrk"])

            ffn_proj(16, xrk3, "xrk", sig_epi)
            for c in range(CH):
                W(lambda c=c: A("V", lambda e, c=c: e.scalar_tensor_tensor(
                    out=xrk3[:, c, :], in0=call3[:, c, :],
                    scalar=mu2[:, CH + c:CH + c + 1],
                    in1=xn3[:, c, 0:512], op0=OP.mult, op1=OP.add),
                    reads=["call", "xn", "mu2"], writes=["xrk"]))
            ffn_proj(20, xrk3, "xrk", relu_epi)
            ffn_proj(24, call3, "call", wv_epi)
            W(lambda: A("V", lambda e: e.tensor_tensor(
                out=xrk3, in0=xrk3, in1=xres3, op=OP.add),
                reads=["xres", "xrk"], writes=["xrk"]))
            LOOKAHEAD = 30
            marked = set(ffn_loadidx)
            loads = [ffn_work[i] for i in sorted(marked)]
            rest = [f for i, f in enumerate(ffn_work) if i not in marked]
            targets = []
            cnt = 0
            for i in range(len(ffn_work)):
                if i in marked:
                    targets.append(max(0, cnt - LOOKAHEAD))
                else:
                    cnt += 1
            woven = []
            ti = 0
            for pos, f in enumerate(rest):
                while ti < len(targets) and targets[ti] <= pos:
                    woven.append(loads[ti])
                    ti += 1
                woven.append(f)
            woven.extend(loads[ti:])
            ffn_work = woven

            n_ffn = len(ffn_work)
            ffn_i = [0]

            def drain_ffn(k):
                while ffn_i[0] < n_ffn and k > 0:
                    ffn_work[ffn_i[0]]()
                    ffn_i[0] += 1
                    k -= 1

            for h in range(H):
                for qtl in range(4):
                    s = 4 * h + qtl
                    kext = 512 * J + 128 * (qtl + 1)
                    nchunks = (kext + 511) // 512
                    if nchunks < 4:
                        if nchunks == 1:
                            base = s % 4
                        elif nchunks == 2:
                            base = 2 * (s % 3)
                        else:
                            base = 3 * (s % 2)
                        bks = PSK[base:base + nchunks]
                        for chn in range(nchunks):
                            N = min(512, kext - 512 * chn)
                            A("T", lambda e, h=h, qtl=qtl, chn=chn, N=N, base=base: e.matmul(
                                ps[:, 512 * (base + chn): 512 * (base + chn) + N],
                                qt_ap[0:64, 512 * h + 128 * qtl: 512 * h + 128 * qtl + 128],
                                kt_ap[0:64, T * h + 512 * chn: T * h + 512 * chn + N],
                                start=True, stop=True),
                              reads=QD + KD, writes=[bks[chn]])
                        A("V", lambda e, kext=kext, base=base: e.tensor_tensor(
                            out=ps[:, 512 * base + kext - 128: 512 * base + kext],
                            in0=ps[:, 512 * base + kext - 128: 512 * base + kext],
                            in1=maskadd[:], op=OP.add),
                          reads=["maskadd"] + bks[:nchunks], writes=[bks[nchunks - 1]])
                        A("S", lambda e, kext=kext, h=h, qtl=qtl, base=base: e.activation(
                            attn[:][:, 0:kext], ps[:, 512 * base: 512 * base + kext],
                            AF.Exp, scale=1.0 / 32.0,
                            accum_out=sgm[:, 4 * h + qtl: 4 * h + qtl + 1]),
                          reads=bks[:nchunks] + ["consts"],
                          writes=[f"attn{i}" for i in range(nchunks)] + ["sgm"])
                    else:
                        pL = 2 * (s % 3)
                        pH = 2 * ((s + 1) % 3)
                        bksH = PSK[pH:pH + 2]
                        bksL = PSK[pL:pL + 2]
                        for chn in (2, 3):
                            N = min(512, kext - 512 * chn)
                            A("T", lambda e, h=h, qtl=qtl, chn=chn, N=N, pH=pH: e.matmul(
                                ps[:, 512 * (pH + chn - 2): 512 * (pH + chn - 2) + N],
                                qt_ap[0:64, 512 * h + 128 * qtl: 512 * h + 128 * qtl + 128],
                                kt_ap[0:64, T * h + 512 * chn: T * h + 512 * chn + N],
                                start=True, stop=True),
                              reads=QD + KD, writes=[PSK[pH + chn - 2]])
                        A("V", lambda e, kext=kext, pH=pH: e.tensor_tensor(
                            out=ps[:, 512 * pH + kext - 1152: 512 * pH + kext - 1024],
                            in0=ps[:, 512 * pH + kext - 1152: 512 * pH + kext - 1024],
                            in1=maskadd[:], op=OP.add),
                          reads=["maskadd"] + bksH, writes=[PSK[pH + 1]])
                        A("S", lambda e, kext=kext, h=h, qtl=qtl, pH=pH: e.activation(
                            attn[:][:, 0:kext - 1024],
                            ps[:, 512 * pH: 512 * pH + kext - 1024],
                            AF.Exp, scale=1.0 / 32.0,
                            accum_out=lnout[:, 4 * h + qtl: 4 * h + qtl + 1]),
                          reads=bksH + ["consts"],
                          writes=["attn0", "attn1", "lnout"])
                        for chn in (0, 1):
                            A("T", lambda e, h=h, qtl=qtl, chn=chn, pL=pL: e.matmul(
                                ps[:, 512 * (pL + chn): 512 * (pL + chn) + 512],
                                qt_ap[0:64, 512 * h + 128 * qtl: 512 * h + 128 * qtl + 128],
                                kt_ap[0:64, T * h + 512 * chn: T * h + 512 * chn + 512],
                                start=True, stop=True),
                              reads=QD + KD, writes=[PSK[pL + chn]])
                        A("S", lambda e, h=h, qtl=qtl, pL=pL: e.activation(
                            attn[:][:, 0:1024], ps[:, 512 * pL: 512 * pL + 1024],
                            AF.Exp, scale=1.0 / 32.0,
                            accum_out=sgm[:, 4 * h + qtl: 4 * h + qtl + 1]),
                          reads=bksL + ["consts"],
                          writes=["attn0", "attn1", "sgm"])
                    drain_ffn(5)
            drain_ffn(n_ffn)
            if ln_phases:
                ln_phases[0]()
                ln_phases[1]()
                ln_phases = ln_phases[2:]
                A("V", lambda e: e.tensor_copy(xres[:], outb[:]),
                  reads=["outb"], writes=["xres"])
            if J == NJ - 1:
                A("V", lambda e: e.tensor_tensor(
                    out=sgm[:], in0=sgm[:], in1=lnout[:], op=OP.add),
                  reads=["sgm", "lnout"], writes=["sgm"])
            A("S", lambda e: e.activation(lnout[:], sgm[:], AF.Ln),
              reads=["sgm", "consts"], writes=["lnout"])
            A("V", lambda e: e.tensor_scalar(out=b2bf[:], in0=lnout[:], scalar1=-32.0,
                                             scalar2=120.0, op0=OP.mult, op1=OP.add),
              reads=["lnout"], writes=["b2bf"])
            A("T", lambda e: e.matmul(ps[0:64, 0:128], b2bf[:], i128[:],
                                      start=True, stop=True),
              reads=["b2bf", "i128"], writes=["ps0"])
            A("V", lambda e: e.tensor_copy(augs[:], ps[0:64, 0:128]),
              reads=["ps0"], writes=["augs"])
            A("G", lambda e: e.dma_start(out=aug_h, in_=augs[:]),
              reads=["augs"], writes=["augh"], dma_sem="db1")
            A("Y", lambda e: e.dma_start(
                out=qt_ap[64:65, :],
                in_=aug_h.rearrange("c q -> (c q)").unsqueeze(0)),
              reads=["augh"], writes=["qaug"], dma_sem="db2")

            wo_loads = [load_w(g) for g in range(12, 16)]

            def mk_chain(hp2):
                def chain(hp2=hp2):
                    A("S", lambda e: e.activation(sln, srowt[:], AF.Ln),
                      reads=["srow", "consts"], writes=["rsigt"])
                    A("S", lambda e: e.activation(recb[:], sln, AF.Exp, scale=-1.0),
                      reads=["rsigt", "consts"], writes=["recb"])
                    for j in range(2):
                        cpair = hp2 - 1 + j
                        co = 512 * j
                        A("T", lambda e, co=co: e.matmul(
                            bank(5), indic[0:1, :], recb[0:1, co:co + 512],
                            start=True, stop=False),
                          reads=["recb", "indic"], writes=["ps5"])
                        A("T", lambda e, co=co: e.matmul(
                            bank(5), indic[32:33, :], recb[32:33, co:co + 512],
                            start=False, stop=True),
                          reads=["recb", "indic"], writes=["ps5"])
                        A("V", lambda e, cpair=cpair: e.tensor_tensor(
                            out=attn3[:, cpair, :], in0=attn3[:, cpair, :],
                            in1=bank(5), op=OP.mult),
                          reads=["ps5", f"attn{cpair}"], writes=[f"attn{cpair}"])
                return chain

            pending_chain = [None]
            for hp2 in range(CH):
                nkt = 4 * J + 4
                hs = (2 * hp2, 2 * hp2 + 1)
                sb_of = (0, 6)
                def emit_pv(kti):
                    qs = max(0, 128 * (kti - 4 * J))
                    for j in range(2):
                        h = hs[j]
                        pb = 2 * j + kti % 2
                        A("T", lambda e, h=h, kti=kti, qs=qs, pb=pb, j=j, nkt=nkt: e.matmul(
                            ps[0:65, 512 * (2 + j) + qs: 512 * (2 + j) + 512],
                            vt_ap[:, 65 * H * kti + 65 * h: 65 * H * kti + 65 * h + 65],
                            p2[pb][:, qs:512],
                            start=(kti == 0), stop=(kti == nkt - 1)),
                          reads=[f"p2_{pb}", "vtok", "vtones"], writes=[PSK[2 + j]])

                for kti in range(nkt):
                    qs = max(0, 128 * (kti - 4 * J))
                    for j in range(2):
                        h = hs[j]
                        sbk = sb_of[j] + kti % 2
                        skey = PSK[sbk]
                        A("T", lambda e, h=h, kti=kti, qs=qs, sbk=sbk: e.matmul(
                            ps[:, 512 * sbk + qs: 512 * sbk + 512],
                            kt_ap[0:65, T * h + 128 * kti: T * h + 128 * kti + 128],
                            qt_ap[0:65, 512 * h + qs: 512 * h + 512],
                            start=True, stop=True),
                          reads=KD + QD + ["qaug", "ktones"], writes=[skey])
                    for j in range(2):
                        h = hs[j]
                        sbk = sb_of[j] + kti % 2
                        pb = 2 * j + kti % 2
                        A("S", lambda e, kti=kti, qs=qs, sbk=sbk, pb=pb: e.activation(
                            p2[pb][:, qs:512], ps[:, 512 * sbk + qs: 512 * sbk + 512],
                            AF.Exp, scale=0.125),
                          reads=[PSK[sbk], "consts"], writes=[f"p2_{pb}"])
                    if kti >= 4 * J:
                        for j in range(2):
                            pb = 2 * j + kti % 2
                            A("V", lambda e, kti=kti, qs=qs, pb=pb: e.tensor_tensor(
                                out=p2[pb][:, qs:qs + 128],
                                in0=p2[pb][:, qs:qs + 128],
                                in1=tri01[:], op=OP.mult),
                              reads=["tri01", f"p2_{pb}"], writes=[f"p2_{pb}"])
                    if kti > 0:
                        emit_pv(kti - 1)
                    if kti == 2 and pending_chain[0] is not None:
                        pending_chain[0]()
                        pending_chain[0] = None
                emit_pv(nkt - 1)
                for j in range(2):
                    h = hs[j]
                    pvb = 2 + j
                    pvk = PSK[pvb]
                    A("V", lambda e, h=h, pvb=pvb: e.tensor_copy(
                        attn[:][64 * (h % 2): 64 * (h % 2) + 64,
                                512 * (h // 2): 512 * (h // 2) + 512],
                        ps[0:64, 512 * pvb: 512 * pvb + 512]),
                      reads=[pvk], writes=[f"attn{hp2}"])
                    A("V", lambda e, h=h, pvb=pvb: e.tensor_copy(
                        srowt[32 * (h % 2):32 * (h % 2) + 1][:, 512 * ((h // 2) % 2):512 * ((h // 2) % 2) + 512],
                        ps[64:65, 512 * pvb: 512 * pvb + 512]),
                      reads=[pvk], writes=["srow"])
                if hp2 % 2 == 1:
                    pending_chain[0] = mk_chain(hp2)
                if hp2 < len(ln_phases):
                    ln_phases[hp2]()

            def mk_wo(wo_loads=wo_loads, seq=seq, J=J, chain=pending_chain[0]):
                chain_h = [chain]

                def wo_emit():
                    for g in range(12, 16):
                        wst, slot, wkey = wo_loads[g - 12]
                        for fc2 in range(2):
                            fco = (g - 12) * 2 + fc2
                            bk = fco % 2
                            bkey = PSK[bk]
                            for dc in range(CH):
                                if dc == 6 and chain_h[0] is not None:
                                    chain_h[0]()
                                    chain_h[0] = None
                                A("T", lambda e, slot=slot, fc2=fc2, dc=dc, bk=bk: e.matmul(
                                    bank(bk), wb3[slot][:, dc, 128 * fc2:128 * fc2 + 128],
                                    attn3[:, dc, :],
                                    start=(dc == 0), stop=(dc == CH - 1)),
                                  reads=[wkey, f"attn{dc}"],
                                  writes=[bkey] if dc in (0, CH - 1) else [], deps=[wst])
                            A("V", lambda e, fco=fco, bk=bk: e.tensor_tensor(
                                out=outb3[:, fco, :], in0=bank(bk), in1=xrk3[:, fco, :],
                                op=OP.add),
                              reads=[bkey, "xrk"], writes=["outb"])
                    A("G", lambda e, seq=seq, J=J: e.dma_start(
                        out=y_h[seq, J], in_=outb[:]),
                      reads=["outb"], dma_sem="dy")
                return wo_emit

            pending_wo[0] = mk_wo()
            pending_chain[0] = None
            if is_last:
                pending_wo[0]()
                pending_wo[0] = None
            first_pass[0] = False

    last_y = [s for s in P.steps if s.sem == "dy"][-1]
    A("Y", lambda e: e.nop(), deps=[last_y], dma_sem=None)
    P.steps[-1].sem = "fin"
    P.steps[-1].val = 1
    A("G", lambda e: e.nop(), deps=[last_y])
    P.steps[-1].sem = "fin"
    P.steps[-1].val = 2

    with es:
        with nc.Block() as block:
            P.emit(block, semh)
    return nc


def _deint_perm():
    perm = np.arange(3 * C)
    for blk in range(2):
        for h in range(H):
            base = blk * C + h * DH
            idx = np.concatenate([np.arange(0, DH, 2), np.arange(1, DH, 2)])
            perm[base: base + DH] = base + idx
    return perm


def _host_inputs(inputs):
    x = np.ascontiguousarray(np.asarray(inputs["L0_attn_x"], dtype=np.float32))
    wkqv = np.asarray(inputs["L0_attn_W_kqv"], dtype=np.float32)
    wo = np.asarray(inputs["L0_attn_W_o"], dtype=np.float32)
    wr = np.asarray(inputs["L0_ffn_W_r"], dtype=np.float32)
    wk = np.asarray(inputs["L0_ffn_W_k"], dtype=np.float32)
    wv = np.asarray(inputs["L0_ffn_W_v"], dtype=np.float32)
    mu_r = np.asarray(inputs["L0_ffn_mu_r"], dtype=np.float32)
    mu_k = np.asarray(inputs["L0_ffn_mu_k"], dtype=np.float32)

    wkqvt = np.ascontiguousarray(np.ascontiguousarray(wkqv.T)[:, _deint_perm()])
    wot = np.ascontiguousarray(wo.T)
    wrt = np.ascontiguousarray(wr.T)
    wkt = np.ascontiguousarray(wk.T)
    wvt = np.ascontiguousarray(wv.T)
    mu2 = np.ascontiguousarray(np.concatenate(
        [mu_r.reshape(CH, 128).T, mu_k.reshape(CH, 128).T], axis=1))

    pos = np.arange(T, dtype=np.float64)
    inv = THETA ** (-(np.arange(32, dtype=np.float64) / 32.0))
    ang = pos[None, :] * inv[:, None]
    cos32 = np.cos(ang)
    sin32 = np.sin(ang)
    cmap = np.concatenate([cos32, cos32, cos32, cos32], axis=0)
    smap = np.concatenate([sin32, -sin32, sin32, -sin32], axis=0)
    maps = np.ascontiguousarray(
        np.concatenate([cmap, smap], axis=1).astype(np.float32))

    in_maps = []
    for c in range(NCORES):
        xb = x[BSEQ * c: BSEQ * (c + 1)]
        xc = np.ascontiguousarray(
            xb.reshape(BSEQ, NJ, 512, CH, 128).transpose(0, 1, 4, 3, 2)
        ).reshape(BSEQ, NJ, 128, CH * 512)
        in_maps.append({
            "x": xc, "wkqvt": wkqvt, "wrt": wrt, "wkt": wkt, "wvt": wvt,
            "wot": wot, "mu2": mu2, "ropemaps": maps,
        })
    return in_maps


def kernel(**inputs):
    in_maps = _host_inputs(inputs)
    nc = build_program()
    res = run_bass_kernel_spmd(nc, in_maps, core_ids=list(range(NCORES)))
    outs = []
    for c in range(NCORES):
        y = np.asarray(res.results[c]["y"]).reshape(BSEQ, NJ, 128, CH, 512)
        outs.append(y.transpose(0, 1, 4, 3, 2).reshape(BSEQ, T, C))
    return np.ascontiguousarray(np.concatenate(outs, axis=0), dtype=np.float32)


if __name__ == "__main__":
    nc = build_program()
    print("built ok, instructions:", sum(1 for _ in nc.all_instructions()))
